# version 02
# baseline (speedup 1.0000x reference)
import math
import numpy as np
import concourse.bass as bass
import concourse.mybir as mybir
from concourse.bass_utils import run_bass_kernel_spmd

F32 = mybir.dt.float32
BF16 = mybir.dt.bfloat16
AF = mybir.ActivationFunctionType
ALU = mybir.AluOpType

B, S, D = 4, 2048, 2048
NCORES = 8
HK, HV, DK, DV = 16, 32, 128, 128
KH_C, VH_C = 8, 16
E_EXP, RANK, DFF = 8, 8, 5504
DFF1 = 5632
NH1, HD1 = 16, 128
OWN = 1024
CH = 128
SLAB = 512
NSLAB = S // SLAB
CPS = SLAB // CH
PAIRS = [[0, 1], [2, 3], [4, 5], [6, 7]]

DBG = []
DBG_SHAPES = {}
DBG_DTYPES = {}


def ts(i, n):
    return slice(i * n, (i + 1) * n)


class KB:
    ENGINES = ("sync", "tensor", "vector", "scalar", "gpsimd")

    def __init__(self, nc):
        self.nc = nc
        self.streams = {e: [] for e in self.ENGINES}
        self.semv = {}
        self.sem_names = []
        self.sems = {}
        self.tensors = {}

    def sem(self, name):
        if name not in self.semv:
            self.semv[name] = 0
            self.sem_names.append(name)
        return name

    def inc(self, name, n=1):
        self.semv[self.sem(name)] += n
        return self.semv[name]

    def val(self, name):
        return self.semv[self.sem(name)]

    def em(self, eng, fn):
        self.streams[eng].append(fn)

    def wait(self, eng, sem, v):
        self.sem(sem)
        self.em(eng, lambda E, s=sem, v=v: E.wait_ge(self.sems[s], v))

    def op(self, eng, fn, inc=None, incn=1, wait=None, drain=None):
        if drain is None:
            drain = eng in ("vector", "scalar", "gpsimd")
        if wait:
            for (s, v) in wait:
                self.wait(eng, s, v)
        if drain:
            self.em(eng, lambda E: E.drain())
        if inc is not None:
            self.sem(inc)
            self.inc(inc, incn)

            def run(E, fn=fn, s=inc, n=incn):
                fn(E).then_inc(self.sems[s], n)
        else:
            def run(E, fn=fn):
                fn(E)
        self.em(eng, run)
        return self.val(inc) if inc is not None else None

    def barrier(self):
        tg = {}
        for e in self.ENGINES:
            tg[e] = self.inc(self.sem("bar_" + e))
        for e in self.ENGINES:
            if e != "sync":
                self.em(e, lambda E: E.drain())
            self.em(e, lambda E, s="bar_" + e: E.nop().then_inc(self.sems[s], 1))
        for e in self.ENGINES:
            for o in self.ENGINES:
                if o != e:
                    self.wait(e, "bar_" + o, tg[o])

    def dbg(self, name, src_ap_fn):
        if name not in DBG:
            return
        t = self.tensors
        self.barrier()
        v = self.op("sync", lambda E: E.dma_start(out=t["dbg_" + name],
                                                  in_=src_ap_fn()),
                    inc="dbgst", incn=16)
        self.wait("sync", "dbgst", v)
        self.barrier()


def _prep_core_inputs(inp, cid):
    b, h = cid // 2, cid % 2
    T = np.ascontiguousarray
    d = {}
    x = np.asarray(inp["L0_attn_hidden_states"])
    d["xT"] = T(x[b].T)

    wqkv = np.asarray(inp["L0_attn_w_qkv"])
    qcols = wqkv[:, 1024 * h:1024 * h + 1024]
    kcols = wqkv[:, 2048 + 1024 * h:2048 + 1024 * h + 1024]
    vcols = wqkv[:, 4096 + 2048 * h:4096 + 2048 * h + 2048]
    d["wqkv_c"] = T(np.concatenate([qcols, kcols, vcols], axis=1))
    cw = np.asarray(inp["L0_attn_conv_w"])
    cwc = np.concatenate(
        [cw[1024 * h:1024 * h + 1024],
         cw[2048 + 1024 * h:2048 + 1024 * h + 1024],
         cw[4096 + 2048 * h:4096 + 2048 * h + 2048]], axis=0)
    d["convw_c"] = T(cwc.reshape(32, 128, 4).transpose(1, 0, 2).reshape(128, 128))
    d["wz_c"] = T(np.asarray(inp["L0_attn_w_z"])[:, 2048 * h:2048 * h + 2048])
    wb = np.asarray(inp["L0_attn_w_b"])[:, 16 * h:16 * h + 16]
    wa = np.asarray(inp["L0_attn_w_a"])[:, 16 * h:16 * h + 16]
    wba = np.concatenate([wb, wa], axis=1)
    d["wba_c"] = T(wba.reshape(16, 128, 32).transpose(1, 0, 2).reshape(128, 512))
    d["wout0_c"] = T(np.asarray(inp["L0_attn_w_out"])[2048 * h:2048 * h + 2048, :])
    d["dtb_c"] = T(np.asarray(inp["L0_attn_dt_bias"])[16 * h:16 * h + 16, None])
    d["alog_c"] = T(np.asarray(inp["L0_attn_A_log"])[16 * h:16 * h + 16, None])
    d["nw0"] = T(np.asarray(inp["L0_attn_norm_weight"])[:, None])

    d["wr"] = T(np.asarray(inp["L0_ffn_W_router"]))
    FH = 22 * 128
    def fcols(w):
        w = np.asarray(w)
        out = np.zeros((w.shape[0], FH), w.dtype)
        piece = w[:, 2816 * h:2816 * (h + 1)]
        out[:, :piece.shape[1]] = piece
        return T(out)
    def frows(w):
        w = np.asarray(w)
        out = np.zeros((FH, w.shape[1]), w.dtype)
        piece = w[2816 * h:2816 * (h + 1), :]
        out[:piece.shape[0]] = piece
        return T(out)
    d["wg"] = fcols(inp["L0_ffn_W_gate"])
    d["wu"] = fcols(inp["L0_ffn_W_up"])
    d["wd"] = frows(inp["L0_ffn_W_down"])

    def astack(sA, A):
        A = np.asarray(A)
        return T(np.concatenate([A[e] for e in range(E_EXP)] + [np.asarray(sA)], axis=1))

    def bstack(sB, Bm):
        Bm = np.asarray(Bm)
        return T(np.concatenate([Bm[e] for e in range(E_EXP)] + [np.asarray(sB)], axis=0))
    d["ag"] = astack(inp["L0_ffn_sA_gate"], inp["L0_ffn_A_gate"])
    d["bg"] = fcols(bstack(inp["L0_ffn_sB_gate"], inp["L0_ffn_B_gate"]))
    d["au"] = astack(inp["L0_ffn_sA_up"], inp["L0_ffn_A_up"])
    d["bu"] = fcols(bstack(inp["L0_ffn_sB_up"], inp["L0_ffn_B_up"]))
    d["ad"] = frows(astack(inp["L0_ffn_sA_down"], inp["L0_ffn_A_down"]))
    d["bd"] = bstack(inp["L0_ffn_sB_down"], inp["L0_ffn_B_down"])
    d["eb"] = T(np.asarray(inp["L0_ffn_expert_bias"])[None, :].repeat(OWN // 8, 0))

    wpT = np.asarray(inp["L1_attn_w_project"]).T
    cols = [wpT[:, p * 2048 + 1024 * h: p * 2048 + 1024 * h + 1024] for p in range(3)]
    d["wpT_c"] = T(np.concatenate(cols, axis=1))
    d["woutT1_c"] = T(np.asarray(inp["L1_attn_w_out"]).T[1024 * h:1024 * h + 1024, :])

    d["w1T"] = T(np.asarray(inp["L1_ffn_w1"]).T[:, 2816 * h:2816 * (h + 1)])
    d["w2T"] = T(np.asarray(inp["L1_ffn_w2"]).T[:, 2816 * h:2816 * (h + 1)])
    d["w3T"] = T(np.asarray(inp["L1_ffn_w3"]).T[2816 * h:2816 * (h + 1), :])
    return d


def build_program():
    import contextlib
    nc = bass.Bass("TRN2", target_bir_lowering=False, debug=False)
    kb = KB(nc)
    t = kb.tensors

    def din(name, shape, dt=F32):
        t[name] = nc.dram_tensor(name, shape, dt, kind="ExternalInput").ap()

    din("xT", [D, S])
    din("wqkv_c", [D, 4096]); din("convw_c", [128, 128]); din("wz_c", [D, 2048])
    din("wba_c", [128, 512]); din("wout0_c", [2048, D])
    din("dtb_c", [16, 1]); din("alog_c", [16, 1]); din("nw0", [128, 1])
    din("wr", [D, 8]); din("wg", [D, 2816]); din("wu", [D, 2816]); din("wd", [2816, D])
    din("ag", [D, 72]); din("bg", [72, 2816]); din("au", [D, 72]); din("bu", [72, 2816])
    din("ad", [2816, 72]); din("bd", [72, D]); din("eb", [128, 8])
    din("wpT_c", [D, 3072]); din("woutT1_c", [1024, D])
    din("w1T", [D, 2816]); din("w2T", [D, 2816]); din("w3T", [2816, D])

    t["out"] = nc.dram_tensor("out", [16, 128, OWN], F32, kind="ExternalOutput").ap()

    t["xn_sp"] = nc.dram_tensor("xn_sp", [D, S], BF16).ap()
    t["xn1_sp"] = nc.dram_tensor("xn1_sp", [D, S], BF16).ap()
    t["x1_sp"] = nc.dram_tensor("x1_sp", [D, S], F32).ap()
    t["cc0_in"] = nc.dram_tensor("cc0_in", [D, S], BF16).ap()
    t["cc0_out"] = nc.dram_tensor("cc0_out", [D, S], BF16).ap()
    t["cc1_in"] = nc.dram_tensor("cc1_in", [2, 16, 128, OWN], BF16).ap()
    t["cc1_out"] = nc.dram_tensor("cc1_out", [16, 128, OWN], BF16).ap()
    t["ccg_in"] = nc.dram_tensor("ccg_in", [4, 1], F32).ap()
    t["ccg_out"] = nc.dram_tensor("ccg_out", [4, 1], F32).ap()

    for name in DBG:
        t["dbg_" + name] = nc.dram_tensor(
            "dbg_" + name, DBG_SHAPES[name], DBG_DTYPES.get(name, F32),
            kind="ExternalOutput").ap()

    ctx = contextlib.ExitStack()

    def sb(name, shape, dt):
        t[name] = ctx.enter_context(nc.sbuf_tensor(name, shape, dt)).ap()

    sb("ident_bf", [128, 128], BF16)
    sb("m_low_strict01", [128, 128], BF16)
    sb("m_low_incl01", [128, 128], BF16)
    sb("neg_low_incl", [128, 128], F32)
    sb("ones_col_bf", [128, 1], BF16)
    sb("ones_row_f32", [1, 128], F32)
    sb("eps_col", [128, 1], F32)
    sb("epsn_col", [128, 1], F32)
    sb("one_col", [128, 1], F32)
    sb("it_row", [128, 128], F32)
    sb("it_col", [128, 128], F32)
    sb("smalls", [128, 32], F32)
    sb("AB", [128, 63488], BF16)
    sb("AF", [128, 11776], F32)
    sb("cstash", [128, 32 * 4], BF16)
    sb("ncg3", [48, S], BF16)
    sb("cqb", [16, S], BF16)
    sb("bscrb", [16, 2048], BF16)
    sb("sel48", [48, 16 * 128], BF16)
    sb("sel16", [16, 16 * 128], BF16)
    sb("idf16", [16, 16], F32)
    sb("ones_row_bf", [1, 128], BF16)
    sb("m_low_strictneg", [128, 128], BF16)
    sb("cw", [128, 128], F32)
    sb("wba_s", [128, 512], BF16)
    sb("cser", [16, 512], F32)
    sb("c16", [16, 64], F32)
    sb("z16", [16, 512], F32)
    sb("nrmc", [128, 64], F32)
    sb("rep8c", [8, 64], BF16)
    sb("rows_bf", [1, 128], BF16)
    sb("ones16f", [16, 128], F32)

    def psum(name, shape, dt=F32):
        t[name] = ctx.enter_context(nc.psum_tensor(name, shape, dt)).ap()
    for i in range(6):
        psum(f"pb{i}", [128, 512], F32)
    psum("pbtA", [128, 1024], BF16)
    psum("pbtB", [128, 1024], BF16)

    t["fscr"] = t["AF"][:, 0:2048]
    t["ser"] = t["AF"][:, 2048:4096]
    t["smastA"] = t["AF"][:, 4096:6144]
    t["smastB"] = t["AF"][:, 6144:8192]
    t["colp"] = t["AF"][:, 8192:8448]
    t["rows"] = t["AF"][0:1, 8448:11008]
    t["baraw"] = t["AF"][0:32, 0:1024]
    t["bscr"] = t["AF"][0:16, 1024:2048]
    t["eq16"] = t["AF"][0:16, 0:2048]
    t["xnA"] = t["AB"][:, 0:8192]
    t["xnB"] = t["AB"][:, 8192:16384]
    t["cscr"] = t["AB"][:, 16384:18432]

    emit_consts(kb)
    kb.barrier()
    emit_l0(kb)
    emit_l1(kb)
    emit_final(kb)

    with ctx:
        with contextlib.ExitStack() as sctx:
            for s in kb.sem_names:
                kb.sems[s] = sctx.enter_context(nc.semaphore(s))
            block = sctx.enter_context(nc.Block())

            def register(eng_name):
                def body(E):
                    for fn in kb.streams[eng_name]:
                        fn(E)
                return body
            block.sync(register("sync"))
            block.tensor(register("tensor"))
            block.vector(register("vector"))
            block.scalar(register("scalar"))
            block.gpsimd(register("gpsimd"))
    return nc, kb


def emit_consts(kb):
    t = kb.tensors
    kb.op("gpsimd", lambda E: E.iota(t["it_row"][:, :], pattern=[[1, 128]], base=0,
                                     channel_multiplier=0,
                                     allow_small_or_imprecise_dtypes=True))
    kb.op("gpsimd", lambda E: E.iota(t["it_col"][:, :], pattern=[[0, 128]], base=0,
                                     channel_multiplier=1,
                                     allow_small_or_imprecise_dtypes=True), inc="gp")
    for name, v in [("ones_col_bf", 1.0), ("ones_row_f32", 1.0), ("ones_row_bf", 1.0),
                    ("eps_col", 1e-6), ("epsn_col", 1e-5), ("one_col", 1.0)]:
        kb.op("vector", lambda E, n=name, v=v: E.memset(t[n][:, :], v))
    kb.wait("vector", "gp", kb.val("gp"))
    kb.op("vector", lambda E: E.tensor_tensor(t["ident_bf"][:, :], t["it_col"][:, :],
                                              t["it_row"][:, :], ALU.is_equal))
    kb.op("vector", lambda E: E.tensor_tensor(t["m_low_strict01"][:, :], t["it_col"][:, :],
                                              t["it_row"][:, :], ALU.is_gt))
    kb.op("vector", lambda E: E.tensor_tensor(t["m_low_incl01"][:, :], t["it_col"][:, :],
                                              t["it_row"][:, :], ALU.is_ge))
    kb.op("vector", lambda E: E.tensor_tensor(t["neg_low_incl"][:, :], t["it_col"][:, :],
                                              t["it_row"][:, :], ALU.is_lt))
    kb.op("vector", lambda E: E.tensor_scalar(t["neg_low_incl"][:, :], t["neg_low_incl"][:, :],
                                              -1e9, None, ALU.mult))
    kb.op("vector", lambda E: E.memset(t["ones16f"][:, :], 1.0))
    kb.op("vector", lambda E: E.tensor_scalar(t["m_low_strictneg"][:, :], t["m_low_strict01"][:, :],
                                              -1.0, None, ALU.mult))
    kb.op("vector", lambda E: E.tensor_tensor(t["idf16"][:, :], t["it_col"][0:16, 0:16],
                                              t["it_row"][0:16, 0:16], ALU.is_equal))
    kb.op("gpsimd", lambda E: E.iota(t["eq16"][0:8, 0:64], pattern=[[1, 8], [0, 8]],
                                     base=0, channel_multiplier=0,
                                     allow_small_or_imprecise_dtypes=True), inc="gp")
    kb.op("gpsimd", lambda E: E.iota(t["ser"][0:8, 0:64], pattern=[[0, 64]],
                                     base=0, channel_multiplier=1,
                                     allow_small_or_imprecise_dtypes=True), inc="gp")
    kb.wait("vector", "gp", kb.val("gp"))
    r8v = kb.op("vector", lambda E: E.tensor_tensor(t["rep8c"][0:8, :], t["eq16"][0:8, 0:64],
                                                    t["ser"][0:8, 0:64], ALU.is_equal), inc="ve")
    kb.wait("gpsimd", "ve", r8v)
    kb.op("gpsimd", lambda E: E.iota(t["eq16"][0:16, 0:2048], pattern=[[1, 16], [0, 128]],
                                     base=0, channel_multiplier=0,
                                     allow_small_or_imprecise_dtypes=True), inc="gp")
    kb.op("gpsimd", lambda E: E.iota(t["ser"][0:16, 0:2048], pattern=[[0, 2048]],
                                     base=0, channel_multiplier=1,
                                     allow_small_or_imprecise_dtypes=True), inc="gp")
    kb.wait("vector", "gp", kb.val("gp"))
    vv = kb.op("vector", lambda E: E.tensor_tensor(t["sel16"][0:16, :], t["eq16"][0:16, :],
                                                   t["ser"][0:16, 0:2048], ALU.is_equal), inc="ve")
    for j in range(3):
        kb.op("sync", lambda E, j=j: E.dma_start(
            out=t["sel48"][16 * j:16 * (j + 1), :], in_=t["sel16"][0:16, :]),
            wait=[("ve", vv)] if j == 0 else None, inc="ld", incn=16)
    kb.wait("vector", "ld", kb.val("ld"))
    kb.wait("tensor", "ld", kb.val("ld"))
    kb.wait("sync", "ld", kb.val("ld"))
    ldv = kb.op("sync", lambda E: E.dma_start(out=t["smalls"][0:16, 0:1], in_=t["dtb_c"]),
                inc="ld", incn=16)
    ldv = kb.op("sync", lambda E: E.dma_start(out=t["smalls"][0:16, 1:2], in_=t["alog_c"]),
                inc="ld", incn=16)
    ldv = kb.op("sync", lambda E: E.dma_start(out=t["smalls"][0:128, 2:3], in_=t["nw0"]),
                inc="ld", incn=16)
    av = kb.op("scalar", lambda E: E.activation(t["smalls"][0:16, 3:4], t["smalls"][0:16, 1:2],
                                                AF.Exp), wait=[("ld", ldv)], inc="ac_c")
    kb.op("vector", lambda E: E.tensor_scalar(t["smalls"][0:16, 3:4], t["smalls"][0:16, 3:4],
                                              -1.0, None, ALU.mult),
          wait=[("ac_c", av)])


def emit_l0(kb):
    emit_l0_norm(kb)
    kb.dbg("xn", lambda: kb.tensors["xn_sp"])
    kb.barrier()
    for sl in range(NSLAB):
        emit_l0_projslab(kb, sl)
        kb.barrier()
        emit_l0_scanslab(kb, sl)
        kb.barrier()
    emit_l0_ffn(kb)
    kb.barrier()
    emit_l0_combine(kb)
    kb.barrier()


def emit_l0_norm(kb):
    t = kb.tensors
    n_ldx = "l0n_ld"; n_pe = "l0n_pe"; n_ve = "l0n_ve"; n_st = "l0n_st"
    n_ac = "l0n_ac"; n_cast = "l0n_cast"; n_ap = "l0n_ap"
    n_mm2 = "l0n_mm2"
    for tile in range(NSLAB):
        cur = t["xnA"] if tile % 2 == 0 else t["xnB"]
        if tile >= 2:
            kb.wait("vector", n_st + str(tile % 2), 16 * (tile // 2))
        mm2_base = kb.val(n_mm2) if kb.semv.get(n_mm2) is not None else 0
        for c in range(16):
            g = tile * 16 + c
            if g >= 4:
                kb.wait("sync", n_cast, g - 3)
            lv = kb.op("sync", lambda E, c=c, tile=tile:
                       E.dma_start(out=t["fscr"][:, ts(c % 4, 512)],
                                   in_=t["xT"][ts(c, 128), ts(tile, 512)]),
                       inc=n_ldx + str(c % 4), incn=16)
            cv = kb.op("vector", lambda E, c=c, cur=cur:
                       E.tensor_copy(cur[:, ts(c, 512)], t["fscr"][:, ts(c % 4, 512)]),
                       wait=[(n_ldx + str(c % 4), lv)], inc=n_cast)
            sqw = [(n_cast, cv)]
            if c >= 4:
                sqw.append((n_mm2, mm2_base + c - 3))
            sq = kb.op("scalar", lambda E, c=c, cur=cur:
                       E.activation(t["cscr"][:, ts(c % 4, 512)], cur[:, ts(c, 512)],
                                    AF.Square), wait=sqw, inc=n_ac)
            kb.op("tensor", lambda E, c=c, cur=cur:
                  E.matmul(t["pb0"][0:1, :], t["ones_col_bf"][:, :], cur[:, ts(c, 512)],
                           start=(c == 0), stop=(c == 15)), wait=[(n_cast, cv)],
                  inc=(n_pe if c == 15 else None))
            pv = kb.op("tensor", lambda E, c=c:
                       E.matmul(t["pb1"][0:1, :], t["ones_col_bf"][:, :],
                                t["cscr"][:, ts(c % 4, 512)],
                                start=(c == 0), stop=(c == 15)),
                       wait=[(n_ac, sq)], inc=n_mm2)
        kb.op("tensor", lambda E: E.nop(), inc=n_pe, drain=False,
              wait=[(n_mm2, kb.val(n_mm2))])
        def R(i):
            return t["rows"][0:1, 512 * i:512 * (i + 1)]
        kb.wait("vector", n_pe, kb.val(n_pe))
        kb.op("vector", lambda E: E.tensor_scalar(R(0), t["pb0"][0:1, :],
                                                  1.0 / D, None, ALU.mult))
        kb.op("vector", lambda E: E.tensor_scalar(R(1), t["pb1"][0:1, :],
                                                  1.0 / D, None, ALU.mult))
        kb.op("vector", lambda E: E.tensor_tensor(R(2), R(0), R(0), ALU.mult))
        vv = kb.op("vector", lambda E: E.tensor_tensor(R(2), R(1), R(2),
                                                       ALU.subtract), inc=n_ve)
        sv = kb.op("scalar", lambda E: E.activation(
            R(3), R(2), AF.Sqrt, bias=t["epsn_col"][0:1, :]),
            wait=[(n_ve, vv)], inc=n_ac)
        vv = kb.op("vector", lambda E: E.reciprocal(R(3), R(3)), wait=[(n_ac, sv)])
        vv = kb.op("vector", lambda E: E.tensor_tensor(R(4), R(0), R(3),
                                                       ALU.mult), inc=n_ve)
        pv = kb.op("tensor", lambda E: E.matmul(t["pb2"][:, :], t["ones_row_f32"][0:1, :],
                                                R(3), start=True, stop=True),
                   wait=[(n_ve, vv)])
        pv = kb.op("tensor", lambda E: E.matmul(t["pb3"][:, :], t["ones_row_f32"][0:1, :],
                                                R(4), start=True, stop=True),
                   inc=n_pe)
        kb.wait("vector", n_pe, kb.val(n_pe))
        for c in range(16):
            kb.op("vector", lambda E, c=c, cur=cur: E.tensor_tensor(
                cur[:, ts(c, 512)], cur[:, ts(c, 512)], t["pb2"][:, :], ALU.mult))
            kb.op("vector", lambda E, c=c, cur=cur: E.tensor_tensor(
                cur[:, ts(c, 512)], cur[:, ts(c, 512)], t["pb3"][:, :], ALU.subtract),
                inc=n_ap)
        sv = kb.op("sync", lambda E, tile=tile, cur=cur: E.dma_start(
            out=t["xn_sp"].rearrange("(c p) s -> p c s", p=128)[:, :, ts(tile, 512)],
            in_=cur.rearrange("p (c f) -> p c f", f=512)),
            wait=[(n_ap, kb.val(n_ap))], inc=n_st + str(tile % 2), incn=16)
        kb.wait("tensor", n_ap, kb.val(n_ap))
    for p in range(2):
        if kb.semv.get(n_st + str(p)):
            kb.wait("sync", n_st + str(p), kb.val(n_st + str(p)))


LKA = 0
LQA = 4096
LVA = 8192
LSZ = 16384
LXN = 24576
LWS = 32768
LKT = 36864
LQT = 38912
LVT = 40960
LSC = 45056
LACT = 49152
LSBF = 57344
LGS = 59392
LRAW = 61440
LCSC = 62464


def scs(i):
    return slice(LSC + i * 128, LSC + (i + 1) * 128)


def emit_l0_projslab(kb, sl):
    t = kb.tensors
    AB = t["AB"]
    xn = AB[:, LXN:LXN + 8192]
    n_wl = "pj_wl"
    n_pe = "pj_pe"
    n_ve = "pj_ve"
    n_ac = "pj_ac"
    n_xl = "pj_xl"

    if sl == 0:
        kb.op("sync", lambda E: E.dma_start(
            out=t["cw"][:, 0:128], in_=t["convw_c"]), inc="pj_cw", incn=16)
        kb.op("gpsimd", lambda E: E.dma_start(
            out=t["wba_s"][:, :], in_=t["wba_c"]), inc="pj_cwg", incn=16)
        kb.op("vector", lambda E: E.memset(t["cstash"][:, :], 0.0), inc=n_ve)
        kb.op("vector", lambda E: E.memset(t["c16"][:, :], 0.0), inc=n_ve)
        kb.op("vector", lambda E: E.memset(t["z16"][:, :], 0.0), inc=n_ve)
        kb.wait("tensor", "pj_cw", 16)
        kb.wait("tensor", "pj_cwg", 16)
        kb.wait("vector", "pj_cw", 16)
        kb.wait("vector", "pj_cwg", 16)

    xv = kb.op("sync", lambda E, sl=sl: E.dma_start(
        out=AB.rearrange("p (c f) -> p c f", f=512)[:, LXN // 512:LXN // 512 + 16, :],
        in_=t["xn_sp"].rearrange("(c p) s -> p c s", p=128)[:, :, ts(sl, 512)]),
        inc=n_xl, incn=16)
    kb.wait("tensor", n_xl, kb.val(n_xl))

    evac_q = []
    for cc in range(48):
        slot = cc % 4
        wsrc = t["wqkv_c"] if cc < 32 else t["wz_c"]
        coff = cc * 128 if cc < 32 else (cc - 32) * 128
        if cc >= 4:
            kb.wait("gpsimd", n_pe, kb.val(n_pe) - 3)
        wl = kb.op("gpsimd", lambda E, wsrc=wsrc, coff=coff, slot=slot: E.dma_start(
            out=AB.rearrange("p (s f) -> p s f", f=2048)[:, (LWS // 2048) + slot, :].rearrange("p (c f) -> p c f", f=128),
            in_=wsrc.rearrange("(c p) f -> p c f", p=128)[:, :, coff:coff + 128]),
            inc=n_wl + str(slot), incn=16, drain=False)
        bank = t["pb0"] if cc % 2 == 0 else t["pb1"]
        kb.wait("tensor", n_wl + str(slot), wl)
        if cc >= 2:
            kb.wait("tensor", n_ve, kb.val(n_ve))
        for dc in range(16):
            kb.op("tensor", lambda E, dc=dc, slot=slot, bank=bank: E.matmul(
                bank[:, :], AB[:, LWS + slot * 2048 + dc * 128:LWS + slot * 2048 + dc * 128 + 128],
                xn[:, ts(dc, 512)], start=(dc == 0), stop=(dc == 15)),
                inc=(n_pe if dc == 15 else None))
        mmv = kb.val(n_pe)
        if cc < 32:
            dst = (LQA + cc * 512 if cc < 8 else
                   LKA + (cc - 8) * 512 if cc < 16 else
                   LVA + (cc - 16) * 512)
            emit_conv_silu(kb, cc, bank, AB[:, dst:dst + 512], mmv, n_pe, n_ve, n_ac)
        elif cc < 48:
            zc = cc - 32
            kb.op("scalar", lambda E, zc=zc, bank=bank: E.activation(
                AB[:, LSZ + zc * 512:LSZ + zc * 512 + 512], bank[:, :], AF.Silu),
                wait=[(n_pe, mmv)], inc=n_ve)

    kb.wait("tensor", n_ve, kb.val(n_ve))
    for dc in range(16):
        kb.op("tensor", lambda E, dc=dc: E.matmul(
            t["pb2"][0:16, :], t["wba_s"][:, dc * 32:dc * 32 + 16], xn[:, ts(dc, 512)],
            start=(dc == 0), stop=(dc == 15)))
        kb.op("tensor", lambda E, dc=dc: E.matmul(
            t["pb3"][0:16, :], t["wba_s"][:, dc * 32 + 16:dc * 32 + 32], xn[:, ts(dc, 512)],
            start=(dc == 0), stop=(dc == 15)), inc=(n_pe if dc == 15 else None))
    bav = kb.val(n_pe)
    bw = t["baraw"][:, 0:512]
    aw = t["baraw"][:, 512:1024]
    kb.op("vector", lambda E: E.tensor_copy(t["baraw"][0:16, 0:512], t["pb2"][0:16, :]),
          wait=[(n_pe, bav)])
    kb.op("vector", lambda E: E.tensor_copy(t["baraw"][0:16, 512:1024], t["pb3"][0:16, :]),
          inc=n_ve)
    emit_series(kb, sl, n_ve, n_ac, kb.val(n_ve))


def emit_conv_silu(kb, cc, bank, dst, mmv, n_pe, n_ve, n_ac):
    t = kb.tensors
    AB = t["AB"]
    raw = AB[:, LRAW:LRAW + 520]
    c0 = AB[:, LCSC:LCSC + 512]
    c1 = AB[:, LCSC + 512:LCSC + 1024]
    st = t["cstash"][:, cc * 4:cc * 4 + 3]
    cwc = lambda j: t["cw"][:, cc * 4 + j:cc * 4 + j + 1]
    kb.op("vector", lambda E: E.tensor_copy(raw[:, 0:3], st),
          wait=[(n_pe, mmv), (n_ve, kb.val(n_ve))])
    kb.op("vector", lambda E, bank=bank: E.tensor_copy(raw[:, 3:515], bank[:, :]))
    kb.op("vector", lambda E: E.tensor_copy(st, raw[:, 512:515]))
    kb.op("vector", lambda E: E.tensor_scalar(c0[:, :], raw[:, 0:512], cwc(0), None, ALU.mult))
    kb.op("vector", lambda E: E.scalar_tensor_tensor(c1[:, :], raw[:, 1:513], cwc(1), c0, ALU.mult, ALU.add))
    kb.op("vector", lambda E: E.scalar_tensor_tensor(c0[:, :], raw[:, 2:514], cwc(2), c1, ALU.mult, ALU.add))
    vv = kb.op("vector", lambda E: E.scalar_tensor_tensor(c1[:, :], raw[:, 3:515], cwc(3), c0, ALU.mult, ALU.add), inc=n_ve)
    kb.op("scalar", lambda E: E.activation(dst, c1, AF.Silu), wait=[(n_ve, vv)], inc=n_ve)


def emit_series(kb, sl, n_ve, n_ac, bav):
    t = kb.tensors
    ser = t["ser"]
    co = sl * 512
    s1 = ser[64:80, co:co + 512]
    s2 = ser[96:112, co:co + 512]
    bs0 = t["bscr"][:, 0:512]
    bs1 = t["bscr"][:, 512:1024]
    dtb = t["smalls"][0:16, 0:1]
    nea = t["smalls"][0:16, 3:4]
    aw = t["baraw"][0:16, 512:1024]
    bw = t["baraw"][0:16, 0:512]
    kb.op("scalar", lambda E: E.activation(ser[32:48, co:co + 512], bw, AF.Sigmoid),
          wait=[(n_ve, bav)], inc=n_ac)
    A, Bs, C = bs0, bs1, t["cser"][0:16, 0:512]
    kb.op("vector", lambda E: E.tensor_scalar(A, aw, dtb, None, ALU.add), inc=n_ve)
    kb.op("scalar", lambda E: E.activation(Bs, A, AF.Abs), wait=[(n_ve, kb.val(n_ve))])
    kb.op("scalar", lambda E: E.activation(C, Bs, AF.Exp, scale=-1.0))
    kb.op("scalar", lambda E: E.activation(Bs, C, AF.Ln, bias=t["one_col"][0:16, :]), inc=n_ac)
    kb.op("vector", lambda E: E.tensor_scalar(C, A, 0.0, None, ALU.max),
          wait=[(n_ac, kb.val(n_ac))])
    kb.op("vector", lambda E: E.tensor_tensor(A, Bs, C, ALU.add))
    kb.op("vector", lambda E: E.tensor_scalar(Bs, A, nea, None, ALU.mult), inc=n_ve)
    kb.op("vector", lambda E: E.tensor_tensor_scan(
        ser[0:16, co:co + 512], Bs, t["z16"][0:16, :], t["c16"][0:16, 3:4], ALU.add, ALU.add), inc=n_ve)
    kb.op("vector", lambda E, co=co: E.tensor_copy(t["c16"][0:16, 3:4], ser[0:16, co + 511:co + 512]), inc=n_ve)
    kb.op("vector", lambda E, co=co: E.tensor_scalar(A, ser[0:16, co:co + 512], -1.0, None, ALU.mult))
    kb.op("vector", lambda E: E.tensor_copy(t["bscrb"][0:16, 0:512], A))
    kb.op("vector", lambda E: E.tensor_tensor(C, A, t["bscrb"][0:16, 0:512], ALU.subtract))
    kb.op("vector", lambda E: E.tensor_copy(t["bscrb"][0:16, 512:1024], C))
    kb.op("vector", lambda E: E.tensor_tensor(A, C, t["bscrb"][0:16, 512:1024], ALU.subtract))
    kb.op("vector", lambda E: E.tensor_copy(t["bscrb"][0:16, 1024:1536], A), inc=n_ve)
    sv = kb.val(n_ve)
    for j in range(3):
        kb.op("sync", lambda E, j=j, co=co: E.dma_start(
            out=t["ncg3"][j * 16:(j + 1) * 16, co:co + 512],
            in_=t["bscrb"][0:16, j * 512:(j + 1) * 512]),
            wait=[(n_ve, sv)] if j == 0 else None, inc="pj_ncg", incn=16)
    kb.wait("vector", "pj_ncg", kb.val("pj_ncg"))


def emit_l0_scanslab(kb, sl):
    t = kb.tensors
    AB = t["AB"]
    if sl == 0:
        kb.op("vector", lambda E: E.memset(t["smastA"][:, :], 0.0))
        kb.op("vector", lambda E: E.memset(AB[:, LSBF:LSBF + 2048], 0.0), inc="sc_ve")
    for cis in range(CPS):
        gc = sl * CPS + cis
        emit_chunk_prep(kb, sl, cis, gc)
        for h in range(VH_C):
            emit_chunk_head(kb, sl, cis, gc, h)
    emit_outproj_slab(kb, sl)


def emit_chunk_prep(kb, sl, cis, gc):
    t = kb.tensors
    AB = t["AB"]
    ser = t["ser"]
    co = gc * 128
    n_tr = "sc_tr"; n_ve = "sc_ve"; n_ac = "sc_ac"; n_pe = "sc_pe"; n_trb = "sc_trb"
    par = gc % 2
    kb.wait("scalar", "sc_ve", kb.val("sc_ve"))
    for vh in range(VH_C):
        kb.op("scalar", lambda E, vh=vh, cis=cis, par=par: E.dma_start_transpose(
            out=AB[:, LVT + par * 2048 + vh * 128:LVT + par * 2048 + vh * 128 + 128],
            in_=AB[:, LVA + vh * 512 + cis * 128:LVA + vh * 512 + cis * 128 + 128]),
            inc=n_tr, incn=16, drain=False)
    for kh in range(KH_C):
        kb.op("scalar", lambda E, kh=kh, cis=cis: E.dma_start_transpose(
            out=AB[:, LKT + kh * 128:LKT + kh * 128 + 128],
            in_=AB[:, LKA + kh * 512 + cis * 128:LKA + kh * 512 + cis * 128 + 128]),
            inc=n_tr, incn=16, drain=False)
        kb.op("scalar", lambda E, kh=kh, cis=cis: E.dma_start_transpose(
            out=AB[:, LQT + kh * 128:LQT + kh * 128 + 128],
            in_=AB[:, LQA + kh * 512 + cis * 128:LQA + kh * 512 + cis * 128 + 128]),
            inc=n_tr, incn=16, drain=False)
    trv = kb.val(n_tr)
    kb.wait("vector", n_tr, trv)
    for (base, abase, ncol) in ((LKT, LKA, 0), (LQT, LQA, 8)):
        for kh in range(KH_C):
            rawS = AB[:, base + kh * 128:base + kh * 128 + 128]
            nrmS = AB[:, base + 1024 + kh * 128:base + 1024 + kh * 128 + 128]
            sq = AB[:, LGS:LGS + 128]
            rk = t["nrmc"][:, ncol + kh:ncol + kh + 1]
            kb.op("vector", lambda E, rawS=rawS, sq=sq: E.tensor_tensor(sq, rawS, rawS, ALU.mult))
            kb.op("vector", lambda E, sq=sq, rk=rk: E.reduce_sum(rk, sq, axis=mybir.AxisListType.X))
            vv = kb.op("vector", lambda E, rk=rk: E.tensor_scalar(rk, rk, 1e-6, None, ALU.add), inc=n_ve)
            av = kb.op("scalar", lambda E, rk=rk: E.activation(rk, rk, AF.Sqrt),
                       wait=[(n_ve, vv)], inc=n_ac)
            kb.op("vector", lambda E, rk=rk: E.reciprocal(rk, rk), wait=[(n_ac, av)])
            vv = kb.op("vector", lambda E, nrmS=nrmS, rawS=rawS, rk=rk: E.tensor_scalar(
                nrmS, rawS, rk, None, ALU.mult), inc=n_ve)
            kb.op("scalar", lambda E, nrmS=nrmS, abase=abase, kh=kh, cis=cis: E.dma_start_transpose(
                out=AB[:, abase + kh * 512 + cis * 128:abase + kh * 512 + cis * 128 + 128],
                in_=nrmS), wait=[(n_ve, vv)], inc=n_trb, incn=16, drain=False)
    kb.wait("tensor", n_trb, kb.val(n_trb))
    kb.wait("tensor", n_tr, trv)

    cser = t["cser"]
    c16 = t["c16"]
    if gc == 0:
        kb.op("vector", lambda E: E.memset(c16[0:16, 0:1], 0.0))
    else:
        kb.op("vector", lambda E, co=co: E.tensor_scalar(
            c16[0:16, 0:1], ser[0:16, co - 1:co], -1.0, None, ALU.mult))
    kb.op("vector", lambda E, co=co: E.tensor_copy(c16[0:16, 1:2], ser[0:16, co + 127:co + 128]), inc=n_ve)
    av = kb.op("scalar", lambda E, co=co: E.activation(
        cser[0:16, 0:128], ser[0:16, co:co + 128], AF.Exp, bias=c16[0:16, 0:1]),
        wait=[(n_ve, kb.val(n_ve))], inc=n_ac)
    kb.op("vector", lambda E, co=co: E.tensor_scalar(
        t["cqb"][0:16, co:co + 128], cser[0:16, 0:128], DK ** -0.5, None, ALU.mult),
        wait=[(n_ac, av)])
    kb.op("vector", lambda E, co=co: E.tensor_copy(cser[0:16, 384:512], ser[32:48, co:co + 128]))
    kb.op("vector", lambda E: E.tensor_tensor(
        cser[0:16, 128:256], cser[0:16, 384:512], cser[0:16, 0:128], ALU.mult), inc=n_ve)
    av = kb.op("scalar", lambda E, co=co: E.activation(
        cser[0:16, 256:384], ser[0:16, co:co + 128], AF.Exp, scale=-1.0, bias=c16[0:16, 1:2]), inc=n_ac)
    av = kb.op("scalar", lambda E: E.activation(
        c16[0:16, 2:3], c16[0:16, 1:2], AF.Exp, bias=c16[0:16, 0:1]), inc=n_ac)
    kb.op("vector", lambda E: E.tensor_scalar(c16[0:16, 8:24], t["idf16"][:, :], c16[0:16, 2:3], None, ALU.mult),
          wait=[(n_ac, kb.val(n_ac))], inc=n_ve)
    kb.wait("tensor", n_ve, kb.val(n_ve))
    kb.wait("tensor", n_ac, kb.val(n_ac))
    kb.wait("tensor", n_ve, kb.val(n_ve))
    kb.op("tensor", lambda E, co=co: E.matmul(t["pb2"][:, 0:16], ser[0:16, co:co + 128], t["idf16"][:, :], start=True, stop=True))
    kb.op("tensor", lambda E: E.matmul(t["pb2"][:, 16:32], cser[0:16, 384:512], t["idf16"][:, :], start=True, stop=True))
    kb.op("tensor", lambda E: E.matmul(t["pb2"][:, 32:48], cser[0:16, 128:256], t["idf16"][:, :], start=True, stop=True))
    kb.op("tensor", lambda E: E.matmul(t["pb2"][:, 48:64], cser[0:16, 256:384], t["idf16"][:, :], start=True, stop=True))
    pv = kb.op("tensor", lambda E: E.matmul(t["pb2"][:, 64:80], t["ones16f"][0:16, :], c16[0:16, 8:24], start=True, stop=True), inc=n_pe)
    vv = kb.op("vector", lambda E: E.tensor_copy(t["colp"][:, 0:80], t["pb2"][:, 0:80]),
               wait=[(n_pe, pv)], inc=n_ve)
    kb.wait("tensor", n_ve, vv)
    kb.wait("scalar", n_ve, vv)


def emit_chunk_head(kb, sl, cis, gc, h):
    t = kb.tensors
    AB = t["AB"]
    kh = h // 2
    par = gc % 2
    co = gc * 128
    n_pe = "sc_pe"; n_ve = "sc_ve"; n_ac = "sc_ac"
    colp = t["colp"]
    cgcol = colp[:, 0 + h:1 + h]
    bcol = colp[:, 16 + h:17 + h]
    bgcol = colp[:, 32 + h:33 + h]
    ksecol = colp[:, 48 + h:49 + h]
    gccol = colp[:, 64 + h:65 + h]
    Kf = AB[:, LKA + kh * 512 + cis * 128:LKA + kh * 512 + cis * 128 + 128]
    Qf = AB[:, LQA + kh * 512 + cis * 128:LQA + kh * 512 + cis * 128 + 128]
    Vt = AB[:, LVT + par * 2048 + h * 128:LVT + par * 2048 + h * 128 + 128]
    sz = AB[:, LSZ + h * 512 + cis * 128:LSZ + h * 512 + cis * 128 + 128]
    Sm0 = t["smastA"] if gc % 2 == 0 else t["smastB"]
    Sm1 = t["smastB"] if gc % 2 == 0 else t["smastA"]
    S0 = Sm0[:, h * 128:(h + 1) * 128]
    S1 = Sm1[:, h * 128:(h + 1) * 128]
    S0b = AB[:, LSBF + h * 128:LSBF + h * 128 + 128]
    act = AB[:, LACT + h * 512 + cis * 128:LACT + h * 512 + cis * 128 + 128]
    Ktn = AB[:, LKT + 1024 + kh * 128:LKT + 1024 + kh * 128 + 128]

    EI = AB[:, scs(0)]
    ES = AB[:, scs(1)]
    MQ = AB[:, scs(2)]
    MT = AB[:, scs(3)]
    NL = {1: AB[:, scs(4)]}
    NU = {}
    for i, lvl in enumerate([2, 4, 8, 16, 32]):
        NL[lvl] = AB[:, scs(5 + i)]
    for i, lvl in enumerate([1, 2, 4, 8, 16, 32, 64]):
        NU[lvl] = AB[:, scs(10 + i)]
    Rs = AB[:, scs(17)]
    X0 = AB[:, scs(18)]
    X1 = AB[:, scs(19)]
    T1 = AB[:, scs(20)]
    T2 = AB[:, scs(21)]
    CQR = AB[:, scs(22)]
    OT = AB[:, scs(23)]
    fs = t["fscr"][:, 0:128]

    pb0, pb1, pb2, pb3, pb4, pb5 = (t[f"pb{i}"] for i in range(6))
    G = pb0[:, 0:128]; M = pb1[:, 0:128]
    REP = pb2[:, 128:256]
    W = pb3[:, 0:128]
    SQA = pb4[:, 0:128]; SQB = pb5[:, 0:128]
    O1 = pb4[:, 128:256]; O2 = pb5[:, 128:256]
    SUP = pb3[:, 128:256]
    RMS = pb2[:, 256:384]

    if h % 2 == 0:
        pv = kb.op("tensor", lambda E: E.matmul(G, Kf, Kf, start=True, stop=True))
        pv = kb.op("tensor", lambda E: E.matmul(M, Kf, Qf, start=True, stop=True), inc=n_pe)
        vv = kb.op("vector", lambda E: E.tensor_scalar(MQ, M, DK ** -0.5, None, ALU.mult),
                   wait=[(n_pe, pv)], inc=n_ve)
        kb.wait("tensor", n_ve, vv)

    pv = kb.op("tensor", lambda E: E.matmul(
        REP, t["sel48"][:, ts(h, 128)], t["ncg3"][:, co:co + 128], start=True, stop=True), inc=n_pe)
    vv = kb.op("vector", lambda E: E.tensor_tensor(fs, REP, t["neg_low_incl"][:, :], ALU.add),
               wait=[(n_pe, pv)], inc=n_ve)
    av = kb.op("scalar", lambda E: E.activation(EI, fs, AF.Exp, bias=cgcol),
               wait=[(n_ve, vv)], inc=n_ac)
    vv = kb.op("vector", lambda E: E.tensor_tensor(ES, EI, t["m_low_strictneg"][:, :], ALU.mult),
               wait=[(n_ac, av)])
    vv = kb.op("vector", lambda E: E.scalar_tensor_tensor(NL[1], G, bcol, ES, ALU.mult, ALU.mult), inc=n_ve)
    kb.wait("tensor", n_ve, vv)
    pv = kb.op("tensor", lambda E: E.transpose(t["pbtA"][:, 0:128], EI, t["ident_bf"][:, :]))
    pv = kb.op("tensor", lambda E: E.transpose(t["pbtB"][:, 0:128], NL[1], t["ident_bf"][:, :]), inc=n_pe)
    vv = kb.op("vector", lambda E: E.tensor_tensor(MT, t["pbtA"][:, 0:128], MQ, ALU.mult),
               wait=[(n_pe, pv)])
    vv = kb.op("vector", lambda E: E.tensor_copy(NU[1], t["pbtB"][:, 0:128]), inc=n_ve)
    kb.wait("tensor", n_ve, vv)
    prev = 1
    for lvl in [2, 4, 8, 16, 32, 64]:
        if lvl < 64:
            kb.op("tensor", lambda E, p=prev, lvl=lvl: E.matmul(SQA, NU[p], NL[p], start=True, stop=True))
            pv = kb.op("tensor", lambda E, p=prev, lvl=lvl: E.matmul(SQB, NL[p], NU[p], start=True, stop=True), inc=n_pe)
            vv = kb.op("vector", lambda E, lvl=lvl: E.tensor_copy(NL[lvl], SQA), wait=[(n_pe, pv)],
                       inc=n_ve)
            av = kb.op("scalar", lambda E, lvl=lvl: E.activation(NU[lvl], SQB, AF.Copy),
                       wait=[(n_pe, pv)], inc=n_ac)
            kb.wait("tensor", n_ac, av)
            kb.wait("tensor", n_ve, kb.val(n_ve))
        else:
            pv = kb.op("tensor", lambda E, p=prev: E.matmul(SQA, NL[p], NU[p], start=True, stop=True), inc=n_pe)
            vv = kb.op("vector", lambda E: E.tensor_copy(NU[64], SQA), wait=[(n_pe, pv)], inc=n_ve)
            kb.wait("tensor", n_ve, vv)
        prev = lvl
    pv = kb.op("tensor", lambda E: E.matmul(W, Kf, S0b, start=True, stop=True), inc=n_pe)
    vv = kb.op("vector", lambda E: E.tensor_scalar(T1, W, bgcol, None, ALU.mult),
               wait=[(n_pe, pv)])
    vv = kb.op("vector", lambda E: E.scalar_tensor_tensor(Rs, Vt, bcol, T1, ALU.mult, ALU.subtract), inc=n_ve)
    kb.wait("tensor", n_ve, vv)
    X = Rs
    Xalt = [X0, X1]
    for i, lvl in enumerate([64, 32, 16, 8, 4, 2, 1]):
        pv = kb.op("tensor", lambda E, lvl=lvl, X=X: E.matmul(SUP, NU[lvl], X, start=True, stop=True), inc=n_pe)
        nX = Xalt[i % 2]
        vv = kb.op("vector", lambda E, nX=nX, X=X: E.tensor_tensor(nX, SUP, X, ALU.add),
                   wait=[(n_pe, pv)], inc=n_ve)
        kb.wait("tensor", n_ve, vv)
        X = nX
    U = X
    kb.op("tensor", lambda E, U=U: E.matmul(O1, U, MT, start=True, stop=True))
    pv = kb.op("tensor", lambda E: E.matmul(O2, S0b, Qf, start=True, stop=True))
    pv = kb.op("tensor", lambda E: E.matmul(REP, t["sel16"][:, ts(h, 128)], t["cqb"][:, co:co + 128], start=True, stop=True), inc=n_pe)
    vv = kb.op("vector", lambda E: E.tensor_copy(CQR, REP), wait=[(n_pe, pv)])
    vv = kb.op("vector", lambda E: E.tensor_tensor(T2, O2, CQR, ALU.mult))
    vv = kb.op("vector", lambda E: E.tensor_tensor(OT, O1, T2, ALU.add), inc=n_ve)
    kb.wait("tensor", n_ve, vv)
    vv = kb.op("vector", lambda E: E.tensor_tensor(T1, OT, OT, ALU.mult), inc=n_ve)
    pv = kb.op("tensor", lambda E: E.matmul(RMS[0:1, :], t["ones_col_bf"][:, :], T1, start=True, stop=True),
               wait=[(n_ve, vv)], inc=n_pe)
    av = kb.op("scalar", lambda E: E.activation(t["rows"][0:1, 0:128], RMS[0:1, :], AF.Sqrt,
                                                scale=1.0 / DV, bias=t["eps_col"][0:1, :]),
               wait=[(n_pe, pv)], inc=n_ac)
    kb.op("vector", lambda E: E.reciprocal(t["rows"][0:1, 0:128], t["rows"][0:1, 0:128]),
          wait=[(n_ac, av)])
    vv = kb.op("vector", lambda E: E.tensor_copy(t["rows_bf"][0:1, 0:128], t["rows"][0:1, 0:128]), inc=n_ve)
    pv = kb.op("tensor", lambda E: E.matmul(RMS, t["ones_row_bf"][0:1, :], t["rows_bf"][0:1, 0:128], start=True, stop=True),
               wait=[(n_ve, vv)], inc=n_pe)
    vv = kb.op("vector", lambda E: E.tensor_scalar(T1, OT, t["smalls"][0:128, 2:3], None, ALU.mult),
               wait=[(n_pe, pv)])
    vv = kb.op("vector", lambda E: E.tensor_tensor(T2, T1, sz, ALU.mult))
    vv = kb.op("vector", lambda E, act=act: E.tensor_tensor(act, T2, RMS, ALU.mult), inc=n_ve)
    kb.wait("tensor", n_ve, vv)
    vv = kb.op("vector", lambda E: E.tensor_scalar(T1, Ktn, ksecol, None, ALU.mult), inc=n_ve)
    pv = kb.op("tensor", lambda E, U=U: E.matmul(SUP, T1, U, start=True, stop=True),
               wait=[(n_ve, vv)], inc=n_pe)
    vv = kb.op("vector", lambda E, S0=S0, S1=S1: E.scalar_tensor_tensor(S1, S0, gccol, SUP, ALU.mult, ALU.add),
               wait=[(n_pe, pv)])
    vv = kb.op("vector", lambda E, S1=S1, S0b=S0b: E.tensor_copy(S0b, S1), inc=n_ve)
    kb.wait("tensor", n_ve, vv)


def emit_outproj_slab(kb, sl):
    t = kb.tensors
    AB = t["AB"]
    n_wl = "op_wl"; n_pe = "op_pe"; n_ve = "sc_ve"; n_st = "op_st"
    for dmc in range(16):
        slot = dmc % 2
        if dmc >= 2:
            kb.wait("gpsimd", n_pe, kb.val(n_pe) - 1)
        wl = kb.op("gpsimd", lambda E, dmc=dmc, slot=slot: E.dma_start(
            out=AB.rearrange("p (s f) -> p s f", f=2048)[:, LWS // 2048 + slot, :].rearrange("p (c f) -> p c f", f=128),
            in_=t["wout0_c"].rearrange("(c p) f -> p c f", p=128)[:, :, ts(dmc, 128)]),
            inc=n_wl + str(slot), incn=16, drain=False)
        kb.wait("tensor", n_wl + str(slot), wl)
        bank = t[f"pb{dmc % 2}"]
        if dmc >= 2:
            kb.wait("tensor", n_ve, kb.val(n_ve))
        for vc in range(VH_C):
            kb.op("tensor", lambda E, vc=vc, dmc=dmc, slot=slot, bank=bank: E.matmul(
                bank[:, 0:512],
                AB[:, LWS + slot * 2048 + vc * 128:LWS + slot * 2048 + vc * 128 + 128],
                AB[:, LACT + vc * 512:LACT + vc * 512 + 512],
                start=(vc == 0), stop=(vc == 15)),
                inc=(n_pe if vc == 15 else None))
        mmv = kb.val(n_pe)
        ev = AB[:, LVT + (dmc % 2) * 2048:LVT + (dmc % 2) * 2048 + 512]
        evw = [(n_pe, mmv)]
        if dmc >= 2:
            evw.append((n_st + str(dmc % 2), kb.val(n_st + str(dmc % 2))))
        vv = kb.op("vector", lambda E, bank=bank, ev=ev: E.tensor_copy(ev, bank[:, 0:512]),
                   wait=evw, inc=n_ve)
        kb.op("sync", lambda E, dmc=dmc, sl=sl, ev=ev: E.dma_start(
            out=t["cc0_in"][ts(dmc, 128), ts(sl, 512)], in_=ev),
            wait=[(n_ve, vv)], inc=n_st + str(dmc % 2), incn=16)
        if dmc >= 2:
            pass
    for p in range(2):
        kb.wait("vector", n_st + str(p), kb.val(n_st + str(p)))


NFC = 22
FXN = 0
FHID = 16384
FWS = 38912
FAGU = 47104
FBGF = 49408
FADF = 49920
FW72 = 50176
FPG = 51200
FPU = 52224
FPD = 53248
FEW = 54272
FDE = 55296
FBD = 56320
FHG = 58368
FRB = 59392


def emit_l0_ffn(kb):
    t = kb.tensors
    AB = t["AB"]
    n_pe = "ff_pe"; n_ve = "ff_ve"; n_ac = "ff_ac"; n_st = "ff_st"; n_gl = "ff_gl"
    kb.op("gpsimd", lambda E: E.dma_start(
        out=t["smalls"][:, 4:12].rearrange("p (a b) -> p a b", a=1),
        in_=t["eb"].rearrange("p (a b) -> p a b", a=1)), inc=n_gl, incn=16, drain=False)
    kb.op("gpsimd", lambda E: E.dma_start(
        out=AB[0:72, FBD:FBD + 2048], in_=t["bd"]), inc=n_gl, incn=16, drain=False)
    kb.op("gpsimd", lambda E: E.dma_start(
        out=AB[:, FAGU:FAGU + 1152].rearrange("p (c f) -> p c f", f=72),
        in_=t["ag"].rearrange("(c p) f -> p c f", p=128)), inc=n_gl, incn=16, drain=False)
    kb.op("gpsimd", lambda E: E.dma_start(
        out=AB[:, FAGU + 1152:FAGU + 2304].rearrange("p (c f) -> p c f", f=72),
        in_=t["au"].rearrange("(c p) f -> p c f", p=128)), inc=n_gl, incn=16, drain=False)
    kb.op("gpsimd", lambda E: E.dma_start(
        out=t["wba_s"][:, 0:128].rearrange("p (c f) -> p c f", f=8),
        in_=t["wr"].rearrange("(c p) f -> p c f", p=128)), inc=n_gl, incn=16, drain=False)
    for osl in range(2):
        emit_ffn0_slab(kb, osl)


def emit_ffn0_slab(kb, osl):
    t = kb.tensors
    AB = t["AB"]
    n_pe = "ff_pe"; n_ve = "ff_ve"; n_ac = "ff_ac"; n_st = "ff_st"
    n_wl = "ff_wl"; n_xl = "ff_xl"; n_gl = "ff_gl"
    xn = AB[:, FXN:FXN + 16384]
    TOK = 1024
    kb.wait("sync", n_pe, kb.val(n_pe))
    xv = kb.op("sync", lambda E, osl=osl: E.dma_start(
        out=AB.rearrange("p (c f) -> p c f", f=1024)[:, FXN // 1024:FXN // 1024 + 16, :],
        in_=t["xn_sp"].rearrange("(c p) s -> p c s", p=128)[:, :, ts(osl, 1024)]),
        inc=n_xl, incn=16)
    kb.wait("tensor", n_xl, kb.val(n_xl))
    kb.wait("vector", n_gl, kb.val(n_gl))
    kb.wait("tensor", n_gl, kb.val(n_gl))
    kb.wait("scalar", n_gl, kb.val(n_gl))

    for tt in range(8):
        pv = None
        for dc in range(16):
            pv = kb.op("tensor", lambda E, tt=tt, dc=dc: E.matmul(
                t["pb5"][:, 0:8], xn[:, dc * 1024 + tt * 128:dc * 1024 + tt * 128 + 128],
                t["wba_s"][:, dc * 8:dc * 8 + 8], start=(dc == 0), stop=(dc == 15)),
                inc=(n_pe if dc == 15 else None))
        R8 = t["fscr"][:, 0:64]
        sc8 = lambda i: R8[:, i * 8:(i + 1) * 8]
        c1 = t["nrmc"][:, 40:41]; c2 = t["nrmc"][:, 41:42]; c3 = t["nrmc"][:, 42:43]
        av = kb.op("scalar", lambda E: E.activation(sc8(0), t["pb5"][:, 0:8], AF.Sigmoid),
                   wait=[(n_pe, pv)], inc=n_ac)
        kb.wait("tensor", n_ac, av)
        kb.op("vector", lambda E: E.tensor_tensor(sc8(1), sc8(0), t["smalls"][:, 4:12], ALU.add),
              wait=[(n_ac, av)])
        kb.op("vector", lambda E: E.reduce_max(c1, sc8(1), axis=mybir.AxisListType.X))
        kb.op("vector", lambda E: E.tensor_scalar(sc8(2), sc8(1), c1, None, ALU.is_ge))
        kb.op("vector", lambda E: E.scalar_tensor_tensor(sc8(3), sc8(2), -1e9, sc8(1), ALU.mult, ALU.add))
        kb.op("vector", lambda E: E.reduce_max(c2, sc8(3), axis=mybir.AxisListType.X))
        kb.op("vector", lambda E: E.tensor_scalar(sc8(3), sc8(3), c2, None, ALU.is_ge))
        kb.op("vector", lambda E: E.tensor_tensor(sc8(2), sc8(2), sc8(3), ALU.add))
        kb.op("vector", lambda E: E.tensor_tensor(sc8(2), sc8(2), sc8(0), ALU.mult))
        kb.op("vector", lambda E: E.reduce_sum(c3, sc8(2), axis=mybir.AxisListType.X))
        kb.op("vector", lambda E: E.tensor_scalar(c3, c3, 1e-8, None, ALU.add))
        kb.op("vector", lambda E: E.reciprocal(c3, c3))
        vv = kb.op("vector", lambda E: E.tensor_scalar(
            t["fscr"][:, 64:72] if False else AB[:, FRB:FRB + 8],
            sc8(2), c3, 0.125, ALU.mult, ALU.mult), inc=n_ve)
        pv = kb.op("tensor", lambda E: E.matmul(
            t["pb5"][0:8, 0:128], AB[:, FRB:FRB + 8], t["ident_bf"][:, :],
            start=True, stop=True), wait=[(n_ve, vv)], inc=n_pe)
        vv = kb.op("vector", lambda E, tt=tt: E.tensor_copy(
            AB[0:8, FEW + tt * 128:FEW + tt * 128 + 128], t["pb5"][0:8, 0:128]),
            wait=[(n_pe, pv)], inc=n_ve)
        kb.wait("tensor", n_ve, vv)
    pv = kb.op("tensor", lambda E: E.matmul(
        t["pb5"][0:64, 0:512], t["rep8c"][:, :], AB[0:8, FEW:FEW + 512],
        start=True, stop=True))
    pv = kb.op("tensor", lambda E: E.matmul(
        t["pb4"][0:64, 0:512], t["rep8c"][:, :], AB[0:8, FEW + 512:FEW + 1024],
        start=True, stop=True), inc=n_pe)
    kb.op("vector", lambda E: E.tensor_copy(AB[0:64, FW72:FW72 + 512], t["pb5"][0:64, 0:512]),
          wait=[(n_pe, pv)])
    kb.op("vector", lambda E: E.tensor_copy(AB[0:64, FW72 + 512:FW72 + 1024], t["pb4"][0:64, 0:512]))
    vv = kb.op("vector", lambda E: E.memset(AB[64:72, FW72:FW72 + 1024], 0.125), inc=n_ve)
    kb.wait("tensor", n_ve, vv)

    for (aoff, dst) in ((0, FPG), (1152, FPU)):
        pv = None
        for dc in range(16):
            pv = kb.op("tensor", lambda E, dc=dc, aoff=aoff, tt2=0: E.matmul(
                t["pb4"][0:72, 0:512], AB[:, FAGU + aoff + dc * 72:FAGU + aoff + dc * 72 + 72],
                xn[:, dc * 1024:dc * 1024 + 512], start=(dc == 0), stop=(dc == 15)))
            pv = kb.op("tensor", lambda E, dc=dc, aoff=aoff: E.matmul(
                t["pb5"][0:72, 0:512], AB[:, FAGU + aoff + dc * 72:FAGU + aoff + dc * 72 + 72],
                xn[:, dc * 1024 + 512:dc * 1024 + 1024], start=(dc == 0), stop=(dc == 15)),
                inc=(n_pe if dc == 15 else None))
        vv = kb.op("vector", lambda E, dst=dst: E.tensor_tensor(
            AB[0:72, dst:dst + 512], t["pb4"][0:72, 0:512], AB[0:72, FW72:FW72 + 512], ALU.mult),
            wait=[(n_pe, pv)])
        vv = kb.op("vector", lambda E, dst=dst: E.tensor_tensor(
            AB[0:72, dst + 512:dst + 1024], t["pb5"][0:72, 0:512],
            AB[0:72, FW72 + 512:FW72 + 1024], ALU.mult), inc=n_ve)
        kb.wait("tensor", n_ve, vv)

    for fc in range(NFC):
        slot = fc % 2
        if fc >= 2:
            kb.wait("gpsimd", n_pe, kb.val(n_pe) - 3)
        wl = kb.op("gpsimd", lambda E, fc=fc, slot=slot: E.dma_start(
            out=AB.rearrange("p (s f) -> p s f", f=2048)[:, FWS // 2048 + slot, :].rearrange("p (c f) -> p c f", f=128),
            in_=t["wg"].rearrange("(c p) f -> p c f", p=128)[:, :, ts(fc, 128)]),
            inc=n_wl + str(slot), incn=16, drain=False)
        wl2 = kb.op("gpsimd", lambda E, fc=fc, slot=slot: E.dma_start(
            out=AB.rearrange("p (s f) -> p s f", f=2048)[:, FWS // 2048 + 2 + slot, :].rearrange("p (c f) -> p c f", f=128),
            in_=t["wu"].rearrange("(c p) f -> p c f", p=128)[:, :, ts(fc, 128)]),
            inc=n_wl + str(slot), incn=16, drain=False)
        bgl = kb.op("gpsimd", lambda E, fc=fc, slot=slot: E.dma_start(
            out=AB[0:72, FBGF + slot * 128:FBGF + slot * 128 + 128],
            in_=t["bg"][:, ts(fc, 128)]), inc=n_wl + str(slot), incn=16, drain=False)
        bul = kb.op("gpsimd", lambda E, fc=fc, slot=slot: E.dma_start(
            out=AB[0:72, FBGF + 256 + slot * 128:FBGF + 256 + slot * 128 + 128],
            in_=t["bu"][:, ts(fc, 128)]), inc=n_wl + str(slot), incn=16, drain=False)
        adl = kb.op("gpsimd", lambda E, fc=fc, slot=slot: E.dma_start(
            out=AB[:, FADF + slot * 128:FADF + slot * 128 + 128].rearrange("p (c f) -> p c f", f=72) if False
            else AB[:, FADF + slot * 128:FADF + slot * 128 + 72],
            in_=t["ad"][ts(fc, 128), :]), inc=n_wl + str(slot), incn=16, drain=False)
        kb.wait("tensor", n_wl + str(slot), adl)
        for half in range(2):
            gb = t[f"pb{half}"]; ub = t[f"pb{2 + half}"]
            for dc in range(16):
                kb.op("tensor", lambda E, dc=dc, slot=slot, half=half, gb=gb: E.matmul(
                    gb[:, 0:512], AB[:, FWS + slot * 2048 + dc * 128:FWS + slot * 2048 + dc * 128 + 128],
                    xn[:, dc * 1024 + half * 512:dc * 1024 + half * 512 + 512],
                    start=(dc == 0), stop=False))
                kb.op("tensor", lambda E, dc=dc, slot=slot, half=half, ub=ub: E.matmul(
                    ub[:, 0:512], AB[:, FWS + (2 + slot) * 2048 + dc * 128:FWS + (2 + slot) * 2048 + dc * 128 + 128],
                    xn[:, dc * 1024 + half * 512:dc * 1024 + half * 512 + 512],
                    start=(dc == 0), stop=False))
            kb.op("tensor", lambda E, slot=slot, half=half, gb=gb: E.matmul(
                gb[:, 0:512], AB[0:72, FBGF + slot * 128:FBGF + slot * 128 + 128],
                AB[0:72, FPG + half * 512:FPG + half * 512 + 512], start=False, stop=True))
            pv = kb.op("tensor", lambda E, slot=slot, half=half, ub=ub: E.matmul(
                ub[:, 0:512], AB[0:72, FBGF + 256 + slot * 128:FBGF + 256 + slot * 128 + 128],
                AB[0:72, FPU + half * 512:FPU + half * 512 + 512], start=False, stop=True),
                inc=n_pe)
            hg = AB[:, FHG + half * 512:FHG + half * 512 + 512]
            av = kb.op("scalar", lambda E, hg=hg, gb=gb: E.activation(hg, gb[:, 0:512], AF.Silu),
                       wait=[(n_pe, pv)], inc=n_ac)
            vv = kb.op("vector", lambda E, fc=fc, half=half, hg=hg, ub=ub: E.tensor_tensor(
                AB[:, FHID + fc * 1024 + half * 512:FHID + fc * 1024 + half * 512 + 512],
                hg, ub[:, 0:512], ALU.mult), wait=[(n_ac, av)], inc=n_ve)
            kb.wait("tensor", n_ve, vv)
        pv = kb.op("tensor", lambda E, fc=fc, slot=slot: E.matmul(
            t["pb4"][0:72, 0:512], AB[:, FADF + slot * 128:FADF + slot * 128 + 72],
            AB[:, FHID + fc * 1024:FHID + fc * 1024 + 512],
            start=(fc == 0), stop=(fc == NFC - 1)))
        pv = kb.op("tensor", lambda E, fc=fc, slot=slot: E.matmul(
            t["pb5"][0:72, 0:512], AB[:, FADF + slot * 128:FADF + slot * 128 + 72],
            AB[:, FHID + fc * 1024 + 512:FHID + fc * 1024 + 1024],
            start=(fc == 0), stop=(fc == NFC - 1)), inc=n_pe)
    pv = kb.val(n_pe)
    kb.op("vector", lambda E: E.tensor_tensor(
        AB[0:72, FPD:FPD + 512], t["pb4"][0:72, 0:512], AB[0:72, FW72:FW72 + 512], ALU.mult),
        wait=[(n_pe, pv)])
    vv = kb.op("vector", lambda E: E.tensor_tensor(
        AB[0:72, FPD + 512:FPD + 1024], t["pb5"][0:72, 0:512],
        AB[0:72, FW72 + 512:FW72 + 1024], ALU.mult), inc=n_ve)
    kb.wait("tensor", n_ve, vv)

    kb.wait("gpsimd", n_pe, kb.val(n_pe))
    for dmc in range(16):
        slot = dmc % 2
        if dmc >= 2:
            kb.wait("gpsimd", n_pe, kb.val(n_pe) - 1)
        wl = kb.op("gpsimd", lambda E, dmc=dmc, slot=slot: E.dma_start(
            out=AB[:, FWS + slot * 2816:FWS + slot * 2816 + 2816].rearrange("p (c f) -> p c f", f=128),
            in_=t["wd"].rearrange("(c p) f -> p c f", p=128)[:, :, ts(dmc, 128)]),
            inc="ff_wl" + str(slot), incn=16, drain=False)
        kb.wait("tensor", "ff_wl" + str(slot), wl)
        bank = t[f"pb{dmc % 2}"]
        bank2 = t[f"pb{2 + dmc % 2}"]
        if dmc >= 2:
            kb.wait("tensor", n_st + str(slot), kb.val(n_st + str(slot)))
        for fc in range(NFC):
            kb.op("tensor", lambda E, fc=fc, slot=slot, bank=bank: E.matmul(
                bank[:, 0:512], AB[:, FWS + slot * 2816 + fc * 128:FWS + slot * 2816 + fc * 128 + 128],
                AB[:, FHID + fc * 1024:FHID + fc * 1024 + 512],
                start=(fc == 0), stop=False))
            kb.op("tensor", lambda E, fc=fc, slot=slot, bank2=bank2: E.matmul(
                bank2[:, 0:512], AB[:, FWS + slot * 2816 + fc * 128:FWS + slot * 2816 + fc * 128 + 128],
                AB[:, FHID + fc * 1024 + 512:FHID + fc * 1024 + 1024],
                start=(fc == 0), stop=False))
        kb.op("tensor", lambda E, dmc=dmc, bank=bank: E.matmul(
            bank[:, 0:512], AB[0:72, FBD + dmc * 128:FBD + dmc * 128 + 128],
            AB[0:72, FPD:FPD + 512], start=False, stop=True))
        pv = kb.op("tensor", lambda E, dmc=dmc, bank2=bank2: E.matmul(
            bank2[:, 0:512], AB[0:72, FBD + dmc * 128:FBD + dmc * 128 + 128],
            AB[0:72, FPD + 512:FPD + 1024], start=False, stop=True), inc=n_pe)
        ev = AB[:, FDE + slot * 512:FDE + slot * 512 + 512]
        vv = kb.op("vector", lambda E, bank=bank, ev=ev: E.tensor_copy(ev[:, 0:512], bank[:, 0:512]),
                   wait=[(n_pe, pv)], inc=n_ve)
        kb.op("gpsimd", lambda E, dmc=dmc, osl=osl, ev=ev: E.dma_start(
            out=t["cc0_in"][ts(dmc, 128), osl * 1024:osl * 1024 + 512], in_=ev[:, 0:512],
            accum_op=ALU.add), wait=[(n_ve, vv)], inc=n_st + str(slot), incn=16, drain=False)
        vv = kb.op("vector", lambda E, bank2=bank2, ev=ev: E.tensor_copy(ev[:, 0:512], bank2[:, 0:512]),
                   wait=[(n_st + str(slot), kb.val(n_st + str(slot)))], inc=n_ve)
        kb.op("gpsimd", lambda E, dmc=dmc, osl=osl, ev=ev: E.dma_start(
            out=t["cc0_in"][ts(dmc, 128), osl * 1024 + 512:osl * 1024 + 1024], in_=ev[:, 0:512],
            accum_op=ALU.add), wait=[(n_ve, vv)], inc=n_st + str(slot), incn=16, drain=False)
    for p in range(2):
        kb.wait("vector", n_st + str(p), kb.val(n_st + str(p)))
        kb.wait("gpsimd", n_st + str(p), kb.val(n_st + str(p)))


def emit_l0_combine(kb):
    t = kb.tensors
    AB = t["AB"]
    cv = kb.op("gpsimd", lambda E: E.collective_compute(
        "AllReduce", ALU.add, replica_groups=PAIRS,
        ins=[t["cc0_in"].opt() if hasattr(t["cc0_in"], "opt") else t["cc0_in"]],
        outs=[t["cc0_out"].opt() if hasattr(t["cc0_out"], "opt") else t["cc0_out"]]),
        inc="cc0", drain=False)
    kb.wait("sync", "cc0", cv)
    kb.wait("gpsimd", "cc0", cv)
    emit_norm_x1(kb)


def emit_norm_x1(kb):
    t = kb.tensors
    AB = t["AB"]
    n_ldx = "n1_ld"; n_ldd = "n1_ldd"; n_pe = "n1_pe"; n_ve = "n1_ve"
    n_ac = "n1_ac"; n_cast = "n1_cast"; n_ap = "n1_ap"; n_mm2 = "n1_mm2"
    n_sx = "n1_sx"
    for tile in range(NSLAB):
        cur = t["xnA"] if tile % 2 == 0 else t["xnB"]
        if tile >= 2:
            kb.wait("vector", "n1_st" + str(tile % 2), 16 * (tile // 2))
        mm2_base = kb.semv.get(n_mm2, 0)
        for c in range(16):
            g = tile * 16 + c
            if g >= 2:
                kb.wait("sync", n_cast, g - 1)
                kb.wait("gpsimd", n_cast, g - 1)
            lv = kb.op("sync", lambda E, c=c, tile=tile:
                       E.dma_start(out=t["fscr"][:, ts(c % 2, 512)],
                                   in_=t["xT"][ts(c, 128), ts(tile, 512)]),
                       inc=n_ldx + str(c % 2), incn=16)
            dv = kb.op("gpsimd", lambda E, c=c, tile=tile:
                       E.dma_start(out=t["cscr"][:, ts(c % 2, 512)],
                                   in_=t["cc0_out"][ts(c, 128), ts(tile, 512)]),
                       inc=n_ldd + str(c % 2), incn=16, drain=False)
            vv = kb.op("vector", lambda E, c=c:
                       E.tensor_tensor(t["fscr"][:, ts(2 + c % 2, 512)],
                                       t["fscr"][:, ts(c % 2, 512)],
                                       t["cscr"][:, ts(c % 2, 512)], ALU.add),
                       wait=[(n_ldx + str(c % 2), lv), (n_ldd + str(c % 2), dv)])
            cv = kb.op("vector", lambda E, c=c, cur=cur:
                       E.tensor_copy(cur[:, ts(c, 512)], t["fscr"][:, ts(2 + c % 2, 512)]),
                       inc=n_cast)
            sxv = kb.op("sync", lambda E, c=c, tile=tile:
                        E.dma_start(out=t["x1_sp"][ts(c, 128), ts(tile, 512)],
                                    in_=t["fscr"][:, ts(2 + c % 2, 512)]),
                        wait=[(n_cast, cv)], inc=n_sx + str(c % 2), incn=16)
            kb.wait("vector", n_sx + str(c % 2), kb.val(n_sx + str(c % 2)))
            sq = kb.op("scalar", lambda E, c=c, cur=cur:
                       E.activation(AB[:, LCSC:LCSC + 512], cur[:, ts(c, 512)],
                                    AF.Square),
                       wait=([(n_cast, cv), (n_mm2, mm2_base + c)] if c >= 1
                             else [(n_cast, cv)]), inc=n_ac)
            kb.op("tensor", lambda E, c=c, cur=cur:
                  E.matmul(t["pb0"][0:1, :], t["ones_col_bf"][:, :], cur[:, ts(c, 512)],
                           start=(c == 0), stop=(c == 15)), wait=[(n_cast, cv)],
                  inc=(n_pe if c == 15 else None))
            kb.op("tensor", lambda E, c=c:
                  E.matmul(t["pb1"][0:1, :], t["ones_col_bf"][:, :],
                           AB[:, LCSC:LCSC + 512],
                           start=(c == 0), stop=(c == 15)),
                  wait=[(n_ac, sq)], inc=n_mm2)
        kb.op("tensor", lambda E: E.nop(), inc=n_pe, drain=False,
              wait=[(n_mm2, kb.val(n_mm2))])
        def R(i):
            return t["rows"][0:1, 512 * i:512 * (i + 1)]
        kb.wait("vector", n_pe, kb.val(n_pe))
        kb.op("vector", lambda E: E.tensor_scalar(R(0), t["pb0"][0:1, :], 1.0 / D, None, ALU.mult))
        kb.op("vector", lambda E: E.tensor_scalar(R(1), t["pb1"][0:1, :], 1.0 / D, None, ALU.mult))
        kb.op("vector", lambda E: E.tensor_tensor(R(2), R(0), R(0), ALU.mult))
        vv = kb.op("vector", lambda E: E.tensor_tensor(R(2), R(1), R(2), ALU.subtract), inc=n_ve)
        sv = kb.op("scalar", lambda E: E.activation(R(3), R(2), AF.Sqrt,
                                                    bias=t["epsn_col"][0:1, :]),
                   wait=[(n_ve, vv)], inc=n_ac)
        vv = kb.op("vector", lambda E: E.reciprocal(R(3), R(3)), wait=[(n_ac, sv)])
        vv = kb.op("vector", lambda E: E.tensor_tensor(R(4), R(0), R(3), ALU.mult), inc=n_ve)
        pv = kb.op("tensor", lambda E: E.matmul(t["pb2"][:, :], t["ones_row_f32"][0:1, :],
                                                R(3), start=True, stop=True),
                   wait=[(n_ve, vv)])
        pv = kb.op("tensor", lambda E: E.matmul(t["pb3"][:, :], t["ones_row_f32"][0:1, :],
                                                R(4), start=True, stop=True), inc=n_pe)
        kb.wait("vector", n_pe, kb.val(n_pe))
        for c in range(16):
            kb.op("vector", lambda E, c=c, cur=cur: E.tensor_tensor(
                cur[:, ts(c, 512)], cur[:, ts(c, 512)], t["pb2"][:, :], ALU.mult))
            kb.op("vector", lambda E, c=c, cur=cur: E.tensor_tensor(
                cur[:, ts(c, 512)], cur[:, ts(c, 512)], t["pb3"][:, :], ALU.subtract),
                inc=n_ap)
        kb.op("sync", lambda E, tile=tile, cur=cur: E.dma_start(
            out=t["xn1_sp"].rearrange("(c p) s -> p c s", p=128)[:, :, ts(tile, 512)],
            in_=cur.rearrange("p (c f) -> p c f", f=512)),
            wait=[(n_ap, kb.val(n_ap))], inc="n1_st" + str(tile % 2), incn=16)
        kb.wait("tensor", n_ap, kb.val(n_ap))
    for p in range(2):
        if kb.semv.get("n1_st" + str(p)):
            kb.wait("sync", "n1_st" + str(p), kb.val("n1_st" + str(p)))
        if kb.semv.get(n_sx + str(p)):
            kb.wait("sync", n_sx + str(p), kb.val(n_sx + str(p)))


def emit_l1(kb):
    emit_l1_attn(kb)
    kb.barrier()
    emit_l1_ffn(kb)
    kb.barrier()


A1K = 0
A1VT = 16384
A1XN = 32768
A1Q = 40960
A1P = 45056
A1PT = 47104
A1O = 55296
A1W = 59392
A1E = 40960


def emit_l1_attn(kb):
    t = kb.tensors
    AB = t["AB"]
    n_wl = "a1_wl"; n_pe = "a1_pe"; n_ve = "a1_ve"; n_ac = "a1_ac"
    n_xl = "a1_xl"; n_tr = "a1_tr"; n_st = "a1_st"

    for tt in range(4):
        xv = kb.op("sync", lambda E, tt=tt: E.dma_start(
            out=AB.rearrange("p (c f) -> p c f", f=512)[:, A1XN // 512:A1XN // 512 + 16, :],
            in_=t["xn1_sp"].rearrange("(c p) s -> p c s", p=128)[:, :, ts(tt, 512)]),
            inc=n_xl + str(tt % 2), incn=16, wait=[(n_pe, kb.val(n_pe))])
        kb.wait("tensor", n_xl + str(tt % 2), xv)
        for cc in range(16):
            isv = cc >= 8
            head = cc - 8 if isv else cc
            wcol = (2048 if isv else 1024) + head * 128
            slot = cc % 2
            if tt * 16 + cc >= 2:
                kb.wait("gpsimd", n_pe, kb.val(n_pe) - 1)
            wl = kb.op("gpsimd", lambda E, wcol=wcol, slot=slot: E.dma_start(
                out=AB[:, A1W + slot * 2048:A1W + slot * 2048 + 2048].rearrange("p (c f) -> p c f", f=128),
                in_=t["wpT_c"].rearrange("(c p) f -> p c f", p=128)[:, :, wcol:wcol + 128]),
                inc=n_wl + str(slot), incn=16, drain=False)
            kb.wait("tensor", n_wl + str(slot), wl)
            bank = t[f"pb{cc % 2}"]
            kb.wait("tensor", n_ve, kb.val(n_ve))
            for dc in range(16):
                kb.op("tensor", lambda E, dc=dc, slot=slot, bank=bank: E.matmul(
                    bank[:, 0:512], AB[:, A1W + slot * 2048 + dc * 128:A1W + slot * 2048 + dc * 128 + 128],
                    AB[:, A1XN + dc * 512:A1XN + dc * 512 + 512],
                    start=(dc == 0), stop=(dc == 15)), inc=(n_pe if dc == 15 else None))
            mmv = kb.val(n_pe)
            if not isv:
                vv = kb.op("vector", lambda E, head=head, tt=tt, bank=bank: E.tensor_copy(
                    AB[:, A1K + head * 2048 + tt * 512:A1K + head * 2048 + tt * 512 + 512],
                    bank[:, 0:512]), wait=[(n_pe, mmv)], inc=n_ve)
            else:
                ev = AB[:, A1E + (cc % 2) * 512:A1E + (cc % 2) * 512 + 512]
                vv = kb.op("vector", lambda E, ev=ev, bank=bank: E.tensor_copy(ev, bank[:, 0:512]),
                           wait=[(n_pe, mmv), (n_tr, kb.val(n_tr))], inc=n_ve)
                for j in range(4):
                    kb.op("scalar", lambda E, head=head, tt=tt, j=j, ev=ev: E.dma_start_transpose(
                        out=AB[:, A1VT + head * 2048 + (tt * 4 + j) * 128:A1VT + head * 2048 + (tt * 4 + j) * 128 + 128],
                        in_=ev[:, j * 128:j * 128 + 128]),
                        wait=([(n_ve, vv)] if j == 0 else None), inc=n_tr, incn=16, drain=False)
    kb.wait("tensor", n_tr, kb.val(n_tr))
    kb.wait("sync", n_tr, kb.val(n_tr))

    for qs in range(4):
        xv = kb.op("sync", lambda E, qs=qs: E.dma_start(
            out=AB.rearrange("p (c f) -> p c f", f=512)[:, A1XN // 512:A1XN // 512 + 16, :],
            in_=t["xn1_sp"].rearrange("(c p) s -> p c s", p=128)[:, :, ts(qs, 512)]),
            inc=n_xl + str(qs % 2), incn=16,
            wait=[(n_pe, kb.val(n_pe))])
        kb.wait("tensor", n_xl + str(qs % 2), xv)
        for p_ in range(2):
            kb.wait("scalar", n_st + str(p_), kb.val(n_st + str(p_)))
            kb.wait("vector", n_st + str(p_), kb.val(n_st + str(p_)))
        for head in range(8):
            slot = head % 2
            if qs * 8 + head >= 2:
                kb.wait("gpsimd", n_pe, kb.val(n_pe) - 1)
            else:
                kb.wait("gpsimd", n_pe, kb.val(n_pe))
            wl = kb.op("gpsimd", lambda E, head=head, slot=slot: E.dma_start(
                out=AB[:, A1W + slot * 2048:A1W + slot * 2048 + 2048].rearrange("p (c f) -> p c f", f=128),
                in_=t["wpT_c"].rearrange("(c p) f -> p c f", p=128)[:, :, head * 128:head * 128 + 128]),
                inc=n_wl + str(slot), incn=16, drain=False)
            kb.wait("tensor", n_wl + str(slot), wl)
            bank = t[f"pb{head % 2}"]
            kb.wait("tensor", n_ve, kb.val(n_ve))
            for dc in range(16):
                kb.op("tensor", lambda E, dc=dc, slot=slot, bank=bank: E.matmul(
                    bank[:, 0:512], AB[:, A1W + slot * 2048 + dc * 128:A1W + slot * 2048 + dc * 128 + 128],
                    AB[:, A1XN + dc * 512:A1XN + dc * 512 + 512],
                    start=(dc == 0), stop=(dc == 15)), inc=(n_pe if dc == 15 else None))
            vv = kb.op("vector", lambda E, head=head, bank=bank: E.tensor_scalar(
                AB[:, A1Q + head * 512:A1Q + head * 512 + 512], bank[:, 0:512],
                HD1 ** -0.5, None, ALU.mult), wait=[(n_pe, kb.val(n_pe))], inc=n_ve)
        kb.wait("tensor", n_ve, kb.val(n_ve))
        for head in range(8):
            for qt in range(4):
                gq = qs * 4 + qt
                kmax = (gq + 1) * 128
                nk5 = (kmax + 511) // 512
                Qt = AB[:, A1Q + head * 512 + qt * 128:A1Q + head * 512 + qt * 128 + 128]
                pv = None
                for k5 in range(nk5):
                    kw = min(512, kmax - k5 * 512)
                    pv = kb.op("tensor", lambda E, head=head, k5=k5, kw=kw, Qt=Qt: E.matmul(
                        t[f"pb{k5}"][:, 0:kw], Qt,
                        AB[:, A1K + head * 2048 + k5 * 512:A1K + head * 2048 + k5 * 512 + kw],
                        start=True, stop=True), inc=n_pe)
                mx = t["nrmc"][:, 48:49]
                m2 = t["nrmc"][:, 49:50]
                first = True
                kb.wait("vector", n_pe, pv)
                for k5 in range(nk5):
                    kw = min(512, kmax - k5 * 512)
                    if first:
                        kb.op("vector", lambda E, k5=k5, kw=kw: E.reduce_max(
                            mx, t[f"pb{k5}"][:, 0:kw], axis=mybir.AxisListType.X))
                        first = False
                    else:
                        kb.op("vector", lambda E, k5=k5, kw=kw: E.reduce_max(
                            m2, t[f"pb{k5}"][:, 0:kw], axis=mybir.AxisListType.X))
                        kb.op("vector", lambda E: E.tensor_tensor(mx, mx, m2, ALU.max))
                vv = kb.op("vector", lambda E: E.tensor_scalar(m2, mx, -1.0, None, ALU.mult), inc=n_ve)
                sums = t["nrmc"][:, 50:55]
                for k5 in range(nk5):
                    kw = min(512, kmax - k5 * 512)
                    kb.op("scalar", lambda E, k5=k5, kw=kw: E.activation(
                        AB[:, A1P + k5 * 512:A1P + k5 * 512 + kw], t[f"pb{k5}"][:, 0:kw],
                        AF.Exp, bias=m2, accum_out=sums[:, k5:k5 + 1]),
                        wait=[(n_ve, vv)], inc=(n_ac if k5 == nk5 - 1 else None))
                av = kb.val(n_ac)
                vv = kb.op("vector", lambda E, gq=gq: E.tensor_tensor(
                    AB[:, A1P + gq * 128:A1P + gq * 128 + 128],
                    AB[:, A1P + gq * 128:A1P + gq * 128 + 128],
                    t["m_low_incl01"][:, :], ALU.mult), wait=[(n_ac, av)])
                dk5 = gq // 4
                kwd = min(512, kmax - dk5 * 512)
                kb.op("vector", lambda E, dk5=dk5, kwd=kwd: E.reduce_sum(
                    sums[:, dk5:dk5 + 1], AB[:, A1P + dk5 * 512:A1P + dk5 * 512 + kwd],
                    axis=mybir.AxisListType.X))
                tot = t["nrmc"][:, 55:56]
                kb.op("vector", lambda E, nk5=nk5: E.reduce_sum(
                    tot, sums[:, 0:nk5], axis=mybir.AxisListType.X))
                vv = kb.op("vector", lambda E: E.reciprocal(tot, tot), inc=n_ve)
                for k5 in range(nk5):
                    kw = min(512, kmax - k5 * 512)
                    kb.op("vector", lambda E, k5=k5, kw=kw: E.tensor_scalar(
                        AB[:, A1P + k5 * 512:A1P + k5 * 512 + kw],
                        AB[:, A1P + k5 * 512:A1P + k5 * 512 + kw], tot, None, ALU.mult),
                        inc=(n_ve if k5 == nk5 - 1 else None))
                vv = kb.val(n_ve)
                nkc = kmax // 128
                for kc in range(nkc):
                    kb.op("scalar", lambda E, kc=kc, qt=qt: E.dma_start_transpose(
                        out=AB[:, A1PT + kc * 512 + qt * 128:A1PT + kc * 512 + qt * 128 + 128],
                        in_=AB[:, A1P + kc * 128:A1P + kc * 128 + 128]),
                        wait=([(n_ve, vv)] if kc == 0 else None), inc=n_tr, incn=16, drain=False)
                kb.wait("tensor", n_tr, kb.val(n_tr))
                kb.wait("vector", n_tr, kb.val(n_tr))
            nkc_s = ((qs + 1) * 512) // 128
            pv = None
            for kc in range(nkc_s):
                pv = kb.op("tensor", lambda E, head=head, kc=kc, nkc_s=nkc_s: E.matmul(
                    t["pb5"][:, 0:512],
                    AB[:, A1VT + head * 2048 + kc * 128:A1VT + head * 2048 + kc * 128 + 128],
                    AB[:, A1PT + kc * 512:A1PT + kc * 512 + 512],
                    start=(kc == 0), stop=(kc == nkc_s - 1)), inc=(n_pe if kc == nkc_s - 1 else None))
            vv = kb.op("vector", lambda E, head=head: E.tensor_copy(
                AB[:, A1O + head * 512:A1O + head * 512 + 512], t["pb5"][:, 0:512]),
                wait=[(n_pe, pv)], inc=n_ve)
            kb.wait("tensor", n_ve, vv)
            kb.wait("scalar", n_ve, vv)
        kb.wait("gpsimd", n_pe, kb.val(n_pe))
        for dmc in range(16):
            slot = dmc % 2
            if dmc >= 2:
                kb.wait("gpsimd", n_pe, kb.val(n_pe) - 1)
            wl = kb.op("gpsimd", lambda E, dmc=dmc, slot=slot: E.dma_start(
                out=AB[:, A1Q + slot * 1024:A1Q + slot * 1024 + 1024].rearrange("p (c f) -> p c f", f=128),
                in_=t["woutT1_c"].rearrange("(c p) f -> p c f", p=128)[:, :, ts(dmc, 128)]),
                inc="a1o_wl" + str(slot), incn=16, drain=False)
            kb.wait("tensor", "a1o_wl" + str(slot), wl)
            bank = t[f"pb{dmc % 2}"]
            if dmc >= 2:
                kb.wait("tensor", n_st + str(slot), kb.val(n_st + str(slot)))
            for hc in range(8):
                kb.op("tensor", lambda E, hc=hc, slot=slot, bank=bank: E.matmul(
                    bank[:, 0:512], AB[:, A1Q + slot * 1024 + hc * 128:A1Q + slot * 1024 + hc * 128 + 128],
                    AB[:, A1O + hc * 512:A1O + hc * 512 + 512],
                    start=(hc == 0), stop=(hc == 7)), inc=(n_pe if hc == 7 else None))
            ev = AB[:, A1P + slot * 512:A1P + slot * 512 + 512]
            vv = kb.op("vector", lambda E, ev=ev, bank=bank: E.tensor_copy(ev, bank[:, 0:512]),
                       wait=[(n_pe, kb.val(n_pe))], inc=n_ve)
            half = qs // 2
            off = (qs % 2) * 512
            kb.op("sync", lambda E, dmc=dmc, half=half, off=off, ev=ev: E.dma_start(
                out=t["cc1_in"][half, dmc, :, off:off + 512], in_=ev),
                wait=[(n_ve, vv)], inc=n_st + str(slot), incn=16)
    for p in range(2):
        kb.wait("sync", n_st + str(p), kb.val(n_st + str(p)))
        kb.wait("gpsimd", n_st + str(p), kb.val(n_st + str(p)))


BXN = 0
BHID = 16384
BWS = 38912
BQ1 = 38912
BQ2 = 43008
BQ3 = 47104
BHG = 52736
BDE = 53760
BX1 = 54784


def emit_l1_ffn(kb):
    t = kb.tensors
    AB = t["AB"]
    n_pe = "b1_pe"; n_ve = "b1_ve"; n_ac = "b1_ac"; n_st = "b1_st"
    n_gl = "b1_gl"; n_xl = "b1_xl"
    accs = t["nrmc"][:, 56:60]
    kb.op("vector", lambda E: E.memset(t["nrmc"][:, 56:60], 0.0), inc=n_ve)
    for wi, wname in enumerate(["w1T", "w2T", "w3T"]):
        nchunks = 16 if wi < 2 else 22
        width = 2816 if wi < 2 else 2048
        for c in range(nchunks):
            lv = kb.op("sync", lambda E, wname=wname, c=c, width=width: E.dma_start(
                out=t["fscr"][:, 0:width] if width == 2048 else t["AF"][:, 0:2816],
                in_=t[wname][ts(c, 128), :]), inc="b1g_ld", incn=16,
                wait=[("b1g_ve", kb.val("b1g_ve"))])
            nsub = width // 512 + (1 if width % 512 else 0)
            for s in range(width // 512 + (1 if width % 512 else 0)):
                sw = min(512, width - s * 512)
                av = kb.op("scalar", lambda E, s=s, sw=sw, width=width: E.activation(
                    AB[:, BHG:BHG + sw],
                    (t["fscr"][:, 0:width] if width == 2048 else t["AF"][:, 0:2816])[:, s * 512:s * 512 + sw],
                    AF.Abs, accum_out=t["nrmc"][:, 60:61]),
                    wait=[("b1g_ld", lv), ("b1g_ve", kb.val("b1g_ve"))], inc=n_ac)
                kb.op("vector", lambda E, wi=wi: E.tensor_tensor(
                    accs[:, wi:wi + 1], accs[:, wi:wi + 1], t["nrmc"][:, 60:61], ALU.add),
                    wait=[(n_ac, av)], inc="b1g_ve")
    vv = kb.val("b1g_ve")
    pv = kb.op("tensor", lambda E: E.matmul(
        t["pb0"][0:4, 0:1], accs[:, 0:4], t["one_col"][:, 0:1], start=True, stop=True),
        wait=[("b1g_ve", vv)], inc=n_pe)
    vv = kb.op("vector", lambda E: E.tensor_copy(t["nrmc"][0:4, 61:62], t["pb0"][0:4, 0:1]),
               wait=[(n_pe, pv)], inc=n_ve)
    sv = kb.op("sync", lambda E: E.dma_start(out=t["ccg_in"], in_=t["nrmc"][0:4, 61:62]),
               wait=[(n_ve, vv)], inc="b1g_st", incn=16)
    kb.wait("gpsimd", "b1g_st", kb.val("b1g_st"))
    cv = kb.op("gpsimd", lambda E: E.collective_compute(
        "AllReduce", ALU.add, replica_groups=PAIRS,
        ins=[t["ccg_in"]], outs=[t["ccg_out"]]), inc="ccg", drain=False)
    kb.wait("sync", "ccg", cv)
    lv = kb.op("sync", lambda E: E.dma_start(out=t["nrmc"][0:4, 61:62], in_=t["ccg_out"]),
               inc="b1g_ld", incn=16)
    kb.wait("vector", "b1g_ld", kb.val("b1g_ld"))
    kb.op("vector", lambda E: E.tensor_scalar(t["nrmc"][0:4, 62:63], t["nrmc"][0:4, 61:62],
                                              1.0 / (DFF1 * D), 1e-8, ALU.mult, ALU.add))
    vv = kb.op("vector", lambda E: E.reciprocal(t["nrmc"][0:4, 62:63], t["nrmc"][0:4, 62:63]), inc=n_ve)
    pv = kb.op("tensor", lambda E: E.matmul(
        t["pb0"][0:1, 0:4], t["nrmc"][0:4, 62:63], t["idf16"][0:4, 0:4],
        start=True, stop=True), wait=[(n_ve, vv)], inc=n_pe)
    vv = kb.op("vector", lambda E: E.tensor_copy(t["rows"][0:1, 0:4], t["pb0"][0:1, 0:4]),
               wait=[(n_pe, pv)], inc=n_ve)
    kb.wait("tensor", n_ve, vv)
    for wi in range(3):
        pv = kb.op("tensor", lambda E, wi=wi: E.matmul(
            t["pb1"][:, wi:wi + 1], t["ones_row_f32"][0:1, :], t["rows"][0:1, wi:wi + 1],
            start=True, stop=True), inc=(n_pe if wi == 2 else None))
    vv = kb.op("vector", lambda E: E.tensor_copy(t["nrmc"][:, 57:60], t["pb1"][:, 0:3]),
               wait=[(n_pe, kb.val(n_pe))], inc=n_ve)
    kb.wait("sync", n_ve, vv)
    kb.wait("gpsimd", n_ve, vv)

    def quantize(wslot_fp, dst, rg, wv, tag):
        W = 2048
        ops = []
        for s in range(W // 512):
            sl = slice(s * 512, s * 512 + 512)
            kb.op("vector", lambda E, sl=sl, wslot_fp=wslot_fp, rg=rg: E.tensor_scalar(
                AB[:, BHG:BHG + 512], wslot_fp[:, sl], rg, 0.5, ALU.mult, ALU.is_ge),
                wait=([(tag, wv)] if s == 0 else None))
            kb.op("vector", lambda E, sl=sl, wslot_fp=wslot_fp, rg=rg: E.tensor_scalar(
                AB[:, BHG + 512:BHG + 1024], wslot_fp[:, sl], rg, -0.5, ALU.mult, ALU.is_le))
            kb.op("vector", lambda E, sl=sl, dst=dst: E.tensor_tensor(
                dst[:, sl], AB[:, BHG:BHG + 512], AB[:, BHG + 512:BHG + 1024], ALU.subtract),
                inc=n_ve)
        return kb.val(n_ve)

    rg1 = t["nrmc"][:, 57:58]; rg2 = t["nrmc"][:, 58:59]; rg3 = t["nrmc"][:, 59:60]
    for osl in range(2):
        kb.wait("sync", n_pe, kb.val(n_pe))
        xv = kb.op("sync", lambda E, osl=osl: E.dma_start(
            out=AB.rearrange("p (c f) -> p c f", f=1024)[:, BXN // 1024:BXN // 1024 + 16, :],
            in_=t["xn1_sp"].rearrange("(c p) s -> p c s", p=128)[:, :, ts(osl, 1024)]),
            inc=n_xl, incn=16)
        kb.wait("tensor", n_xl, kb.val(n_xl))
        xn = AB[:, BXN:BXN + 16384]
        for fc in range(NFC):
            slot = fc % 2
            if fc >= 2:
                kb.wait("sync", n_pe, kb.val(n_pe) - 3)
            lv1 = kb.op("sync", lambda E, fc=fc, slot=slot: E.dma_start(
                out=t["fscr"][:, 0:2048].rearrange("p (c f) -> p c f", f=128) if False
                else t["fscr"][:, 0:2048],
                in_=t["w1T"].rearrange("(c p) f -> p c f", p=128)[:, :, ts(fc, 128)].rearrange("p c f -> p (c f)") if False
                else t["w1T"].rearrange("(c p) f -> p c f", p=128)[:, :, ts(fc, 128)]),
                inc="b1w_ld" + str(slot), incn=16, wait=[(n_ve, kb.val(n_ve))])
            qv1 = quantize(t["fscr"][:, 0:2048], AB[:, BQ1 + slot * 2048:BQ1 + slot * 2048 + 2048],
                           rg1, lv1, "b1w_ld" + str(slot))
            lv2 = kb.op("sync", lambda E, fc=fc, slot=slot: E.dma_start(
                out=t["fscr"][:, 0:2048],
                in_=t["w2T"].rearrange("(c p) f -> p c f", p=128)[:, :, ts(fc, 128)]),
                inc="b1w_ld" + str(slot), incn=16, wait=[(n_ve, qv1)])
            qv2 = quantize(t["fscr"][:, 0:2048], AB[:, BQ2 + slot * 2048:BQ2 + slot * 2048 + 2048],
                           rg2, lv2, "b1w_ld" + str(slot))
            kb.wait("tensor", n_ve, qv2)
            for half in range(2):
                gb = t[f"pb{half}"]; ub = t[f"pb{2 + half}"]
                for dc in range(16):
                    kb.op("tensor", lambda E, dc=dc, slot=slot, half=half, gb=gb: E.matmul(
                        gb[:, 0:512], AB[:, BQ1 + slot * 2048 + dc * 128:BQ1 + slot * 2048 + dc * 128 + 128],
                        xn[:, dc * 1024 + half * 512:dc * 1024 + half * 512 + 512],
                        start=(dc == 0), stop=(dc == 15)))
                    kb.op("tensor", lambda E, dc=dc, slot=slot, half=half, ub=ub: E.matmul(
                        ub[:, 0:512], AB[:, BQ2 + slot * 2048 + dc * 128:BQ2 + slot * 2048 + dc * 128 + 128],
                        xn[:, dc * 1024 + half * 512:dc * 1024 + half * 512 + 512],
                        start=(dc == 0), stop=(dc == 15)),
                        inc=(n_pe if dc == 15 else None))
                pv = kb.val(n_pe)
                hg = AB[:, BHG + half * 512:BHG + half * 512 + 512]
                av = kb.op("scalar", lambda E, hg=hg, gb=gb: E.activation(hg, gb[:, 0:512], AF.Sigmoid),
                           wait=[(n_pe, pv)], inc=n_ac)
                vv = kb.op("vector", lambda E, fc=fc, half=half, hg=hg, ub=ub: E.tensor_tensor(
                    AB[:, BHID + fc * 1024 + half * 512:BHID + fc * 1024 + half * 512 + 512],
                    hg, ub[:, 0:512], ALU.mult), wait=[(n_ac, av)], inc=n_ve)
                kb.wait("tensor", n_ve, vv)
        kb.wait("sync", n_pe, kb.val(n_pe))
        for dmc in range(16):
            slot = dmc % 2
            if dmc >= 2:
                kb.wait("sync", n_pe, kb.val(n_pe) - 1)
            lv3 = kb.op("sync", lambda E, dmc=dmc: E.dma_start(
                out=t["AF"][:, 0:2816],
                in_=t["w3T"].rearrange("(c p) f -> p c f", p=128)[:, :, ts(dmc, 128)]),
                inc="b1w_ld0", incn=16, wait=[(n_ve, kb.val(n_ve))])
            for s in range(6):
                sw = min(512, 2816 - s * 512)
                kb.op("vector", lambda E, s=s, sw=sw: E.tensor_scalar(
                    AB[:, BHG:BHG + sw], t["AF"][:, s * 512:s * 512 + sw], rg3, 0.5,
                    ALU.mult, ALU.is_ge), wait=([("b1w_ld0", lv3)] if s == 0 else None))
                kb.op("vector", lambda E, s=s, sw=sw: E.tensor_scalar(
                    AB[:, BHG + 512:BHG + 512 + sw], t["AF"][:, s * 512:s * 512 + sw], rg3, -0.5,
                    ALU.mult, ALU.is_le))
                kb.op("vector", lambda E, s=s, sw=sw, slot=slot: E.tensor_tensor(
                    AB[:, BQ3 + slot * 2816 + s * 512:BQ3 + slot * 2816 + s * 512 + sw],
                    AB[:, BHG:BHG + sw], AB[:, BHG + 512:BHG + 512 + sw], ALU.subtract),
                    inc=n_ve)
            kb.wait("tensor", n_ve, kb.val(n_ve))
            bank = t[f"pb{dmc % 2}"]; bank2 = t[f"pb{2 + dmc % 2}"]
            if dmc >= 2:
                kb.wait("tensor", n_st + str(slot), kb.val(n_st + str(slot)))
            for fc in range(NFC):
                kb.op("tensor", lambda E, fc=fc, slot=slot, bank=bank: E.matmul(
                    bank[:, 0:512], AB[:, BQ3 + slot * 2816 + fc * 128:BQ3 + slot * 2816 + fc * 128 + 128],
                    AB[:, BHID + fc * 1024:BHID + fc * 1024 + 512],
                    start=(fc == 0), stop=(fc == NFC - 1)))
                kb.op("tensor", lambda E, fc=fc, slot=slot, bank2=bank2: E.matmul(
                    bank2[:, 0:512], AB[:, BQ3 + slot * 2816 + fc * 128:BQ3 + slot * 2816 + fc * 128 + 128],
                    AB[:, BHID + fc * 1024 + 512:BHID + fc * 1024 + 1024],
                    start=(fc == 0), stop=(fc == NFC - 1)), inc=(n_pe if fc == NFC - 1 else None))
            pv = kb.val(n_pe)
            for half in range(2):
                bk = bank if half == 0 else bank2
                ev = AB[:, BDE + slot * 512:BDE + slot * 512 + 512]
                w = [(n_pe, pv), (n_st + str(slot), kb.val(n_st + str(slot)))]
                vv = kb.op("vector", lambda E, bk=bk, ev=ev: E.tensor_copy(ev, bk[:, 0:512]),
                           wait=w, inc=n_ve)
                kb.op("gpsimd", lambda E, dmc=dmc, osl=osl, half=half, ev=ev: E.dma_start(
                    out=t["cc1_in"][osl, dmc, :, half * 512:half * 512 + 512], in_=ev,
                    accum_op=ALU.add), wait=[(n_ve, vv)], inc=n_st + str(slot), incn=16, drain=False)
    for p in range(2):
        kb.wait("gpsimd", n_st + str(p), kb.val(n_st + str(p)))
        kb.wait("sync", n_st + str(p), kb.val(n_st + str(p)))


def emit_final(kb):
    t = kb.tensors
    AB = t["AB"]
    n_ld = "fn_ld"; n_ve = "fn_ve"; n_st = "fn_st"
    for c in range(16):
        for half in range(2):
            for off in range(2):
                i = (c * 4 + half * 2 + off) % 2
                lv = kb.op("sync", lambda E, c=c, half=half, off=off, i=i: E.dma_start(
                    out=t["fscr"][:, ts(i, 512)],
                    in_=t["x1_sp"][ts(c, 128), half * 1024 + off * 512:half * 1024 + off * 512 + 512]),
                    inc=n_ld + str(i), incn=16,
                    wait=[(n_st + str(i), kb.val(n_st + str(i)))])
                vv = kb.op("vector", lambda E, i=i: E.tensor_scalar(
                    AB[:, BX1 + i * 512:BX1 + i * 512 + 512], t["fscr"][:, ts(i, 512)],
                    0.5, None, ALU.mult), wait=[(n_ld + str(i), lv)], inc=n_ve)
                kb.op("gpsimd", lambda E, c=c, half=half, off=off, i=i: E.dma_start(
                    out=t["cc1_in"][half, c, :, off * 512:off * 512 + 512],
                    in_=AB[:, BX1 + i * 512:BX1 + i * 512 + 512], accum_op=ALU.add),
                    wait=[(n_ve, vv)], inc=n_st + str(i), incn=16, drain=False)
    for p in range(2):
        kb.wait("gpsimd", n_st + str(p), kb.val(n_st + str(p)))
    kb.barrier()
    cv = kb.op("gpsimd", lambda E: E.collective_compute(
        "ReduceScatter", ALU.add, replica_groups=PAIRS,
        ins=[t["cc1_in"]], outs=[t["cc1_out"]]), inc="cc1", drain=False)
    kb.wait("sync", "cc1", cv)
    for c in range(16):
        for off in range(2):
            i = (c * 2 + off) % 2
            lv = kb.op("sync", lambda E, c=c, off=off, i=i: E.dma_start(
                out=AB[:, BX1 + i * 512:BX1 + i * 512 + 512],
                in_=t["cc1_out"][c, :, off * 512:off * 512 + 512]),
                inc="fn_ld" + str(i), incn=16,
                wait=[("fo_s" + str(i), kb.val("fo_s" + str(i)))])
            vv = kb.op("vector", lambda E, i=i: E.tensor_copy(
                t["fscr"][:, ts(i, 512)], AB[:, BX1 + i * 512:BX1 + i * 512 + 512]),
                wait=[("fn_ld" + str(i), lv)], inc="fn_ve")
            kb.op("sync", lambda E, c=c, off=off, i=i: E.dma_start(
                out=t["out"][c, :, off * 512:off * 512 + 512], in_=t["fscr"][:, ts(i, 512)]),
                wait=[("fn_ve", vv)], inc="fo_s" + str(i), incn=16)
    for p in range(2):
        kb.wait("sync", "fo_s" + str(p), kb.val("fo_s" + str(p)))


def kernel(**inputs):
    in_maps = [_prep_core_inputs(inputs, cid) for cid in range(NCORES)]
    nc, kb = build_program()
    res = run_bass_kernel_spmd(nc, in_maps, core_ids=list(range(NCORES)))
    out = np.empty((B, S, D), np.float32)
    for cid in range(NCORES):
        b, h = cid // 2, cid % 2
        o = np.asarray(res.results[cid]["out"])
        out[b, 1024 * h:1024 * h + 1024, :] = o.reshape(D, OWN).T
    return out


# version 04
# speedup vs baseline: 1.0407x; 1.0407x over previous
import math
import numpy as np
import concourse.bass as bass
import concourse.mybir as mybir
from concourse.bass_utils import run_bass_kernel_spmd

F32 = mybir.dt.float32
BF16 = mybir.dt.bfloat16
AF = mybir.ActivationFunctionType
ALU = mybir.AluOpType

B, S, D = 4, 2048, 2048
NCORES = 8
HK, HV, DK, DV = 16, 32, 128, 128
KH_C, VH_C = 8, 16
E_EXP, RANK, DFF = 8, 8, 5504
DFF1 = 5632
NH1, HD1 = 16, 128
OWN = 1024
CH = 128
SLAB = 512
NSLAB = S // SLAB
CPS = SLAB // CH
PAIRS = [[0, 1], [2, 3], [4, 5], [6, 7]]

DBG = []
DBG_SHAPES = {}
DBG_DTYPES = {}


def ts(i, n):
    return slice(i * n, (i + 1) * n)


class KB:
    ENGINES = ("sync", "tensor", "vector", "scalar", "gpsimd")

    def __init__(self, nc):
        self.nc = nc
        self.streams = {e: [] for e in self.ENGINES}
        self.semv = {}
        self.sem_names = []
        self.sems = {}
        self.tensors = {}

    def sem(self, name):
        if name not in self.semv:
            self.semv[name] = 0
            self.sem_names.append(name)
        return name

    def inc(self, name, n=1):
        self.semv[self.sem(name)] += n
        return self.semv[name]

    def val(self, name):
        return self.semv[self.sem(name)]

    def em(self, eng, fn):
        self.streams[eng].append(fn)

    def wait(self, eng, sem, v):
        self.sem(sem)
        self.em(eng, lambda E, s=sem, v=v: E.wait_ge(self.sems[s], v))

    def op(self, eng, fn, inc=None, incn=1, wait=None, drain=None):
        if drain is None:
            drain = eng in ("vector", "scalar", "gpsimd")
        if wait:
            for (s, v) in wait:
                self.wait(eng, s, v)
        if drain:
            self.em(eng, lambda E: E.drain())
        if inc is not None:
            self.sem(inc)
            self.inc(inc, incn)

            def run(E, fn=fn, s=inc, n=incn):
                fn(E).then_inc(self.sems[s], n)
        else:
            def run(E, fn=fn):
                fn(E)
        self.em(eng, run)
        return self.val(inc) if inc is not None else None

    def barrier(self):
        tg = {}
        for e in self.ENGINES:
            tg[e] = self.inc(self.sem("bar_" + e))
        for e in self.ENGINES:
            if e != "sync":
                self.em(e, lambda E: E.drain())
            self.em(e, lambda E, s="bar_" + e: E.nop().then_inc(self.sems[s], 1))
        for e in self.ENGINES:
            for o in self.ENGINES:
                if o != e:
                    self.wait(e, "bar_" + o, tg[o])

    def dbg(self, name, src_ap_fn):
        if name not in DBG:
            return
        t = self.tensors
        self.barrier()
        v = self.op("sync", lambda E: E.dma_start(out=t["dbg_" + name],
                                                  in_=src_ap_fn()),
                    inc="dbgst", incn=16)
        self.wait("sync", "dbgst", v)
        self.barrier()


def _prep_core_inputs(inp, cid):
    b, h = cid // 2, cid % 2
    T = np.ascontiguousarray
    d = {}
    x = np.asarray(inp["L0_attn_hidden_states"])
    d["xT"] = T(x[b].T)

    wqkv = np.asarray(inp["L0_attn_w_qkv"])
    qcols = wqkv[:, 1024 * h:1024 * h + 1024]
    kcols = wqkv[:, 2048 + 1024 * h:2048 + 1024 * h + 1024]
    vcols = wqkv[:, 4096 + 2048 * h:4096 + 2048 * h + 2048]
    d["wqkv_c"] = T(np.concatenate([qcols, kcols, vcols], axis=1))
    cw = np.asarray(inp["L0_attn_conv_w"])
    cwc = np.concatenate(
        [cw[1024 * h:1024 * h + 1024],
         cw[2048 + 1024 * h:2048 + 1024 * h + 1024],
         cw[4096 + 2048 * h:4096 + 2048 * h + 2048]], axis=0)
    d["convw_c"] = T(cwc.reshape(32, 128, 4).transpose(1, 0, 2).reshape(128, 128))
    d["wz_c"] = T(np.asarray(inp["L0_attn_w_z"])[:, 2048 * h:2048 * h + 2048])
    wb = np.asarray(inp["L0_attn_w_b"])[:, 16 * h:16 * h + 16]
    wa = np.asarray(inp["L0_attn_w_a"])[:, 16 * h:16 * h + 16]
    wba = np.concatenate([wb, wa], axis=1)
    d["wba_c"] = T(wba.reshape(16, 128, 32).transpose(1, 0, 2).reshape(128, 512))
    d["wout0_c"] = T(np.asarray(inp["L0_attn_w_out"])[2048 * h:2048 * h + 2048, :])
    d["dtb_c"] = T(np.asarray(inp["L0_attn_dt_bias"])[16 * h:16 * h + 16, None])
    d["alog_c"] = T(np.asarray(inp["L0_attn_A_log"])[16 * h:16 * h + 16, None])
    d["nw0"] = T(np.asarray(inp["L0_attn_norm_weight"])[:, None])

    d["wr"] = T(np.asarray(inp["L0_ffn_W_router"]))
    FH = 22 * 128
    def fcols(w):
        w = np.asarray(w)
        out = np.zeros((w.shape[0], FH), w.dtype)
        piece = w[:, 2816 * h:2816 * (h + 1)]
        out[:, :piece.shape[1]] = piece
        return T(out)
    def frows(w):
        w = np.asarray(w)
        out = np.zeros((FH, w.shape[1]), w.dtype)
        piece = w[2816 * h:2816 * (h + 1), :]
        out[:piece.shape[0]] = piece
        return T(out)
    d["wg"] = fcols(inp["L0_ffn_W_gate"])
    d["wu"] = fcols(inp["L0_ffn_W_up"])
    d["wd"] = frows(inp["L0_ffn_W_down"])

    def astack(sA, A):
        A = np.asarray(A)
        return T(np.concatenate([A[e] for e in range(E_EXP)] + [np.asarray(sA)], axis=1))

    def bstack(sB, Bm):
        Bm = np.asarray(Bm)
        return T(np.concatenate([Bm[e] for e in range(E_EXP)] + [np.asarray(sB)], axis=0))
    d["ag"] = astack(inp["L0_ffn_sA_gate"], inp["L0_ffn_A_gate"])
    d["bg"] = fcols(bstack(inp["L0_ffn_sB_gate"], inp["L0_ffn_B_gate"]))
    d["au"] = astack(inp["L0_ffn_sA_up"], inp["L0_ffn_A_up"])
    d["bu"] = fcols(bstack(inp["L0_ffn_sB_up"], inp["L0_ffn_B_up"]))
    d["ad"] = frows(astack(inp["L0_ffn_sA_down"], inp["L0_ffn_A_down"]))
    d["bd"] = bstack(inp["L0_ffn_sB_down"], inp["L0_ffn_B_down"])
    d["eb"] = T(np.asarray(inp["L0_ffn_expert_bias"])[None, :].repeat(OWN // 8, 0))

    wpT = np.asarray(inp["L1_attn_w_project"]).T
    cols = [wpT[:, p * 2048 + 1024 * h: p * 2048 + 1024 * h + 1024] for p in range(3)]
    d["wpT_c"] = T(np.concatenate(cols, axis=1))
    d["woutT1_c"] = T(np.asarray(inp["L1_attn_w_out"]).T[1024 * h:1024 * h + 1024, :])

    d["w1T"] = T(np.asarray(inp["L1_ffn_w1"]).T[:, 2816 * h:2816 * (h + 1)])
    d["w2T"] = T(np.asarray(inp["L1_ffn_w2"]).T[:, 2816 * h:2816 * (h + 1)])
    d["w3T"] = T(np.asarray(inp["L1_ffn_w3"]).T[2816 * h:2816 * (h + 1), :])
    return d


def build_program():
    import contextlib
    nc = bass.Bass("TRN2", target_bir_lowering=False, debug=False)
    kb = KB(nc)
    t = kb.tensors

    def din(name, shape, dt=F32):
        t[name] = nc.dram_tensor(name, shape, dt, kind="ExternalInput").ap()

    din("xT", [D, S])
    din("wqkv_c", [D, 4096]); din("convw_c", [128, 128]); din("wz_c", [D, 2048])
    din("wba_c", [128, 512]); din("wout0_c", [2048, D])
    din("dtb_c", [16, 1]); din("alog_c", [16, 1]); din("nw0", [128, 1])
    din("wr", [D, 8]); din("wg", [D, 2816]); din("wu", [D, 2816]); din("wd", [2816, D])
    din("ag", [D, 72]); din("bg", [72, 2816]); din("au", [D, 72]); din("bu", [72, 2816])
    din("ad", [2816, 72]); din("bd", [72, D]); din("eb", [128, 8])
    din("wpT_c", [D, 3072]); din("woutT1_c", [1024, D])
    din("w1T", [D, 2816]); din("w2T", [D, 2816]); din("w3T", [2816, D])

    t["out"] = nc.dram_tensor("out", [16, 128, OWN], F32, kind="ExternalOutput").ap()

    t["xn_sp"] = nc.dram_tensor("xn_sp", [D, S], BF16).ap()
    t["xn1_sp"] = nc.dram_tensor("xn1_sp", [D, S], BF16).ap()
    t["x1_sp"] = nc.dram_tensor("x1_sp", [D, S], F32).ap()
    t["cc0_in"] = nc.dram_tensor("cc0_in", [D, S], BF16).ap()
    t["cc0_out"] = nc.dram_tensor("cc0_out", [D, S], BF16).ap()
    t["cc1_in"] = nc.dram_tensor("cc1_in", [2, 16, 128, OWN], BF16).ap()
    t["cc1_out"] = nc.dram_tensor("cc1_out", [16, 128, OWN], BF16).ap()
    t["ccg_in"] = nc.dram_tensor("ccg_in", [4, 1], F32).ap()
    t["ccg_out"] = nc.dram_tensor("ccg_out", [4, 1], F32).ap()

    for name in DBG:
        t["dbg_" + name] = nc.dram_tensor(
            "dbg_" + name, DBG_SHAPES[name], DBG_DTYPES.get(name, F32),
            kind="ExternalOutput").ap()

    ctx = contextlib.ExitStack()

    def sb(name, shape, dt):
        t[name] = ctx.enter_context(nc.sbuf_tensor(name, shape, dt)).ap()

    sb("ident_bf", [128, 128], BF16)
    sb("m_low_strict01", [128, 128], BF16)
    sb("m_low_incl01", [128, 128], BF16)
    sb("neg_low_incl", [128, 128], F32)
    sb("ones_col_bf", [128, 1], BF16)
    sb("ones_row_f32", [1, 128], F32)
    sb("eps_col", [128, 1], F32)
    sb("epsn_col", [128, 1], F32)
    sb("one_col", [128, 1], F32)
    sb("it_row", [128, 128], F32)
    sb("it_col", [128, 128], F32)
    sb("smalls", [128, 32], F32)
    sb("AB", [128, 63488], BF16)
    sb("AF", [128, 11008], F32)
    sb("cstash", [128, 32 * 4], BF16)
    sb("ncg3", [48, S], BF16)
    sb("cqb", [16, S], BF16)
    sb("bscrb", [16, 1536], BF16)
    sb("sel48", [48, 16 * 128], BF16)
    sb("sel16", [16, 16 * 128], BF16)
    sb("idf16", [16, 16], F32)
    sb("ones_row_bf", [1, 128], BF16)
    sb("m_low_strictneg", [128, 128], BF16)
    sb("cw", [128, 128], F32)
    sb("wba_s", [128, 512], BF16)
    sb("cser", [16, 512], F32)
    sb("c16", [16, 64], F32)
    sb("z16", [16, 512], F32)
    sb("nrmc", [128, 64], F32)
    sb("rep8c", [8, 64], BF16)
    sb("rows_bf", [1, 128], BF16)
    sb("rows_bf2", [1, 2048], BF16)
    sb("ones16f", [16, 128], F32)

    def psum(name, shape, dt=F32):
        t[name] = ctx.enter_context(nc.psum_tensor(name, shape, dt)).ap()
    for i in range(6):
        psum(f"pb{i}", [128, 512], F32)
    psum("pbtA", [128, 1024], BF16)
    psum("pbtB", [128, 1024], BF16)

    t["fscr"] = t["AF"][:, 0:2048]
    t["ser"] = t["AF"][:, 2048:4096]
    t["smastA"] = t["AF"][:, 4096:6144]
    t["smastB"] = t["AF"][:, 6144:8192]
    t["colp"] = t["AF"][:, 8192:8448]
    t["rows"] = t["AF"][0:1, 8448:11008]
    t["baraw"] = t["AF"][0:32, 0:1024]
    t["bscr"] = t["AF"][0:16, 1024:2048]
    t["eq16"] = t["AF"][0:16, 0:2048]
    t["xnA"] = t["AB"][:, 0:8192]
    t["xnB"] = t["AB"][:, 8192:16384]
    t["cscr"] = t["AB"][:, 16384:18432]

    emit_consts(kb)
    kb.barrier()
    emit_l0(kb)
    emit_l1(kb)
    emit_final(kb)

    with ctx:
        with contextlib.ExitStack() as sctx:
            for s in kb.sem_names:
                kb.sems[s] = sctx.enter_context(nc.semaphore(s))
            block = sctx.enter_context(nc.Block())

            def register(eng_name):
                def body(E):
                    for fn in kb.streams[eng_name]:
                        fn(E)
                return body
            block.sync(register("sync"))
            block.tensor(register("tensor"))
            block.vector(register("vector"))
            block.scalar(register("scalar"))
            block.gpsimd(register("gpsimd"))
    return nc, kb


def emit_consts(kb):
    t = kb.tensors
    kb.op("gpsimd", lambda E: E.iota(t["it_row"][:, :], pattern=[[1, 128]], base=0,
                                     channel_multiplier=0,
                                     allow_small_or_imprecise_dtypes=True))
    kb.op("gpsimd", lambda E: E.iota(t["it_col"][:, :], pattern=[[0, 128]], base=0,
                                     channel_multiplier=1,
                                     allow_small_or_imprecise_dtypes=True), inc="gp")
    for name, v in [("ones_col_bf", 1.0), ("ones_row_f32", 1.0), ("ones_row_bf", 1.0),
                    ("eps_col", 1e-6), ("epsn_col", 1e-5), ("one_col", 1.0)]:
        kb.op("vector", lambda E, n=name, v=v: E.memset(t[n][:, :], v))
    kb.wait("vector", "gp", kb.val("gp"))
    kb.op("vector", lambda E: E.tensor_tensor(t["ident_bf"][:, :], t["it_col"][:, :],
                                              t["it_row"][:, :], ALU.is_equal))
    kb.op("vector", lambda E: E.tensor_tensor(t["m_low_strict01"][:, :], t["it_col"][:, :],
                                              t["it_row"][:, :], ALU.is_gt))
    kb.op("vector", lambda E: E.tensor_tensor(t["m_low_incl01"][:, :], t["it_col"][:, :],
                                              t["it_row"][:, :], ALU.is_ge))
    kb.op("vector", lambda E: E.tensor_tensor(t["neg_low_incl"][:, :], t["it_col"][:, :],
                                              t["it_row"][:, :], ALU.is_lt))
    kb.op("vector", lambda E: E.tensor_scalar(t["neg_low_incl"][:, :], t["neg_low_incl"][:, :],
                                              -1e9, None, ALU.mult))
    kb.op("vector", lambda E: E.memset(t["ones16f"][:, :], 1.0))
    kb.op("vector", lambda E: E.tensor_scalar(t["m_low_strictneg"][:, :], t["m_low_strict01"][:, :],
                                              -1.0, None, ALU.mult))
    kb.op("vector", lambda E: E.tensor_tensor(t["idf16"][:, :], t["it_col"][0:16, 0:16],
                                              t["it_row"][0:16, 0:16], ALU.is_equal))
    kb.op("gpsimd", lambda E: E.iota(t["eq16"][0:8, 0:64], pattern=[[1, 8], [0, 8]],
                                     base=0, channel_multiplier=0,
                                     allow_small_or_imprecise_dtypes=True), inc="gp")
    kb.op("gpsimd", lambda E: E.iota(t["ser"][0:8, 0:64], pattern=[[0, 64]],
                                     base=0, channel_multiplier=1,
                                     allow_small_or_imprecise_dtypes=True), inc="gp")
    kb.wait("vector", "gp", kb.val("gp"))
    r8v = kb.op("vector", lambda E: E.tensor_tensor(t["rep8c"][0:8, :], t["eq16"][0:8, 0:64],
                                                    t["ser"][0:8, 0:64], ALU.is_equal), inc="ve")
    kb.wait("gpsimd", "ve", r8v)
    kb.op("gpsimd", lambda E: E.iota(t["eq16"][0:16, 0:2048], pattern=[[1, 16], [0, 128]],
                                     base=0, channel_multiplier=0,
                                     allow_small_or_imprecise_dtypes=True), inc="gp")
    kb.op("gpsimd", lambda E: E.iota(t["ser"][0:16, 0:2048], pattern=[[0, 2048]],
                                     base=0, channel_multiplier=1,
                                     allow_small_or_imprecise_dtypes=True), inc="gp")
    kb.wait("vector", "gp", kb.val("gp"))
    vv = kb.op("vector", lambda E: E.tensor_tensor(t["sel16"][0:16, :], t["eq16"][0:16, :],
                                                   t["ser"][0:16, 0:2048], ALU.is_equal), inc="ve")
    for j in range(3):
        kb.op("sync", lambda E, j=j: E.dma_start(
            out=t["sel48"][16 * j:16 * (j + 1), :], in_=t["sel16"][0:16, :]),
            wait=[("ve", vv)] if j == 0 else None, inc="ld", incn=16)
    kb.wait("vector", "ld", kb.val("ld"))
    kb.wait("tensor", "ld", kb.val("ld"))
    kb.wait("sync", "ld", kb.val("ld"))
    ldv = kb.op("sync", lambda E: E.dma_start(out=t["smalls"][0:16, 0:1], in_=t["dtb_c"]),
                inc="ld", incn=16)
    ldv = kb.op("sync", lambda E: E.dma_start(out=t["smalls"][0:16, 1:2], in_=t["alog_c"]),
                inc="ld", incn=16)
    ldv = kb.op("sync", lambda E: E.dma_start(out=t["smalls"][0:128, 2:3], in_=t["nw0"]),
                inc="ld", incn=16)
    av = kb.op("scalar", lambda E: E.activation(t["smalls"][0:16, 3:4], t["smalls"][0:16, 1:2],
                                                AF.Exp), wait=[("ld", ldv)], inc="ac_c")
    kb.op("vector", lambda E: E.tensor_scalar(t["smalls"][0:16, 3:4], t["smalls"][0:16, 3:4],
                                              -1.0, None, ALU.mult),
          wait=[("ac_c", av)])


def emit_l0(kb):
    emit_l0_norm(kb)
    kb.dbg("xn", lambda: kb.tensors["xn_sp"])
    kb.barrier()
    for sl in range(NSLAB):
        emit_l0_projslab(kb, sl)
        kb.barrier()
        emit_l0_scanslab(kb, sl)
        kb.barrier()
    emit_l0_ffn(kb)
    kb.barrier()
    emit_l0_combine(kb)
    kb.barrier()


def emit_l0_norm(kb):
    t = kb.tensors
    n_ldx = "l0n_ld"; n_pe = "l0n_pe"; n_ve = "l0n_ve"; n_st = "l0n_st"
    n_ac = "l0n_ac"; n_cast = "l0n_cast"; n_ap = "l0n_ap"
    n_mm2 = "l0n_mm2"
    for tile in range(NSLAB):
        cur = t["xnA"] if tile % 2 == 0 else t["xnB"]
        if tile >= 2:
            kb.wait("vector", n_st + str(tile % 2), 16 * (tile // 2))
        mm2_base = kb.val(n_mm2) if kb.semv.get(n_mm2) is not None else 0
        for c in range(16):
            g = tile * 16 + c
            if g >= 4:
                kb.wait("sync", n_cast, g - 3)
            lv = kb.op("sync", lambda E, c=c, tile=tile:
                       E.dma_start(out=t["fscr"][:, ts(c % 4, 512)],
                                   in_=t["xT"][ts(c, 128), ts(tile, 512)]),
                       inc=n_ldx + str(c % 4), incn=16)
            cv = kb.op("vector", lambda E, c=c, cur=cur:
                       E.tensor_copy(cur[:, ts(c, 512)], t["fscr"][:, ts(c % 4, 512)]),
                       wait=[(n_ldx + str(c % 4), lv)], inc=n_cast)
            sqw = [(n_cast, cv)]
            if c >= 4:
                sqw.append((n_mm2, mm2_base + c - 3))
            sq = kb.op("scalar", lambda E, c=c, cur=cur:
                       E.activation(t["cscr"][:, ts(c % 4, 512)], cur[:, ts(c, 512)],
                                    AF.Square), wait=sqw, inc=n_ac)
            kb.op("tensor", lambda E, c=c, cur=cur:
                  E.matmul(t["pb0"][0:1, :], t["ones_col_bf"][:, :], cur[:, ts(c, 512)],
                           start=(c == 0), stop=(c == 15)), wait=[(n_cast, cv)],
                  inc=(n_pe if c == 15 else None))
            pv = kb.op("tensor", lambda E, c=c:
                       E.matmul(t["pb1"][0:1, :], t["ones_col_bf"][:, :],
                                t["cscr"][:, ts(c % 4, 512)],
                                start=(c == 0), stop=(c == 15)),
                       wait=[(n_ac, sq)], inc=n_mm2)
        kb.op("tensor", lambda E: E.nop(), inc=n_pe, drain=False,
              wait=[(n_mm2, kb.val(n_mm2))])
        def R(i):
            return t["rows"][0:1, 512 * i:512 * (i + 1)]
        kb.wait("vector", n_pe, kb.val(n_pe))
        kb.op("vector", lambda E: E.tensor_scalar(R(0), t["pb0"][0:1, :],
                                                  1.0 / D, None, ALU.mult))
        kb.op("vector", lambda E: E.tensor_scalar(R(1), t["pb1"][0:1, :],
                                                  1.0 / D, None, ALU.mult))
        kb.op("vector", lambda E: E.tensor_tensor(R(2), R(0), R(0), ALU.mult))
        vv = kb.op("vector", lambda E: E.tensor_tensor(R(2), R(1), R(2),
                                                       ALU.subtract), inc=n_ve)
        sv = kb.op("scalar", lambda E: E.activation(
            R(3), R(2), AF.Sqrt, bias=t["epsn_col"][0:1, :]),
            wait=[(n_ve, vv)], inc=n_ac)
        vv = kb.op("vector", lambda E: E.reciprocal(R(3), R(3)), wait=[(n_ac, sv)])
        vv = kb.op("vector", lambda E: E.tensor_tensor(R(4), R(0), R(3),
                                                       ALU.mult), inc=n_ve)
        pv = kb.op("tensor", lambda E: E.matmul(t["pb2"][:, :], t["ones_row_f32"][0:1, :],
                                                R(3), start=True, stop=True),
                   wait=[(n_ve, vv)])
        pv = kb.op("tensor", lambda E: E.matmul(t["pb3"][:, :], t["ones_row_f32"][0:1, :],
                                                R(4), start=True, stop=True),
                   inc=n_pe)
        kb.wait("vector", n_pe, kb.val(n_pe))
        for c in range(16):
            kb.op("vector", lambda E, c=c, cur=cur: E.tensor_tensor(
                cur[:, ts(c, 512)], cur[:, ts(c, 512)], t["pb2"][:, :], ALU.mult))
            kb.op("vector", lambda E, c=c, cur=cur: E.tensor_tensor(
                cur[:, ts(c, 512)], cur[:, ts(c, 512)], t["pb3"][:, :], ALU.subtract),
                inc=n_ap)
        sv = kb.op("sync", lambda E, tile=tile, cur=cur: E.dma_start(
            out=t["xn_sp"].rearrange("(c p) s -> p c s", p=128)[:, :, ts(tile, 512)],
            in_=cur.rearrange("p (c f) -> p c f", f=512)),
            wait=[(n_ap, kb.val(n_ap))], inc=n_st + str(tile % 2), incn=16)
        kb.wait("tensor", n_ap, kb.val(n_ap))
    for p in range(2):
        if kb.semv.get(n_st + str(p)):
            kb.wait("sync", n_st + str(p), kb.val(n_st + str(p)))


LKA = 0
LQA = 4096
LVA = 8192
LSZ = 16384
LXN = 24576
LWS = 32768
LKT = 36864
LQT = 38912
LVT = 40960
LSC = 45056
LACT = 49152
LSBF = 57344
LGS = 59392
LRAW = 61440
LCSC = 62464


def scs(i):
    return slice(LSC + i * 128, LSC + (i + 1) * 128)


def emit_l0_projslab(kb, sl):
    t = kb.tensors
    AB = t["AB"]
    xn = AB[:, LXN:LXN + 8192]
    n_wl = "pj_wl"
    n_pe = "pj_pe"
    n_ve = "pj_ve"
    n_ac = "pj_ac"
    n_xl = "pj_xl"

    if sl == 0:
        kb.op("sync", lambda E: E.dma_start(
            out=t["cw"][:, 0:128], in_=t["convw_c"]), inc="pj_cw", incn=16)
        kb.op("gpsimd", lambda E: E.dma_start(
            out=t["wba_s"][:, :], in_=t["wba_c"]), inc="pj_cwg", incn=16)
        kb.op("vector", lambda E: E.memset(t["cstash"][:, :], 0.0), inc=n_ve)
        kb.op("vector", lambda E: E.memset(t["c16"][:, :], 0.0), inc=n_ve)
        kb.op("vector", lambda E: E.memset(t["z16"][:, :], 0.0), inc=n_ve)
        kb.wait("tensor", "pj_cw", 16)
        kb.wait("tensor", "pj_cwg", 16)
        kb.wait("vector", "pj_cw", 16)
        kb.wait("vector", "pj_cwg", 16)

    xv = kb.op("sync", lambda E, sl=sl: E.dma_start(
        out=AB.rearrange("p (c f) -> p c f", f=512)[:, LXN // 512:LXN // 512 + 16, :],
        in_=t["xn_sp"].rearrange("(c p) s -> p c s", p=128)[:, :, ts(sl, 512)]),
        inc=n_xl, incn=16)
    kb.wait("tensor", n_xl, kb.val(n_xl))

    evac_q = []
    for cc in range(48):
        slot = cc % 4
        wsrc = t["wqkv_c"] if cc < 32 else t["wz_c"]
        coff = cc * 128 if cc < 32 else (cc - 32) * 128
        if cc >= 4:
            kb.wait("gpsimd", n_pe, kb.val(n_pe) - 3)
        wl = kb.op("gpsimd", lambda E, wsrc=wsrc, coff=coff, slot=slot: E.dma_start(
            out=AB.rearrange("p (s f) -> p s f", f=2048)[:, (LWS // 2048) + slot, :].rearrange("p (c f) -> p c f", f=128),
            in_=wsrc.rearrange("(c p) f -> p c f", p=128)[:, :, coff:coff + 128]),
            inc=n_wl + str(slot), incn=16, drain=False)
        bank = t["pb0"] if cc % 2 == 0 else t["pb1"]
        kb.wait("tensor", n_wl + str(slot), wl)
        if cc >= 2:
            kb.wait("tensor", n_ve, kb.val(n_ve))
        for dc in range(16):
            kb.op("tensor", lambda E, dc=dc, slot=slot, bank=bank: E.matmul(
                bank[:, :], AB[:, LWS + slot * 2048 + dc * 128:LWS + slot * 2048 + dc * 128 + 128],
                xn[:, ts(dc, 512)], start=(dc == 0), stop=(dc == 15)),
                inc=(n_pe if dc == 15 else None))
        mmv = kb.val(n_pe)
        if cc < 32:
            dst = (LQA + cc * 512 if cc < 8 else
                   LKA + (cc - 8) * 512 if cc < 16 else
                   LVA + (cc - 16) * 512)
            emit_conv_silu(kb, cc, bank, AB[:, dst:dst + 512], mmv, n_pe, n_ve, n_ac)
        elif cc < 48:
            zc = cc - 32
            kb.op("scalar", lambda E, zc=zc, bank=bank: E.activation(
                AB[:, LSZ + zc * 512:LSZ + zc * 512 + 512], bank[:, :], AF.Silu),
                wait=[(n_pe, mmv)], inc=n_ve)

    kb.wait("tensor", n_ve, kb.val(n_ve))
    for dc in range(16):
        kb.op("tensor", lambda E, dc=dc: E.matmul(
            t["pb2"][0:16, :], t["wba_s"][:, dc * 32:dc * 32 + 16], xn[:, ts(dc, 512)],
            start=(dc == 0), stop=(dc == 15)))
        kb.op("tensor", lambda E, dc=dc: E.matmul(
            t["pb3"][0:16, :], t["wba_s"][:, dc * 32 + 16:dc * 32 + 32], xn[:, ts(dc, 512)],
            start=(dc == 0), stop=(dc == 15)), inc=(n_pe if dc == 15 else None))
    bav = kb.val(n_pe)
    bw = t["baraw"][:, 0:512]
    aw = t["baraw"][:, 512:1024]
    kb.op("vector", lambda E: E.tensor_copy(t["baraw"][0:16, 0:512], t["pb2"][0:16, :]),
          wait=[(n_pe, bav)])
    kb.op("vector", lambda E: E.tensor_copy(t["baraw"][0:16, 512:1024], t["pb3"][0:16, :]),
          inc=n_ve)
    emit_series(kb, sl, n_ve, n_ac, kb.val(n_ve))


def emit_conv_silu(kb, cc, bank, dst, mmv, n_pe, n_ve, n_ac):
    t = kb.tensors
    AB = t["AB"]
    raw = AB[:, LRAW:LRAW + 520]
    c0 = AB[:, LCSC:LCSC + 512]
    c1 = AB[:, LCSC + 512:LCSC + 1024]
    st = t["cstash"][:, cc * 4:cc * 4 + 3]
    cwc = lambda j: t["cw"][:, cc * 4 + j:cc * 4 + j + 1]
    kb.op("vector", lambda E: E.tensor_copy(raw[:, 0:3], st),
          wait=[(n_pe, mmv), (n_ve, kb.val(n_ve))])
    kb.op("vector", lambda E, bank=bank: E.tensor_copy(raw[:, 3:515], bank[:, :]))
    kb.op("vector", lambda E: E.tensor_copy(st, raw[:, 512:515]))
    kb.op("vector", lambda E: E.tensor_scalar(c0[:, :], raw[:, 0:512], cwc(0), None, ALU.mult))
    kb.op("vector", lambda E: E.scalar_tensor_tensor(c1[:, :], raw[:, 1:513], cwc(1), c0, ALU.mult, ALU.add))
    kb.op("vector", lambda E: E.scalar_tensor_tensor(c0[:, :], raw[:, 2:514], cwc(2), c1, ALU.mult, ALU.add))
    vv = kb.op("vector", lambda E: E.scalar_tensor_tensor(c1[:, :], raw[:, 3:515], cwc(3), c0, ALU.mult, ALU.add), inc=n_ve)
    kb.op("scalar", lambda E: E.activation(dst, c1, AF.Silu), wait=[(n_ve, vv)], inc=n_ve)


def emit_series(kb, sl, n_ve, n_ac, bav):
    t = kb.tensors
    ser = t["ser"]
    co = sl * 512
    s1 = ser[64:80, co:co + 512]
    s2 = ser[96:112, co:co + 512]
    bs0 = t["bscr"][:, 0:512]
    bs1 = t["bscr"][:, 512:1024]
    dtb = t["smalls"][0:16, 0:1]
    nea = t["smalls"][0:16, 3:4]
    aw = t["baraw"][0:16, 512:1024]
    bw = t["baraw"][0:16, 0:512]
    kb.op("scalar", lambda E: E.activation(ser[32:48, co:co + 512], bw, AF.Sigmoid),
          wait=[(n_ve, bav)], inc=n_ac)
    A, Bs, C = bs0, bs1, t["cser"][0:16, 0:512]
    kb.op("vector", lambda E: E.tensor_scalar(A, aw, dtb, None, ALU.add), inc=n_ve)
    kb.op("scalar", lambda E: E.activation(Bs, A, AF.Abs), wait=[(n_ve, kb.val(n_ve))])
    kb.op("scalar", lambda E: E.activation(C, Bs, AF.Exp, scale=-1.0))
    kb.op("scalar", lambda E: E.activation(Bs, C, AF.Ln, bias=t["one_col"][0:16, :]), inc=n_ac)
    kb.op("vector", lambda E: E.tensor_scalar(C, A, 0.0, None, ALU.max),
          wait=[(n_ac, kb.val(n_ac))])
    kb.op("vector", lambda E: E.tensor_tensor(A, Bs, C, ALU.add))
    kb.op("vector", lambda E: E.tensor_scalar(Bs, A, nea, None, ALU.mult), inc=n_ve)
    kb.op("vector", lambda E: E.tensor_tensor_scan(
        ser[0:16, co:co + 512], Bs, t["z16"][0:16, :], t["c16"][0:16, 3:4], ALU.add, ALU.add), inc=n_ve)
    kb.op("vector", lambda E, co=co: E.tensor_copy(t["c16"][0:16, 3:4], ser[0:16, co + 511:co + 512]), inc=n_ve)
    kb.op("vector", lambda E, co=co: E.tensor_scalar(A, ser[0:16, co:co + 512], -1.0, None, ALU.mult))
    kb.op("vector", lambda E: E.tensor_copy(t["bscrb"][0:16, 0:512], A))
    kb.op("vector", lambda E: E.tensor_tensor(C, A, t["bscrb"][0:16, 0:512], ALU.subtract))
    kb.op("vector", lambda E: E.tensor_copy(t["bscrb"][0:16, 512:1024], C))
    kb.op("vector", lambda E: E.tensor_tensor(A, C, t["bscrb"][0:16, 512:1024], ALU.subtract))
    kb.op("vector", lambda E: E.tensor_copy(t["bscrb"][0:16, 1024:1536], A), inc=n_ve)
    sv = kb.val(n_ve)
    for j in range(3):
        kb.op("sync", lambda E, j=j, co=co: E.dma_start(
            out=t["ncg3"][j * 16:(j + 1) * 16, co:co + 512],
            in_=t["bscrb"][0:16, j * 512:(j + 1) * 512]),
            wait=[(n_ve, sv)] if j == 0 else None, inc="pj_ncg", incn=16)
    kb.wait("vector", "pj_ncg", kb.val("pj_ncg"))


def emit_l0_scanslab(kb, sl):
    t = kb.tensors
    AB = t["AB"]
    if sl == 0:
        kb.op("vector", lambda E: E.memset(t["smastA"][:, :], 0.0))
        kb.op("vector", lambda E: E.memset(AB[:, LSBF:LSBF + 2048], 0.0), inc="sc_ve")
    for cis in range(CPS):
        gc = sl * CPS + cis
        emit_chunk_prep(kb, sl, cis, gc)
        for jp in range(VH_C // 2):
            emit_chunk_pair(kb, sl, cis, gc, jp)
        emit_chunk_gating(kb, sl, cis, gc)
    emit_outproj_slab(kb, sl)


def emit_chunk_prep(kb, sl, cis, gc):
    t = kb.tensors
    AB = t["AB"]
    ser = t["ser"]
    co = gc * 128
    n_tr = "sc_tr"; n_ve = "sc_ve"; n_ac = "sc_ac"; n_pe = "sc_pe"; n_trb = "sc_trb"
    par = gc % 2
    kb.wait("scalar", "sc_ve", kb.val("sc_ve"))
    for vh in range(VH_C):
        kb.op("scalar", lambda E, vh=vh, cis=cis, par=par: E.dma_start_transpose(
            out=AB[:, LVT + par * 2048 + vh * 128:LVT + par * 2048 + vh * 128 + 128],
            in_=AB[:, LVA + vh * 512 + cis * 128:LVA + vh * 512 + cis * 128 + 128]),
            inc=n_tr, incn=16, drain=False)
    for kh in range(KH_C):
        kb.op("scalar", lambda E, kh=kh, cis=cis: E.dma_start_transpose(
            out=AB[:, LKT + kh * 128:LKT + kh * 128 + 128],
            in_=AB[:, LKA + kh * 512 + cis * 128:LKA + kh * 512 + cis * 128 + 128]),
            inc=n_tr, incn=16, drain=False)
        kb.op("scalar", lambda E, kh=kh, cis=cis: E.dma_start_transpose(
            out=AB[:, LQT + kh * 128:LQT + kh * 128 + 128],
            in_=AB[:, LQA + kh * 512 + cis * 128:LQA + kh * 512 + cis * 128 + 128]),
            inc=n_tr, incn=16, drain=False)
    trv = kb.val(n_tr)
    kb.wait("vector", n_tr, trv)
    for (base, abase, ncol) in ((LKT, LKA, 0), (LQT, LQA, 8)):
        for kh in range(KH_C):
            rawS = AB[:, base + kh * 128:base + kh * 128 + 128]
            nrmS = AB[:, base + 1024 + kh * 128:base + 1024 + kh * 128 + 128]
            sq = AB[:, LGS:LGS + 128]
            rk = t["nrmc"][:, ncol + kh:ncol + kh + 1]
            kb.op("vector", lambda E, rawS=rawS, sq=sq: E.tensor_tensor(sq, rawS, rawS, ALU.mult))
            kb.op("vector", lambda E, sq=sq, rk=rk: E.reduce_sum(rk, sq, axis=mybir.AxisListType.X))
            vv = kb.op("vector", lambda E, rk=rk: E.tensor_scalar(rk, rk, 1e-6, None, ALU.add), inc=n_ve)
            av = kb.op("scalar", lambda E, rk=rk: E.activation(rk, rk, AF.Sqrt),
                       wait=[(n_ve, vv)], inc=n_ac)
            kb.op("vector", lambda E, rk=rk: E.reciprocal(rk, rk), wait=[(n_ac, av)])
            vv = kb.op("vector", lambda E, nrmS=nrmS, rawS=rawS, rk=rk: E.tensor_scalar(
                nrmS, rawS, rk, None, ALU.mult), inc=n_ve)
            kb.op("scalar", lambda E, nrmS=nrmS, abase=abase, kh=kh, cis=cis: E.dma_start_transpose(
                out=AB[:, abase + kh * 512 + cis * 128:abase + kh * 512 + cis * 128 + 128],
                in_=nrmS), wait=[(n_ve, vv)], inc=n_trb, incn=16, drain=False)
    kb.wait("tensor", n_trb, kb.val(n_trb))
    kb.wait("tensor", n_tr, trv)

    cser = t["cser"]
    c16 = t["c16"]
    if gc == 0:
        kb.op("vector", lambda E: E.memset(c16[0:16, 0:1], 0.0))
    else:
        kb.op("vector", lambda E, co=co: E.tensor_scalar(
            c16[0:16, 0:1], ser[0:16, co - 1:co], -1.0, None, ALU.mult))
    kb.op("vector", lambda E, co=co: E.tensor_copy(c16[0:16, 1:2], ser[0:16, co + 127:co + 128]), inc=n_ve)
    av = kb.op("scalar", lambda E, co=co: E.activation(
        cser[0:16, 0:128], ser[0:16, co:co + 128], AF.Exp, bias=c16[0:16, 0:1]),
        wait=[(n_ve, kb.val(n_ve))], inc=n_ac)
    kb.op("vector", lambda E, co=co: E.tensor_scalar(
        t["cqb"][0:16, co:co + 128], cser[0:16, 0:128], DK ** -0.5, None, ALU.mult),
        wait=[(n_ac, av)])
    kb.op("vector", lambda E, co=co: E.tensor_copy(cser[0:16, 384:512], ser[32:48, co:co + 128]))
    kb.op("vector", lambda E: E.tensor_tensor(
        cser[0:16, 128:256], cser[0:16, 384:512], cser[0:16, 0:128], ALU.mult), inc=n_ve)
    av = kb.op("scalar", lambda E, co=co: E.activation(
        cser[0:16, 256:384], ser[0:16, co:co + 128], AF.Exp, scale=-1.0, bias=c16[0:16, 1:2]), inc=n_ac)
    av = kb.op("scalar", lambda E: E.activation(
        c16[0:16, 2:3], c16[0:16, 1:2], AF.Exp, bias=c16[0:16, 0:1]), inc=n_ac)
    kb.op("vector", lambda E: E.tensor_scalar(c16[0:16, 8:24], t["idf16"][:, :], c16[0:16, 2:3], None, ALU.mult),
          wait=[(n_ac, kb.val(n_ac))], inc=n_ve)
    kb.wait("tensor", n_ve, kb.val(n_ve))
    kb.wait("tensor", n_ac, kb.val(n_ac))
    kb.wait("tensor", n_ve, kb.val(n_ve))
    kb.op("tensor", lambda E, co=co: E.matmul(t["pb2"][:, 0:16], ser[0:16, co:co + 128], t["idf16"][:, :], start=True, stop=True))
    kb.op("tensor", lambda E: E.matmul(t["pb2"][:, 16:32], cser[0:16, 384:512], t["idf16"][:, :], start=True, stop=True))
    kb.op("tensor", lambda E: E.matmul(t["pb2"][:, 32:48], cser[0:16, 128:256], t["idf16"][:, :], start=True, stop=True))
    kb.op("tensor", lambda E: E.matmul(t["pb2"][:, 48:64], cser[0:16, 256:384], t["idf16"][:, :], start=True, stop=True))
    pv = kb.op("tensor", lambda E: E.matmul(t["pb2"][:, 64:80], t["ones16f"][0:16, :], c16[0:16, 8:24], start=True, stop=True), inc=n_pe)
    vv = kb.op("vector", lambda E: E.tensor_copy(t["colp"][:, 0:80], t["pb2"][:, 0:80]),
               wait=[(n_pe, pv)], inc=n_ve)
    kb.wait("tensor", n_ve, vv)
    kb.wait("scalar", n_ve, vv)


def emit_chunk_pair(kb, sl, cis, gc, j):
    t = kb.tensors
    AB = t["AB"]
    kh = j
    par = gc % 2
    co = gc * 128
    n_pe = "sc_pe"; n_ve = "sc_ve"; n_ac = "sc_ac"
    colp = t["colp"]
    pb0, pb1, pb2, pb3, pb4, pb5 = (t[f"pb{i}"] for i in range(6))
    G = pb0[:, 0:128]; M = pb1[:, 0:128]
    MQ = AB[:, scs(2)]
    Kf = AB[:, LKA + kh * 512 + cis * 128:LKA + kh * 512 + cis * 128 + 128]
    Qf = AB[:, LQA + kh * 512 + cis * 128:LQA + kh * 512 + cis * 128 + 128]
    Ktn = AB[:, LKT + 1024 + kh * 128:LKT + 1024 + kh * 128 + 128]
    Sm0 = t["smastA"] if gc % 2 == 0 else t["smastB"]
    Sm1 = t["smastB"] if gc % 2 == 0 else t["smastA"]

    lanes = []
    for L in range(2):
        h = 2 * j + L
        slot = (lambda i: AB[:, scs(i)]) if L == 0 else \
               (lambda i: AB[:, LXN + i * 128:LXN + (i + 1) * 128])
        R = dict(h=h,
                 cgcol=colp[:, 0 + h:1 + h], bcol=colp[:, 16 + h:17 + h],
                 bgcol=colp[:, 32 + h:33 + h], ksecol=colp[:, 48 + h:49 + h],
                 gccol=colp[:, 64 + h:65 + h],
                 Vt=AB[:, LVT + par * 2048 + h * 128:LVT + par * 2048 + h * 128 + 128],
                 S0=Sm0[:, h * 128:(h + 1) * 128], S1=Sm1[:, h * 128:(h + 1) * 128],
                 S0b=AB[:, LSBF + h * 128:LSBF + h * 128 + 128],
                 OSL=AB[:, LGS + h * 128:LGS + (h + 1) * 128],
                 EI=slot(0), ES=slot(1), MT=slot(3),
                 NL={lvl: slot(4 + i) for i, lvl in enumerate([1, 2, 4, 8, 16, 32])},
                 NU={lvl: slot(10 + i) for i, lvl in enumerate([1, 2, 4, 8, 16, 32, 64])},
                 Rs=slot(17), X0=slot(18), X1=slot(19),
                 T1=slot(20), T2=slot(21), CQR=slot(22), OT=slot(23),
                 fs=t["fscr"][:, L * 128:(L + 1) * 128])
        if L == 0:
            R.update(REP=pb2[:, 128:256], W=pb3[:, 0:128], SUP=pb3[:, 128:256],
                     SQA=pb4[:, 0:128], SQB=pb5[:, 0:128],
                     O1=pb4[:, 128:256], O2=pb5[:, 128:256],
                     TRE=t["pbtA"][:, 0:128], TRN=t["pbtB"][:, 0:128],
                     MB=pb0[0:1, 384:512])
        else:
            R.update(REP=pb2[:, 256:384], W=pb3[:, 256:384], SUP=pb3[:, 384:512],
                     SQA=pb4[:, 256:384], SQB=pb5[:, 256:384],
                     O1=pb4[:, 384:512], O2=pb5[:, 384:512],
                     TRE=t["pbtA"][:, 128:256], TRN=t["pbtB"][:, 128:256],
                     MB=pb1[0:1, 384:512])
        lanes.append(R)

    kb.op("tensor", lambda E: E.matmul(G, Kf, Kf, start=True, stop=True))
    pv = kb.op("tensor", lambda E: E.matmul(M, Kf, Qf, start=True, stop=True), inc=n_pe)
    vv = kb.op("vector", lambda E: E.tensor_scalar(MQ, M, DK ** -0.5, None, ALU.mult),
               wait=[(n_pe, pv)], inc=n_ve)
    kb.wait("tensor", n_ve, vv)

    def s_rep(R):
        pv = kb.op("tensor", lambda E, R=R: E.matmul(
            R["REP"], t["sel48"][:, ts(R["h"], 128)], t["ncg3"][:, co:co + 128],
            start=True, stop=True), inc=n_pe)
        vv = kb.op("vector", lambda E, R=R: E.tensor_tensor(
            R["fs"], R["REP"], t["neg_low_incl"][:, :], ALU.add),
            wait=[(n_pe, pv)], inc=n_ve)
        R["av1"] = kb.op("scalar", lambda E, R=R: E.activation(
            R["EI"], R["fs"], AF.Exp, bias=R["cgcol"]),
            wait=[(n_ve, vv)], inc=n_ac)

    def s_nl(R):
        kb.op("vector", lambda E, R=R: E.tensor_tensor(
            R["ES"], R["EI"], t["m_low_strictneg"][:, :], ALU.mult),
            wait=[(n_ac, R["av1"])])
        vv = kb.op("vector", lambda E, R=R: E.scalar_tensor_tensor(
            R["NL"][1], G, R["bcol"], R["ES"], ALU.mult, ALU.mult), inc=n_ve)
        kb.wait("tensor", n_ve, vv)

    def s_tr(R):
        kb.op("tensor", lambda E, R=R: E.transpose(R["TRE"], R["EI"], t["ident_bf"][:, :]))
        pv = kb.op("tensor", lambda E, R=R: E.transpose(R["TRN"], R["NL"][1], t["ident_bf"][:, :]), inc=n_pe)
        kb.op("vector", lambda E, R=R: E.tensor_tensor(R["MT"], R["TRE"], MQ, ALU.mult),
              wait=[(n_pe, pv)])
        vv = kb.op("vector", lambda E, R=R: E.tensor_copy(R["NU"][1], R["TRN"]), inc=n_ve)
        kb.wait("tensor", n_ve, vv)

    def s_sq(R, prev, lvl):
        if lvl < 64:
            kb.op("tensor", lambda E, R=R, p=prev: E.matmul(R["SQA"], R["NU"][p], R["NL"][p], start=True, stop=True))
            pv = kb.op("tensor", lambda E, R=R, p=prev: E.matmul(R["SQB"], R["NL"][p], R["NU"][p], start=True, stop=True), inc=n_pe)
            kb.op("vector", lambda E, R=R, lvl=lvl: E.tensor_copy(R["NL"][lvl], R["SQA"]),
                  wait=[(n_pe, pv)], inc=n_ve)
            av = kb.op("scalar", lambda E, R=R, lvl=lvl: E.activation(R["NU"][lvl], R["SQB"], AF.Copy),
                       wait=[(n_pe, pv)], inc=n_ac)
            kb.wait("tensor", n_ac, av)
            kb.wait("tensor", n_ve, kb.val(n_ve))
        else:
            pv = kb.op("tensor", lambda E, R=R, p=prev: E.matmul(R["SQA"], R["NL"][p], R["NU"][p], start=True, stop=True), inc=n_pe)
            vv = kb.op("vector", lambda E, R=R: E.tensor_copy(R["NU"][64], R["SQA"]), wait=[(n_pe, pv)], inc=n_ve)
            kb.wait("tensor", n_ve, vv)

    def s_r(R):
        pv = kb.op("tensor", lambda E, R=R: E.matmul(R["W"], Kf, R["S0b"], start=True, stop=True), inc=n_pe)
        kb.op("vector", lambda E, R=R: E.tensor_scalar(R["T1"], R["W"], R["bgcol"], None, ALU.mult),
              wait=[(n_pe, pv)])
        vv = kb.op("vector", lambda E, R=R: E.scalar_tensor_tensor(
            R["Rs"], R["Vt"], R["bcol"], R["T1"], ALU.mult, ALU.subtract), inc=n_ve)
        kb.wait("tensor", n_ve, vv)
        R["X"] = R["Rs"]

    def s_app(R, i, lvl):
        X = R["X"]
        pv = kb.op("tensor", lambda E, R=R, lvl=lvl, X=X: E.matmul(R["SUP"], R["NU"][lvl], X, start=True, stop=True), inc=n_pe)
        nX = (R["X0"], R["X1"])[i % 2]
        vv = kb.op("vector", lambda E, R=R, nX=nX, X=X: E.tensor_tensor(nX, R["SUP"], X, ALU.add),
                   wait=[(n_pe, pv)], inc=n_ve)
        kb.wait("tensor", n_ve, vv)
        R["X"] = nX

    def s_o(R):
        kb.op("tensor", lambda E, R=R, U=R["X"]: E.matmul(R["O1"], U, R["MT"], start=True, stop=True))
        kb.op("tensor", lambda E, R=R: E.matmul(R["O2"], R["S0b"], Qf, start=True, stop=True))
        pv = kb.op("tensor", lambda E, R=R: E.matmul(
            R["REP"], t["sel16"][:, ts(R["h"], 128)], t["cqb"][:, co:co + 128],
            start=True, stop=True), inc=n_pe)
        kb.op("vector", lambda E, R=R: E.tensor_copy(R["CQR"], R["REP"]), wait=[(n_pe, pv)])
        kb.op("vector", lambda E, R=R: E.tensor_tensor(R["T2"], R["O2"], R["CQR"], ALU.mult))
        vv = kb.op("vector", lambda E, R=R: E.tensor_tensor(R["OT"], R["O1"], R["T2"], ALU.add), inc=n_ve)
        kb.wait("tensor", n_ve, vv)

    def s_st2(R):
        kb.op("vector", lambda E, R=R: E.tensor_copy(R["OSL"], R["OT"]), inc=n_ve)
        vv = kb.op("vector", lambda E, R=R: E.tensor_tensor(R["T1"], R["OT"], R["OT"], ALU.mult), inc=n_ve)
        pv = kb.op("tensor", lambda E, R=R: E.matmul(
            R["MB"], t["ones_col_bf"][:, :], R["T1"],
            start=True, stop=True), wait=[(n_ve, vv)], inc=n_pe)
        vv = kb.op("vector", lambda E, R=R: E.tensor_copy(
            t["rows"][0:1, R["h"] * 128:(R["h"] + 1) * 128], R["MB"]),
            wait=[(n_pe, pv)], inc=n_ve)
        kb.wait("tensor", n_ve, vv)

    def s_su(R):
        vv = kb.op("vector", lambda E, R=R: E.tensor_scalar(R["T1"], Ktn, R["ksecol"], None, ALU.mult), inc=n_ve)
        pv = kb.op("tensor", lambda E, R=R: E.matmul(R["SUP"], R["T1"], R["X"], start=True, stop=True),
                   wait=[(n_ve, vv)], inc=n_pe)
        kb.op("vector", lambda E, R=R: E.scalar_tensor_tensor(
            R["S1"], R["S0"], R["gccol"], R["SUP"], ALU.mult, ALU.add),
            wait=[(n_pe, pv)])
        vv = kb.op("vector", lambda E, R=R: E.tensor_copy(R["S0b"], R["S1"]), inc=n_ve)
        kb.wait("tensor", n_ve, vv)

    A, B = lanes
    s_rep(A); s_rep(B)
    s_nl(A); s_nl(B)
    s_tr(A); s_tr(B)
    prev = 1
    for lvl in [2, 4, 8, 16, 32, 64]:
        s_sq(A, prev, lvl); s_sq(B, prev, lvl)
        prev = lvl
    s_r(A); s_r(B)
    for i, lvl in enumerate([64, 32, 16, 8, 4, 2, 1]):
        s_app(A, i, lvl); s_app(B, i, lvl)
    s_o(A); s_o(B)
    s_st2(A); s_st2(B)
    s_su(A); s_su(B)


def emit_chunk_gating(kb, sl, cis, gc):
    t = kb.tensors
    AB = t["AB"]
    n_pe = "sc_pe"; n_ve = "sc_ve"; n_ac = "sc_ac"
    rrow = t["rows"][0:1, 0:2048]
    av = kb.op("scalar", lambda E: E.activation(
        t["rows_bf2"][0:1, 0:2048], rrow, AF.Sqrt, scale=1.0 / DV,
        bias=t["eps_col"][0:1, :]), wait=[(n_ve, kb.val(n_ve))], inc=n_ac)
    av = kb.val(n_ac)
    vv = kb.op("vector", lambda E: E.reciprocal(rrow, t["rows_bf2"][0:1, 0:2048]),
               wait=[(n_ac, av)])
    vv = kb.op("vector", lambda E: E.tensor_copy(t["rows_bf2"][0:1, 0:2048], rrow), inc=n_ve)
    kb.wait("tensor", n_ve, kb.val(n_ve))
    for h in range(VH_C):
        bank = t[f"pb{2 + h % 2}"]
        pv = kb.op("tensor", lambda E, h=h, bank=bank: E.matmul(
            bank[:, 128:256], t["ones_row_bf"][0:1, :],
            t["rows_bf2"][0:1, h * 128:(h + 1) * 128],
            start=True, stop=True), inc=n_pe)
        OSL = AB[:, LGS + h * 128:LGS + (h + 1) * 128]
        sz = AB[:, LSZ + h * 512 + cis * 128:LSZ + h * 512 + cis * 128 + 128]
        act = AB[:, LACT + h * 512 + cis * 128:LACT + h * 512 + cis * 128 + 128]
        T1 = AB[:, scs(20)]
        T2 = AB[:, scs(21)]
        vv = kb.op("vector", lambda E, OSL=OSL, T1=T1: E.tensor_scalar(
            T1, OSL, t["smalls"][0:128, 2:3], None, ALU.mult), wait=[(n_pe, pv)])
        vv = kb.op("vector", lambda E, T1=T1, T2=T2, sz=sz: E.tensor_tensor(T2, T1, sz, ALU.mult))
        vv = kb.op("vector", lambda E, act=act, T2=T2, bank=bank: E.tensor_tensor(
            act, T2, bank[:, 128:256], ALU.mult), inc=n_ve)
        kb.wait("tensor", n_ve, vv)


def emit_outproj_slab(kb, sl):
    t = kb.tensors
    AB = t["AB"]
    n_wl = "op_wl"; n_pe = "op_pe"; n_ve = "sc_ve"; n_st = "op_st"
    for dmc in range(16):
        slot = dmc % 2
        if dmc >= 2:
            kb.wait("gpsimd", n_pe, kb.val(n_pe) - 1)
        wl = kb.op("gpsimd", lambda E, dmc=dmc, slot=slot: E.dma_start(
            out=AB.rearrange("p (s f) -> p s f", f=2048)[:, LWS // 2048 + slot, :].rearrange("p (c f) -> p c f", f=128),
            in_=t["wout0_c"].rearrange("(c p) f -> p c f", p=128)[:, :, ts(dmc, 128)]),
            inc=n_wl + str(slot), incn=16, drain=False)
        kb.wait("tensor", n_wl + str(slot), wl)
        bank = t[f"pb{dmc % 2}"]
        if dmc >= 2:
            kb.wait("tensor", n_ve, kb.val(n_ve))
        for vc in range(VH_C):
            kb.op("tensor", lambda E, vc=vc, dmc=dmc, slot=slot, bank=bank: E.matmul(
                bank[:, 0:512],
                AB[:, LWS + slot * 2048 + vc * 128:LWS + slot * 2048 + vc * 128 + 128],
                AB[:, LACT + vc * 512:LACT + vc * 512 + 512],
                start=(vc == 0), stop=(vc == 15)),
                inc=(n_pe if vc == 15 else None))
        mmv = kb.val(n_pe)
        ev = AB[:, LVT + (dmc % 2) * 2048:LVT + (dmc % 2) * 2048 + 512]
        evw = [(n_pe, mmv)]
        if dmc >= 2:
            evw.append((n_st + str(dmc % 2), kb.val(n_st + str(dmc % 2))))
        vv = kb.op("vector", lambda E, bank=bank, ev=ev: E.tensor_copy(ev, bank[:, 0:512]),
                   wait=evw, inc=n_ve)
        kb.op("sync", lambda E, dmc=dmc, sl=sl, ev=ev: E.dma_start(
            out=t["cc0_in"][ts(dmc, 128), ts(sl, 512)], in_=ev),
            wait=[(n_ve, vv)], inc=n_st + str(dmc % 2), incn=16)
        if dmc >= 2:
            pass
    for p in range(2):
        kb.wait("vector", n_st + str(p), kb.val(n_st + str(p)))


NFC = 22
FXN = 0
FHID = 16384
FWS = 38912
FAGU = 47104
FBGF = 49408
FADF = 49920
FW72 = 50176
FPG = 51200
FPU = 52224
FPD = 53248
FEW = 54272
FDE = 55296
FBD = 56320
FHG = 58368
FRB = 59392


def emit_l0_ffn(kb):
    t = kb.tensors
    AB = t["AB"]
    n_pe = "ff_pe"; n_ve = "ff_ve"; n_ac = "ff_ac"; n_st = "ff_st"; n_gl = "ff_gl"
    kb.op("gpsimd", lambda E: E.dma_start(
        out=t["smalls"][:, 4:12].rearrange("p (a b) -> p a b", a=1),
        in_=t["eb"].rearrange("p (a b) -> p a b", a=1)), inc=n_gl, incn=16, drain=False)
    kb.op("gpsimd", lambda E: E.dma_start(
        out=AB[0:72, FBD:FBD + 2048], in_=t["bd"]), inc=n_gl, incn=16, drain=False)
    kb.op("gpsimd", lambda E: E.dma_start(
        out=AB[:, FAGU:FAGU + 1152].rearrange("p (c f) -> p c f", f=72),
        in_=t["ag"].rearrange("(c p) f -> p c f", p=128)), inc=n_gl, incn=16, drain=False)
    kb.op("gpsimd", lambda E: E.dma_start(
        out=AB[:, FAGU + 1152:FAGU + 2304].rearrange("p (c f) -> p c f", f=72),
        in_=t["au"].rearrange("(c p) f -> p c f", p=128)), inc=n_gl, incn=16, drain=False)
    kb.op("gpsimd", lambda E: E.dma_start(
        out=t["wba_s"][:, 0:128].rearrange("p (c f) -> p c f", f=8),
        in_=t["wr"].rearrange("(c p) f -> p c f", p=128)), inc=n_gl, incn=16, drain=False)
    for osl in range(2):
        emit_ffn0_slab(kb, osl)


def emit_ffn0_slab(kb, osl):
    t = kb.tensors
    AB = t["AB"]
    n_pe = "ff_pe"; n_ve = "ff_ve"; n_ac = "ff_ac"; n_st = "ff_st"
    n_wl = "ff_wl"; n_xl = "ff_xl"; n_gl = "ff_gl"
    xn = AB[:, FXN:FXN + 16384]
    TOK = 1024
    kb.wait("sync", n_pe, kb.val(n_pe))
    xv = kb.op("sync", lambda E, osl=osl: E.dma_start(
        out=AB.rearrange("p (c f) -> p c f", f=1024)[:, FXN // 1024:FXN // 1024 + 16, :],
        in_=t["xn_sp"].rearrange("(c p) s -> p c s", p=128)[:, :, ts(osl, 1024)]),
        inc=n_xl, incn=16)
    kb.wait("tensor", n_xl, kb.val(n_xl))
    kb.wait("vector", n_gl, kb.val(n_gl))
    kb.wait("tensor", n_gl, kb.val(n_gl))
    kb.wait("scalar", n_gl, kb.val(n_gl))

    for tt in range(8):
        pv = None
        for dc in range(16):
            pv = kb.op("tensor", lambda E, tt=tt, dc=dc: E.matmul(
                t["pb5"][:, 0:8], xn[:, dc * 1024 + tt * 128:dc * 1024 + tt * 128 + 128],
                t["wba_s"][:, dc * 8:dc * 8 + 8], start=(dc == 0), stop=(dc == 15)),
                inc=(n_pe if dc == 15 else None))
        R8 = t["fscr"][:, 0:64]
        sc8 = lambda i: R8[:, i * 8:(i + 1) * 8]
        c1 = t["nrmc"][:, 40:41]; c2 = t["nrmc"][:, 41:42]; c3 = t["nrmc"][:, 42:43]
        av = kb.op("scalar", lambda E: E.activation(sc8(0), t["pb5"][:, 0:8], AF.Sigmoid),
                   wait=[(n_pe, pv)], inc=n_ac)
        kb.wait("tensor", n_ac, av)
        kb.op("vector", lambda E: E.tensor_tensor(sc8(1), sc8(0), t["smalls"][:, 4:12], ALU.add),
              wait=[(n_ac, av)])
        kb.op("vector", lambda E: E.reduce_max(c1, sc8(1), axis=mybir.AxisListType.X))
        kb.op("vector", lambda E: E.tensor_scalar(sc8(2), sc8(1), c1, None, ALU.is_ge))
        kb.op("vector", lambda E: E.scalar_tensor_tensor(sc8(3), sc8(2), -1e9, sc8(1), ALU.mult, ALU.add))
        kb.op("vector", lambda E: E.reduce_max(c2, sc8(3), axis=mybir.AxisListType.X))
        kb.op("vector", lambda E: E.tensor_scalar(sc8(3), sc8(3), c2, None, ALU.is_ge))
        kb.op("vector", lambda E: E.tensor_tensor(sc8(2), sc8(2), sc8(3), ALU.add))
        kb.op("vector", lambda E: E.tensor_tensor(sc8(2), sc8(2), sc8(0), ALU.mult))
        kb.op("vector", lambda E: E.reduce_sum(c3, sc8(2), axis=mybir.AxisListType.X))
        kb.op("vector", lambda E: E.tensor_scalar(c3, c3, 1e-8, None, ALU.add))
        kb.op("vector", lambda E: E.reciprocal(c3, c3))
        vv = kb.op("vector", lambda E: E.tensor_scalar(
            t["fscr"][:, 64:72] if False else AB[:, FRB:FRB + 8],
            sc8(2), c3, 0.125, ALU.mult, ALU.mult), inc=n_ve)
        pv = kb.op("tensor", lambda E: E.matmul(
            t["pb5"][0:8, 0:128], AB[:, FRB:FRB + 8], t["ident_bf"][:, :],
            start=True, stop=True), wait=[(n_ve, vv)], inc=n_pe)
        vv = kb.op("vector", lambda E, tt=tt: E.tensor_copy(
            AB[0:8, FEW + tt * 128:FEW + tt * 128 + 128], t["pb5"][0:8, 0:128]),
            wait=[(n_pe, pv)], inc=n_ve)
        kb.wait("tensor", n_ve, vv)
    pv = kb.op("tensor", lambda E: E.matmul(
        t["pb5"][0:64, 0:512], t["rep8c"][:, :], AB[0:8, FEW:FEW + 512],
        start=True, stop=True))
    pv = kb.op("tensor", lambda E: E.matmul(
        t["pb4"][0:64, 0:512], t["rep8c"][:, :], AB[0:8, FEW + 512:FEW + 1024],
        start=True, stop=True), inc=n_pe)
    kb.op("vector", lambda E: E.tensor_copy(AB[0:64, FW72:FW72 + 512], t["pb5"][0:64, 0:512]),
          wait=[(n_pe, pv)])
    kb.op("vector", lambda E: E.tensor_copy(AB[0:64, FW72 + 512:FW72 + 1024], t["pb4"][0:64, 0:512]))
    vv = kb.op("vector", lambda E: E.memset(AB[64:72, FW72:FW72 + 1024], 0.125), inc=n_ve)
    kb.wait("tensor", n_ve, vv)

    for (aoff, dst) in ((0, FPG), (1152, FPU)):
        pv = None
        for dc in range(16):
            pv = kb.op("tensor", lambda E, dc=dc, aoff=aoff, tt2=0: E.matmul(
                t["pb4"][0:72, 0:512], AB[:, FAGU + aoff + dc * 72:FAGU + aoff + dc * 72 + 72],
                xn[:, dc * 1024:dc * 1024 + 512], start=(dc == 0), stop=(dc == 15)))
            pv = kb.op("tensor", lambda E, dc=dc, aoff=aoff: E.matmul(
                t["pb5"][0:72, 0:512], AB[:, FAGU + aoff + dc * 72:FAGU + aoff + dc * 72 + 72],
                xn[:, dc * 1024 + 512:dc * 1024 + 1024], start=(dc == 0), stop=(dc == 15)),
                inc=(n_pe if dc == 15 else None))
        vv = kb.op("vector", lambda E, dst=dst: E.tensor_tensor(
            AB[0:72, dst:dst + 512], t["pb4"][0:72, 0:512], AB[0:72, FW72:FW72 + 512], ALU.mult),
            wait=[(n_pe, pv)])
        vv = kb.op("vector", lambda E, dst=dst: E.tensor_tensor(
            AB[0:72, dst + 512:dst + 1024], t["pb5"][0:72, 0:512],
            AB[0:72, FW72 + 512:FW72 + 1024], ALU.mult), inc=n_ve)
        kb.wait("tensor", n_ve, vv)

    for fc in range(NFC):
        slot = fc % 2
        if fc >= 2:
            kb.wait("gpsimd", n_pe, kb.val(n_pe) - 3)
        wl = kb.op("gpsimd", lambda E, fc=fc, slot=slot: E.dma_start(
            out=AB.rearrange("p (s f) -> p s f", f=2048)[:, FWS // 2048 + slot, :].rearrange("p (c f) -> p c f", f=128),
            in_=t["wg"].rearrange("(c p) f -> p c f", p=128)[:, :, ts(fc, 128)]),
            inc=n_wl + str(slot), incn=16, drain=False)
        wl2 = kb.op("gpsimd", lambda E, fc=fc, slot=slot: E.dma_start(
            out=AB.rearrange("p (s f) -> p s f", f=2048)[:, FWS // 2048 + 2 + slot, :].rearrange("p (c f) -> p c f", f=128),
            in_=t["wu"].rearrange("(c p) f -> p c f", p=128)[:, :, ts(fc, 128)]),
            inc=n_wl + str(slot), incn=16, drain=False)
        bgl = kb.op("gpsimd", lambda E, fc=fc, slot=slot: E.dma_start(
            out=AB[0:72, FBGF + slot * 128:FBGF + slot * 128 + 128],
            in_=t["bg"][:, ts(fc, 128)]), inc=n_wl + str(slot), incn=16, drain=False)
        bul = kb.op("gpsimd", lambda E, fc=fc, slot=slot: E.dma_start(
            out=AB[0:72, FBGF + 256 + slot * 128:FBGF + 256 + slot * 128 + 128],
            in_=t["bu"][:, ts(fc, 128)]), inc=n_wl + str(slot), incn=16, drain=False)
        adl = kb.op("gpsimd", lambda E, fc=fc, slot=slot: E.dma_start(
            out=AB[:, FADF + slot * 128:FADF + slot * 128 + 128].rearrange("p (c f) -> p c f", f=72) if False
            else AB[:, FADF + slot * 128:FADF + slot * 128 + 72],
            in_=t["ad"][ts(fc, 128), :]), inc=n_wl + str(slot), incn=16, drain=False)
        kb.wait("tensor", n_wl + str(slot), adl)
        for half in range(2):
            gb = t[f"pb{half}"]; ub = t[f"pb{2 + half}"]
            for dc in range(16):
                kb.op("tensor", lambda E, dc=dc, slot=slot, half=half, gb=gb: E.matmul(
                    gb[:, 0:512], AB[:, FWS + slot * 2048 + dc * 128:FWS + slot * 2048 + dc * 128 + 128],
                    xn[:, dc * 1024 + half * 512:dc * 1024 + half * 512 + 512],
                    start=(dc == 0), stop=False))
                kb.op("tensor", lambda E, dc=dc, slot=slot, half=half, ub=ub: E.matmul(
                    ub[:, 0:512], AB[:, FWS + (2 + slot) * 2048 + dc * 128:FWS + (2 + slot) * 2048 + dc * 128 + 128],
                    xn[:, dc * 1024 + half * 512:dc * 1024 + half * 512 + 512],
                    start=(dc == 0), stop=False))
            kb.op("tensor", lambda E, slot=slot, half=half, gb=gb: E.matmul(
                gb[:, 0:512], AB[0:72, FBGF + slot * 128:FBGF + slot * 128 + 128],
                AB[0:72, FPG + half * 512:FPG + half * 512 + 512], start=False, stop=True))
            pv = kb.op("tensor", lambda E, slot=slot, half=half, ub=ub: E.matmul(
                ub[:, 0:512], AB[0:72, FBGF + 256 + slot * 128:FBGF + 256 + slot * 128 + 128],
                AB[0:72, FPU + half * 512:FPU + half * 512 + 512], start=False, stop=True),
                inc=n_pe)
            hg = AB[:, FHG + half * 512:FHG + half * 512 + 512]
            av = kb.op("scalar", lambda E, hg=hg, gb=gb: E.activation(hg, gb[:, 0:512], AF.Silu),
                       wait=[(n_pe, pv)], inc=n_ac)
            vv = kb.op("vector", lambda E, fc=fc, half=half, hg=hg, ub=ub: E.tensor_tensor(
                AB[:, FHID + fc * 1024 + half * 512:FHID + fc * 1024 + half * 512 + 512],
                hg, ub[:, 0:512], ALU.mult), wait=[(n_ac, av)], inc=n_ve)
            kb.wait("tensor", n_ve, vv)
        pv = kb.op("tensor", lambda E, fc=fc, slot=slot: E.matmul(
            t["pb4"][0:72, 0:512], AB[:, FADF + slot * 128:FADF + slot * 128 + 72],
            AB[:, FHID + fc * 1024:FHID + fc * 1024 + 512],
            start=(fc == 0), stop=(fc == NFC - 1)))
        pv = kb.op("tensor", lambda E, fc=fc, slot=slot: E.matmul(
            t["pb5"][0:72, 0:512], AB[:, FADF + slot * 128:FADF + slot * 128 + 72],
            AB[:, FHID + fc * 1024 + 512:FHID + fc * 1024 + 1024],
            start=(fc == 0), stop=(fc == NFC - 1)), inc=n_pe)
    pv = kb.val(n_pe)
    kb.op("vector", lambda E: E.tensor_tensor(
        AB[0:72, FPD:FPD + 512], t["pb4"][0:72, 0:512], AB[0:72, FW72:FW72 + 512], ALU.mult),
        wait=[(n_pe, pv)])
    vv = kb.op("vector", lambda E: E.tensor_tensor(
        AB[0:72, FPD + 512:FPD + 1024], t["pb5"][0:72, 0:512],
        AB[0:72, FW72 + 512:FW72 + 1024], ALU.mult), inc=n_ve)
    kb.wait("tensor", n_ve, vv)

    kb.wait("gpsimd", n_pe, kb.val(n_pe))
    for dmc in range(16):
        slot = dmc % 2
        if dmc >= 2:
            kb.wait("gpsimd", n_pe, kb.val(n_pe) - 1)
        wl = kb.op("gpsimd", lambda E, dmc=dmc, slot=slot: E.dma_start(
            out=AB[:, FWS + slot * 2816:FWS + slot * 2816 + 2816].rearrange("p (c f) -> p c f", f=128),
            in_=t["wd"].rearrange("(c p) f -> p c f", p=128)[:, :, ts(dmc, 128)]),
            inc="ff_wl" + str(slot), incn=16, drain=False)
        kb.wait("tensor", "ff_wl" + str(slot), wl)
        bank = t[f"pb{dmc % 2}"]
        bank2 = t[f"pb{2 + dmc % 2}"]
        if dmc >= 2:
            kb.wait("tensor", n_st + str(slot), kb.val(n_st + str(slot)))
        for fc in range(NFC):
            kb.op("tensor", lambda E, fc=fc, slot=slot, bank=bank: E.matmul(
                bank[:, 0:512], AB[:, FWS + slot * 2816 + fc * 128:FWS + slot * 2816 + fc * 128 + 128],
                AB[:, FHID + fc * 1024:FHID + fc * 1024 + 512],
                start=(fc == 0), stop=False))
            kb.op("tensor", lambda E, fc=fc, slot=slot, bank2=bank2: E.matmul(
                bank2[:, 0:512], AB[:, FWS + slot * 2816 + fc * 128:FWS + slot * 2816 + fc * 128 + 128],
                AB[:, FHID + fc * 1024 + 512:FHID + fc * 1024 + 1024],
                start=(fc == 0), stop=False))
        kb.op("tensor", lambda E, dmc=dmc, bank=bank: E.matmul(
            bank[:, 0:512], AB[0:72, FBD + dmc * 128:FBD + dmc * 128 + 128],
            AB[0:72, FPD:FPD + 512], start=False, stop=True))
        pv = kb.op("tensor", lambda E, dmc=dmc, bank2=bank2: E.matmul(
            bank2[:, 0:512], AB[0:72, FBD + dmc * 128:FBD + dmc * 128 + 128],
            AB[0:72, FPD + 512:FPD + 1024], start=False, stop=True), inc=n_pe)
        ev = AB[:, FDE + slot * 512:FDE + slot * 512 + 512]
        vv = kb.op("vector", lambda E, bank=bank, ev=ev: E.tensor_copy(ev[:, 0:512], bank[:, 0:512]),
                   wait=[(n_pe, pv)], inc=n_ve)
        kb.op("gpsimd", lambda E, dmc=dmc, osl=osl, ev=ev: E.dma_start(
            out=t["cc0_in"][ts(dmc, 128), osl * 1024:osl * 1024 + 512], in_=ev[:, 0:512],
            accum_op=ALU.add), wait=[(n_ve, vv)], inc=n_st + str(slot), incn=16, drain=False)
        vv = kb.op("vector", lambda E, bank2=bank2, ev=ev: E.tensor_copy(ev[:, 0:512], bank2[:, 0:512]),
                   wait=[(n_st + str(slot), kb.val(n_st + str(slot)))], inc=n_ve)
        kb.op("gpsimd", lambda E, dmc=dmc, osl=osl, ev=ev: E.dma_start(
            out=t["cc0_in"][ts(dmc, 128), osl * 1024 + 512:osl * 1024 + 1024], in_=ev[:, 0:512],
            accum_op=ALU.add), wait=[(n_ve, vv)], inc=n_st + str(slot), incn=16, drain=False)
    for p in range(2):
        kb.wait("vector", n_st + str(p), kb.val(n_st + str(p)))
        kb.wait("gpsimd", n_st + str(p), kb.val(n_st + str(p)))


def emit_l0_combine(kb):
    t = kb.tensors
    AB = t["AB"]
    cv = kb.op("gpsimd", lambda E: E.collective_compute(
        "AllReduce", ALU.add, replica_groups=PAIRS,
        ins=[t["cc0_in"].opt() if hasattr(t["cc0_in"], "opt") else t["cc0_in"]],
        outs=[t["cc0_out"].opt() if hasattr(t["cc0_out"], "opt") else t["cc0_out"]]),
        inc="cc0", drain=False)
    kb.wait("sync", "cc0", cv)
    kb.wait("gpsimd", "cc0", cv)
    emit_norm_x1(kb)


def emit_norm_x1(kb):
    t = kb.tensors
    AB = t["AB"]
    n_ldx = "n1_ld"; n_ldd = "n1_ldd"; n_pe = "n1_pe"; n_ve = "n1_ve"
    n_ac = "n1_ac"; n_cast = "n1_cast"; n_ap = "n1_ap"; n_mm2 = "n1_mm2"
    n_sx = "n1_sx"
    for tile in range(NSLAB):
        cur = t["xnA"] if tile % 2 == 0 else t["xnB"]
        if tile >= 2:
            kb.wait("vector", "n1_st" + str(tile % 2), 16 * (tile // 2))
        mm2_base = kb.semv.get(n_mm2, 0)
        for c in range(16):
            g = tile * 16 + c
            if g >= 2:
                kb.wait("sync", n_cast, g - 1)
                kb.wait("gpsimd", n_cast, g - 1)
            lv = kb.op("sync", lambda E, c=c, tile=tile:
                       E.dma_start(out=t["fscr"][:, ts(c % 2, 512)],
                                   in_=t["xT"][ts(c, 128), ts(tile, 512)]),
                       inc=n_ldx + str(c % 2), incn=16)
            dv = kb.op("gpsimd", lambda E, c=c, tile=tile:
                       E.dma_start(out=t["cscr"][:, ts(c % 2, 512)],
                                   in_=t["cc0_out"][ts(c, 128), ts(tile, 512)]),
                       inc=n_ldd + str(c % 2), incn=16, drain=False)
            vv = kb.op("vector", lambda E, c=c:
                       E.tensor_tensor(t["fscr"][:, ts(2 + c % 2, 512)],
                                       t["fscr"][:, ts(c % 2, 512)],
                                       t["cscr"][:, ts(c % 2, 512)], ALU.add),
                       wait=[(n_ldx + str(c % 2), lv), (n_ldd + str(c % 2), dv)])
            cv = kb.op("vector", lambda E, c=c, cur=cur:
                       E.tensor_copy(cur[:, ts(c, 512)], t["fscr"][:, ts(2 + c % 2, 512)]),
                       inc=n_cast)
            sxv = kb.op("sync", lambda E, c=c, tile=tile:
                        E.dma_start(out=t["x1_sp"][ts(c, 128), ts(tile, 512)],
                                    in_=t["fscr"][:, ts(2 + c % 2, 512)]),
                        wait=[(n_cast, cv)], inc=n_sx + str(c % 2), incn=16)
            kb.wait("vector", n_sx + str(c % 2), kb.val(n_sx + str(c % 2)))
            sq = kb.op("scalar", lambda E, c=c, cur=cur:
                       E.activation(AB[:, LCSC:LCSC + 512], cur[:, ts(c, 512)],
                                    AF.Square),
                       wait=([(n_cast, cv), (n_mm2, mm2_base + c)] if c >= 1
                             else [(n_cast, cv)]), inc=n_ac)
            kb.op("tensor", lambda E, c=c, cur=cur:
                  E.matmul(t["pb0"][0:1, :], t["ones_col_bf"][:, :], cur[:, ts(c, 512)],
                           start=(c == 0), stop=(c == 15)), wait=[(n_cast, cv)],
                  inc=(n_pe if c == 15 else None))
            kb.op("tensor", lambda E, c=c:
                  E.matmul(t["pb1"][0:1, :], t["ones_col_bf"][:, :],
                           AB[:, LCSC:LCSC + 512],
                           start=(c == 0), stop=(c == 15)),
                  wait=[(n_ac, sq)], inc=n_mm2)
        kb.op("tensor", lambda E: E.nop(), inc=n_pe, drain=False,
              wait=[(n_mm2, kb.val(n_mm2))])
        def R(i):
            return t["rows"][0:1, 512 * i:512 * (i + 1)]
        kb.wait("vector", n_pe, kb.val(n_pe))
        kb.op("vector", lambda E: E.tensor_scalar(R(0), t["pb0"][0:1, :], 1.0 / D, None, ALU.mult))
        kb.op("vector", lambda E: E.tensor_scalar(R(1), t["pb1"][0:1, :], 1.0 / D, None, ALU.mult))
        kb.op("vector", lambda E: E.tensor_tensor(R(2), R(0), R(0), ALU.mult))
        vv = kb.op("vector", lambda E: E.tensor_tensor(R(2), R(1), R(2), ALU.subtract), inc=n_ve)
        sv = kb.op("scalar", lambda E: E.activation(R(3), R(2), AF.Sqrt,
                                                    bias=t["epsn_col"][0:1, :]),
                   wait=[(n_ve, vv)], inc=n_ac)
        vv = kb.op("vector", lambda E: E.reciprocal(R(3), R(3)), wait=[(n_ac, sv)])
        vv = kb.op("vector", lambda E: E.tensor_tensor(R(4), R(0), R(3), ALU.mult), inc=n_ve)
        pv = kb.op("tensor", lambda E: E.matmul(t["pb2"][:, :], t["ones_row_f32"][0:1, :],
                                                R(3), start=True, stop=True),
                   wait=[(n_ve, vv)])
        pv = kb.op("tensor", lambda E: E.matmul(t["pb3"][:, :], t["ones_row_f32"][0:1, :],
                                                R(4), start=True, stop=True), inc=n_pe)
        kb.wait("vector", n_pe, kb.val(n_pe))
        for c in range(16):
            kb.op("vector", lambda E, c=c, cur=cur: E.tensor_tensor(
                cur[:, ts(c, 512)], cur[:, ts(c, 512)], t["pb2"][:, :], ALU.mult))
            kb.op("vector", lambda E, c=c, cur=cur: E.tensor_tensor(
                cur[:, ts(c, 512)], cur[:, ts(c, 512)], t["pb3"][:, :], ALU.subtract),
                inc=n_ap)
        kb.op("sync", lambda E, tile=tile, cur=cur: E.dma_start(
            out=t["xn1_sp"].rearrange("(c p) s -> p c s", p=128)[:, :, ts(tile, 512)],
            in_=cur.rearrange("p (c f) -> p c f", f=512)),
            wait=[(n_ap, kb.val(n_ap))], inc="n1_st" + str(tile % 2), incn=16)
        kb.wait("tensor", n_ap, kb.val(n_ap))
    for p in range(2):
        if kb.semv.get("n1_st" + str(p)):
            kb.wait("sync", "n1_st" + str(p), kb.val("n1_st" + str(p)))
        if kb.semv.get(n_sx + str(p)):
            kb.wait("sync", n_sx + str(p), kb.val(n_sx + str(p)))


def emit_l1(kb):
    emit_l1_attn(kb)
    kb.barrier()
    emit_l1_ffn(kb)
    kb.barrier()


A1K = 0
A1VT = 16384
A1XN = 32768
A1Q = 40960
A1P = 45056
A1PT = 47104
A1O = 55296
A1W = 59392
A1E = 40960


def emit_l1_attn(kb):
    t = kb.tensors
    AB = t["AB"]
    n_wl = "a1_wl"; n_pe = "a1_pe"; n_ve = "a1_ve"; n_ac = "a1_ac"
    n_xl = "a1_xl"; n_tr = "a1_tr"; n_st = "a1_st"

    for tt in range(4):
        xv = kb.op("sync", lambda E, tt=tt: E.dma_start(
            out=AB.rearrange("p (c f) -> p c f", f=512)[:, A1XN // 512:A1XN // 512 + 16, :],
            in_=t["xn1_sp"].rearrange("(c p) s -> p c s", p=128)[:, :, ts(tt, 512)]),
            inc=n_xl + str(tt % 2), incn=16, wait=[(n_pe, kb.val(n_pe))])
        kb.wait("tensor", n_xl + str(tt % 2), xv)
        for cc in range(16):
            isv = cc >= 8
            head = cc - 8 if isv else cc
            wcol = (2048 if isv else 1024) + head * 128
            slot = cc % 2
            if tt * 16 + cc >= 2:
                kb.wait("gpsimd", n_pe, kb.val(n_pe) - 1)
            wl = kb.op("gpsimd", lambda E, wcol=wcol, slot=slot: E.dma_start(
                out=AB[:, A1W + slot * 2048:A1W + slot * 2048 + 2048].rearrange("p (c f) -> p c f", f=128),
                in_=t["wpT_c"].rearrange("(c p) f -> p c f", p=128)[:, :, wcol:wcol + 128]),
                inc=n_wl + str(slot), incn=16, drain=False)
            kb.wait("tensor", n_wl + str(slot), wl)
            bank = t[f"pb{cc % 2}"]
            kb.wait("tensor", n_ve, kb.val(n_ve))
            for dc in range(16):
                kb.op("tensor", lambda E, dc=dc, slot=slot, bank=bank: E.matmul(
                    bank[:, 0:512], AB[:, A1W + slot * 2048 + dc * 128:A1W + slot * 2048 + dc * 128 + 128],
                    AB[:, A1XN + dc * 512:A1XN + dc * 512 + 512],
                    start=(dc == 0), stop=(dc == 15)), inc=(n_pe if dc == 15 else None))
            mmv = kb.val(n_pe)
            if not isv:
                vv = kb.op("vector", lambda E, head=head, tt=tt, bank=bank: E.tensor_copy(
                    AB[:, A1K + head * 2048 + tt * 512:A1K + head * 2048 + tt * 512 + 512],
                    bank[:, 0:512]), wait=[(n_pe, mmv)], inc=n_ve)
            else:
                ev = AB[:, A1E + (cc % 2) * 512:A1E + (cc % 2) * 512 + 512]
                vv = kb.op("vector", lambda E, ev=ev, bank=bank: E.tensor_copy(ev, bank[:, 0:512]),
                           wait=[(n_pe, mmv), (n_tr, kb.val(n_tr))], inc=n_ve)
                for j in range(4):
                    kb.op("scalar", lambda E, head=head, tt=tt, j=j, ev=ev: E.dma_start_transpose(
                        out=AB[:, A1VT + head * 2048 + (tt * 4 + j) * 128:A1VT + head * 2048 + (tt * 4 + j) * 128 + 128],
                        in_=ev[:, j * 128:j * 128 + 128]),
                        wait=([(n_ve, vv)] if j == 0 else None), inc=n_tr, incn=16, drain=False)
    kb.wait("tensor", n_tr, kb.val(n_tr))
    kb.wait("sync", n_tr, kb.val(n_tr))

    for qs in range(4):
        xv = kb.op("sync", lambda E, qs=qs: E.dma_start(
            out=AB.rearrange("p (c f) -> p c f", f=512)[:, A1XN // 512:A1XN // 512 + 16, :],
            in_=t["xn1_sp"].rearrange("(c p) s -> p c s", p=128)[:, :, ts(qs, 512)]),
            inc=n_xl + str(qs % 2), incn=16,
            wait=[(n_pe, kb.val(n_pe))])
        kb.wait("tensor", n_xl + str(qs % 2), xv)
        for p_ in range(2):
            kb.wait("scalar", n_st + str(p_), kb.val(n_st + str(p_)))
            kb.wait("vector", n_st + str(p_), kb.val(n_st + str(p_)))
        for head in range(8):
            slot = head % 2
            if qs * 8 + head >= 2:
                kb.wait("gpsimd", n_pe, kb.val(n_pe) - 1)
            else:
                kb.wait("gpsimd", n_pe, kb.val(n_pe))
            wl = kb.op("gpsimd", lambda E, head=head, slot=slot: E.dma_start(
                out=AB[:, A1W + slot * 2048:A1W + slot * 2048 + 2048].rearrange("p (c f) -> p c f", f=128),
                in_=t["wpT_c"].rearrange("(c p) f -> p c f", p=128)[:, :, head * 128:head * 128 + 128]),
                inc=n_wl + str(slot), incn=16, drain=False)
            kb.wait("tensor", n_wl + str(slot), wl)
            bank = t[f"pb{head % 2}"]
            kb.wait("tensor", n_ve, kb.val(n_ve))
            for dc in range(16):
                kb.op("tensor", lambda E, dc=dc, slot=slot, bank=bank: E.matmul(
                    bank[:, 0:512], AB[:, A1W + slot * 2048 + dc * 128:A1W + slot * 2048 + dc * 128 + 128],
                    AB[:, A1XN + dc * 512:A1XN + dc * 512 + 512],
                    start=(dc == 0), stop=(dc == 15)), inc=(n_pe if dc == 15 else None))
            vv = kb.op("vector", lambda E, head=head, bank=bank: E.tensor_scalar(
                AB[:, A1Q + head * 512:A1Q + head * 512 + 512], bank[:, 0:512],
                HD1 ** -0.5, None, ALU.mult), wait=[(n_pe, kb.val(n_pe))], inc=n_ve)
        kb.wait("tensor", n_ve, kb.val(n_ve))
        for head in range(8):
            for qt in range(4):
                gq = qs * 4 + qt
                kmax = (gq + 1) * 128
                nk5 = (kmax + 511) // 512
                Qt = AB[:, A1Q + head * 512 + qt * 128:A1Q + head * 512 + qt * 128 + 128]
                pv = None
                for k5 in range(nk5):
                    kw = min(512, kmax - k5 * 512)
                    pv = kb.op("tensor", lambda E, head=head, k5=k5, kw=kw, Qt=Qt: E.matmul(
                        t[f"pb{k5}"][:, 0:kw], Qt,
                        AB[:, A1K + head * 2048 + k5 * 512:A1K + head * 2048 + k5 * 512 + kw],
                        start=True, stop=True), inc=n_pe)
                mx = t["nrmc"][:, 48:49]
                m2 = t["nrmc"][:, 49:50]
                first = True
                kb.wait("vector", n_pe, pv)
                for k5 in range(nk5):
                    kw = min(512, kmax - k5 * 512)
                    if first:
                        kb.op("vector", lambda E, k5=k5, kw=kw: E.reduce_max(
                            mx, t[f"pb{k5}"][:, 0:kw], axis=mybir.AxisListType.X))
                        first = False
                    else:
                        kb.op("vector", lambda E, k5=k5, kw=kw: E.reduce_max(
                            m2, t[f"pb{k5}"][:, 0:kw], axis=mybir.AxisListType.X))
                        kb.op("vector", lambda E: E.tensor_tensor(mx, mx, m2, ALU.max))
                vv = kb.op("vector", lambda E: E.tensor_scalar(m2, mx, -1.0, None, ALU.mult), inc=n_ve)
                sums = t["nrmc"][:, 50:55]
                for k5 in range(nk5):
                    kw = min(512, kmax - k5 * 512)
                    kb.op("scalar", lambda E, k5=k5, kw=kw: E.activation(
                        AB[:, A1P + k5 * 512:A1P + k5 * 512 + kw], t[f"pb{k5}"][:, 0:kw],
                        AF.Exp, bias=m2, accum_out=sums[:, k5:k5 + 1]),
                        wait=[(n_ve, vv)], inc=(n_ac if k5 == nk5 - 1 else None))
                av = kb.val(n_ac)
                vv = kb.op("vector", lambda E, gq=gq: E.tensor_tensor(
                    AB[:, A1P + gq * 128:A1P + gq * 128 + 128],
                    AB[:, A1P + gq * 128:A1P + gq * 128 + 128],
                    t["m_low_incl01"][:, :], ALU.mult), wait=[(n_ac, av)])
                dk5 = gq // 4
                kwd = min(512, kmax - dk5 * 512)
                kb.op("vector", lambda E, dk5=dk5, kwd=kwd: E.reduce_sum(
                    sums[:, dk5:dk5 + 1], AB[:, A1P + dk5 * 512:A1P + dk5 * 512 + kwd],
                    axis=mybir.AxisListType.X))
                tot = t["nrmc"][:, 55:56]
                kb.op("vector", lambda E, nk5=nk5: E.reduce_sum(
                    tot, sums[:, 0:nk5], axis=mybir.AxisListType.X))
                vv = kb.op("vector", lambda E: E.reciprocal(tot, tot), inc=n_ve)
                for k5 in range(nk5):
                    kw = min(512, kmax - k5 * 512)
                    kb.op("vector", lambda E, k5=k5, kw=kw: E.tensor_scalar(
                        AB[:, A1P + k5 * 512:A1P + k5 * 512 + kw],
                        AB[:, A1P + k5 * 512:A1P + k5 * 512 + kw], tot, None, ALU.mult),
                        inc=(n_ve if k5 == nk5 - 1 else None))
                vv = kb.val(n_ve)
                nkc = kmax // 128
                for kc in range(nkc):
                    kb.op("scalar", lambda E, kc=kc, qt=qt: E.dma_start_transpose(
                        out=AB[:, A1PT + kc * 512 + qt * 128:A1PT + kc * 512 + qt * 128 + 128],
                        in_=AB[:, A1P + kc * 128:A1P + kc * 128 + 128]),
                        wait=([(n_ve, vv)] if kc == 0 else None), inc=n_tr, incn=16, drain=False)
                kb.wait("tensor", n_tr, kb.val(n_tr))
                kb.wait("vector", n_tr, kb.val(n_tr))
            nkc_s = ((qs + 1) * 512) // 128
            pv = None
            for kc in range(nkc_s):
                pv = kb.op("tensor", lambda E, head=head, kc=kc, nkc_s=nkc_s: E.matmul(
                    t["pb5"][:, 0:512],
                    AB[:, A1VT + head * 2048 + kc * 128:A1VT + head * 2048 + kc * 128 + 128],
                    AB[:, A1PT + kc * 512:A1PT + kc * 512 + 512],
                    start=(kc == 0), stop=(kc == nkc_s - 1)), inc=(n_pe if kc == nkc_s - 1 else None))
            vv = kb.op("vector", lambda E, head=head: E.tensor_copy(
                AB[:, A1O + head * 512:A1O + head * 512 + 512], t["pb5"][:, 0:512]),
                wait=[(n_pe, pv)], inc=n_ve)
            kb.wait("tensor", n_ve, vv)
            kb.wait("scalar", n_ve, vv)
        kb.wait("gpsimd", n_pe, kb.val(n_pe))
        for dmc in range(16):
            slot = dmc % 2
            if dmc >= 2:
                kb.wait("gpsimd", n_pe, kb.val(n_pe) - 1)
            wl = kb.op("gpsimd", lambda E, dmc=dmc, slot=slot: E.dma_start(
                out=AB[:, A1Q + slot * 1024:A1Q + slot * 1024 + 1024].rearrange("p (c f) -> p c f", f=128),
                in_=t["woutT1_c"].rearrange("(c p) f -> p c f", p=128)[:, :, ts(dmc, 128)]),
                inc="a1o_wl" + str(slot), incn=16, drain=False)
            kb.wait("tensor", "a1o_wl" + str(slot), wl)
            bank = t[f"pb{dmc % 2}"]
            if dmc >= 2:
                kb.wait("tensor", n_st + str(slot), kb.val(n_st + str(slot)))
            for hc in range(8):
                kb.op("tensor", lambda E, hc=hc, slot=slot, bank=bank: E.matmul(
                    bank[:, 0:512], AB[:, A1Q + slot * 1024 + hc * 128:A1Q + slot * 1024 + hc * 128 + 128],
                    AB[:, A1O + hc * 512:A1O + hc * 512 + 512],
                    start=(hc == 0), stop=(hc == 7)), inc=(n_pe if hc == 7 else None))
            ev = AB[:, A1P + slot * 512:A1P + slot * 512 + 512]
            vv = kb.op("vector", lambda E, ev=ev, bank=bank: E.tensor_copy(ev, bank[:, 0:512]),
                       wait=[(n_pe, kb.val(n_pe))], inc=n_ve)
            half = qs // 2
            off = (qs % 2) * 512
            kb.op("sync", lambda E, dmc=dmc, half=half, off=off, ev=ev: E.dma_start(
                out=t["cc1_in"][half, dmc, :, off:off + 512], in_=ev),
                wait=[(n_ve, vv)], inc=n_st + str(slot), incn=16)
    for p in range(2):
        kb.wait("sync", n_st + str(p), kb.val(n_st + str(p)))
        kb.wait("gpsimd", n_st + str(p), kb.val(n_st + str(p)))


BXN = 0
BHID = 16384
BWS = 38912
BQ1 = 38912
BQ2 = 43008
BQ3 = 47104
BHG = 52736
BDE = 53760
BX1 = 54784


def emit_l1_ffn(kb):
    t = kb.tensors
    AB = t["AB"]
    n_pe = "b1_pe"; n_ve = "b1_ve"; n_ac = "b1_ac"; n_st = "b1_st"
    n_gl = "b1_gl"; n_xl = "b1_xl"
    accs = t["nrmc"][:, 56:60]
    kb.op("vector", lambda E: E.memset(t["nrmc"][:, 56:60], 0.0), inc=n_ve)
    for wi, wname in enumerate(["w1T", "w2T", "w3T"]):
        nchunks = 16 if wi < 2 else 22
        width = 2816 if wi < 2 else 2048
        for c in range(nchunks):
            lv = kb.op("sync", lambda E, wname=wname, c=c, width=width: E.dma_start(
                out=t["fscr"][:, 0:width] if width == 2048 else t["AF"][:, 0:2816],
                in_=t[wname][ts(c, 128), :]), inc="b1g_ld", incn=16,
                wait=[("b1g_ve", kb.val("b1g_ve"))])
            nsub = width // 512 + (1 if width % 512 else 0)
            for s in range(width // 512 + (1 if width % 512 else 0)):
                sw = min(512, width - s * 512)
                av = kb.op("scalar", lambda E, s=s, sw=sw, width=width: E.activation(
                    AB[:, BHG:BHG + sw],
                    (t["fscr"][:, 0:width] if width == 2048 else t["AF"][:, 0:2816])[:, s * 512:s * 512 + sw],
                    AF.Abs, accum_out=t["nrmc"][:, 60:61]),
                    wait=[("b1g_ld", lv), ("b1g_ve", kb.val("b1g_ve"))], inc=n_ac)
                kb.op("vector", lambda E, wi=wi: E.tensor_tensor(
                    accs[:, wi:wi + 1], accs[:, wi:wi + 1], t["nrmc"][:, 60:61], ALU.add),
                    wait=[(n_ac, av)], inc="b1g_ve")
    vv = kb.val("b1g_ve")
    pv = kb.op("tensor", lambda E: E.matmul(
        t["pb0"][0:4, 0:1], accs[:, 0:4], t["one_col"][:, 0:1], start=True, stop=True),
        wait=[("b1g_ve", vv)], inc=n_pe)
    vv = kb.op("vector", lambda E: E.tensor_copy(t["nrmc"][0:4, 61:62], t["pb0"][0:4, 0:1]),
               wait=[(n_pe, pv)], inc=n_ve)
    sv = kb.op("sync", lambda E: E.dma_start(out=t["ccg_in"], in_=t["nrmc"][0:4, 61:62]),
               wait=[(n_ve, vv)], inc="b1g_st", incn=16)
    kb.wait("gpsimd", "b1g_st", kb.val("b1g_st"))
    cv = kb.op("gpsimd", lambda E: E.collective_compute(
        "AllReduce", ALU.add, replica_groups=PAIRS,
        ins=[t["ccg_in"]], outs=[t["ccg_out"]]), inc="ccg", drain=False)
    kb.wait("sync", "ccg", cv)
    lv = kb.op("sync", lambda E: E.dma_start(out=t["nrmc"][0:4, 61:62], in_=t["ccg_out"]),
               inc="b1g_ld", incn=16)
    kb.wait("vector", "b1g_ld", kb.val("b1g_ld"))
    kb.op("vector", lambda E: E.tensor_scalar(t["nrmc"][0:4, 62:63], t["nrmc"][0:4, 61:62],
                                              1.0 / (DFF1 * D), 1e-8, ALU.mult, ALU.add))
    vv = kb.op("vector", lambda E: E.reciprocal(t["nrmc"][0:4, 62:63], t["nrmc"][0:4, 62:63]), inc=n_ve)
    pv = kb.op("tensor", lambda E: E.matmul(
        t["pb0"][0:1, 0:4], t["nrmc"][0:4, 62:63], t["idf16"][0:4, 0:4],
        start=True, stop=True), wait=[(n_ve, vv)], inc=n_pe)
    vv = kb.op("vector", lambda E: E.tensor_copy(t["rows"][0:1, 0:4], t["pb0"][0:1, 0:4]),
               wait=[(n_pe, pv)], inc=n_ve)
    kb.wait("tensor", n_ve, vv)
    for wi in range(3):
        pv = kb.op("tensor", lambda E, wi=wi: E.matmul(
            t["pb1"][:, wi:wi + 1], t["ones_row_f32"][0:1, :], t["rows"][0:1, wi:wi + 1],
            start=True, stop=True), inc=(n_pe if wi == 2 else None))
    vv = kb.op("vector", lambda E: E.tensor_copy(t["nrmc"][:, 57:60], t["pb1"][:, 0:3]),
               wait=[(n_pe, kb.val(n_pe))], inc=n_ve)
    kb.wait("sync", n_ve, vv)
    kb.wait("gpsimd", n_ve, vv)

    def quantize(wslot_fp, dst, rg, wv, tag):
        W = 2048
        ops = []
        for s in range(W // 512):
            sl = slice(s * 512, s * 512 + 512)
            kb.op("vector", lambda E, sl=sl, wslot_fp=wslot_fp, rg=rg: E.tensor_scalar(
                AB[:, BHG:BHG + 512], wslot_fp[:, sl], rg, 0.5, ALU.mult, ALU.is_ge),
                wait=([(tag, wv)] if s == 0 else None))
            kb.op("vector", lambda E, sl=sl, wslot_fp=wslot_fp, rg=rg: E.tensor_scalar(
                AB[:, BHG + 512:BHG + 1024], wslot_fp[:, sl], rg, -0.5, ALU.mult, ALU.is_le))
            kb.op("vector", lambda E, sl=sl, dst=dst: E.tensor_tensor(
                dst[:, sl], AB[:, BHG:BHG + 512], AB[:, BHG + 512:BHG + 1024], ALU.subtract),
                inc=n_ve)
        return kb.val(n_ve)

    rg1 = t["nrmc"][:, 57:58]; rg2 = t["nrmc"][:, 58:59]; rg3 = t["nrmc"][:, 59:60]
    for osl in range(2):
        kb.wait("sync", n_pe, kb.val(n_pe))
        xv = kb.op("sync", lambda E, osl=osl: E.dma_start(
            out=AB.rearrange("p (c f) -> p c f", f=1024)[:, BXN // 1024:BXN // 1024 + 16, :],
            in_=t["xn1_sp"].rearrange("(c p) s -> p c s", p=128)[:, :, ts(osl, 1024)]),
            inc=n_xl, incn=16)
        kb.wait("tensor", n_xl, kb.val(n_xl))
        xn = AB[:, BXN:BXN + 16384]
        for fc in range(NFC):
            slot = fc % 2
            if fc >= 2:
                kb.wait("sync", n_pe, kb.val(n_pe) - 3)
            lv1 = kb.op("sync", lambda E, fc=fc, slot=slot: E.dma_start(
                out=t["fscr"][:, 0:2048].rearrange("p (c f) -> p c f", f=128) if False
                else t["fscr"][:, 0:2048],
                in_=t["w1T"].rearrange("(c p) f -> p c f", p=128)[:, :, ts(fc, 128)].rearrange("p c f -> p (c f)") if False
                else t["w1T"].rearrange("(c p) f -> p c f", p=128)[:, :, ts(fc, 128)]),
                inc="b1w_ld" + str(slot), incn=16, wait=[(n_ve, kb.val(n_ve))])
            qv1 = quantize(t["fscr"][:, 0:2048], AB[:, BQ1 + slot * 2048:BQ1 + slot * 2048 + 2048],
                           rg1, lv1, "b1w_ld" + str(slot))
            lv2 = kb.op("sync", lambda E, fc=fc, slot=slot: E.dma_start(
                out=t["fscr"][:, 0:2048],
                in_=t["w2T"].rearrange("(c p) f -> p c f", p=128)[:, :, ts(fc, 128)]),
                inc="b1w_ld" + str(slot), incn=16, wait=[(n_ve, qv1)])
            qv2 = quantize(t["fscr"][:, 0:2048], AB[:, BQ2 + slot * 2048:BQ2 + slot * 2048 + 2048],
                           rg2, lv2, "b1w_ld" + str(slot))
            kb.wait("tensor", n_ve, qv2)
            for half in range(2):
                gb = t[f"pb{half}"]; ub = t[f"pb{2 + half}"]
                for dc in range(16):
                    kb.op("tensor", lambda E, dc=dc, slot=slot, half=half, gb=gb: E.matmul(
                        gb[:, 0:512], AB[:, BQ1 + slot * 2048 + dc * 128:BQ1 + slot * 2048 + dc * 128 + 128],
                        xn[:, dc * 1024 + half * 512:dc * 1024 + half * 512 + 512],
                        start=(dc == 0), stop=(dc == 15)))
                    kb.op("tensor", lambda E, dc=dc, slot=slot, half=half, ub=ub: E.matmul(
                        ub[:, 0:512], AB[:, BQ2 + slot * 2048 + dc * 128:BQ2 + slot * 2048 + dc * 128 + 128],
                        xn[:, dc * 1024 + half * 512:dc * 1024 + half * 512 + 512],
                        start=(dc == 0), stop=(dc == 15)),
                        inc=(n_pe if dc == 15 else None))
                pv = kb.val(n_pe)
                hg = AB[:, BHG + half * 512:BHG + half * 512 + 512]
                av = kb.op("scalar", lambda E, hg=hg, gb=gb: E.activation(hg, gb[:, 0:512], AF.Sigmoid),
                           wait=[(n_pe, pv)], inc=n_ac)
                vv = kb.op("vector", lambda E, fc=fc, half=half, hg=hg, ub=ub: E.tensor_tensor(
                    AB[:, BHID + fc * 1024 + half * 512:BHID + fc * 1024 + half * 512 + 512],
                    hg, ub[:, 0:512], ALU.mult), wait=[(n_ac, av)], inc=n_ve)
                kb.wait("tensor", n_ve, vv)
        kb.wait("sync", n_pe, kb.val(n_pe))
        for dmc in range(16):
            slot = dmc % 2
            if dmc >= 2:
                kb.wait("sync", n_pe, kb.val(n_pe) - 1)
            lv3 = kb.op("sync", lambda E, dmc=dmc: E.dma_start(
                out=t["AF"][:, 0:2816],
                in_=t["w3T"].rearrange("(c p) f -> p c f", p=128)[:, :, ts(dmc, 128)]),
                inc="b1w_ld0", incn=16, wait=[(n_ve, kb.val(n_ve))])
            for s in range(6):
                sw = min(512, 2816 - s * 512)
                kb.op("vector", lambda E, s=s, sw=sw: E.tensor_scalar(
                    AB[:, BHG:BHG + sw], t["AF"][:, s * 512:s * 512 + sw], rg3, 0.5,
                    ALU.mult, ALU.is_ge), wait=([("b1w_ld0", lv3)] if s == 0 else None))
                kb.op("vector", lambda E, s=s, sw=sw: E.tensor_scalar(
                    AB[:, BHG + 512:BHG + 512 + sw], t["AF"][:, s * 512:s * 512 + sw], rg3, -0.5,
                    ALU.mult, ALU.is_le))
                kb.op("vector", lambda E, s=s, sw=sw, slot=slot: E.tensor_tensor(
                    AB[:, BQ3 + slot * 2816 + s * 512:BQ3 + slot * 2816 + s * 512 + sw],
                    AB[:, BHG:BHG + sw], AB[:, BHG + 512:BHG + 512 + sw], ALU.subtract),
                    inc=n_ve)
            kb.wait("tensor", n_ve, kb.val(n_ve))
            bank = t[f"pb{dmc % 2}"]; bank2 = t[f"pb{2 + dmc % 2}"]
            if dmc >= 2:
                kb.wait("tensor", n_st + str(slot), kb.val(n_st + str(slot)))
            for fc in range(NFC):
                kb.op("tensor", lambda E, fc=fc, slot=slot, bank=bank: E.matmul(
                    bank[:, 0:512], AB[:, BQ3 + slot * 2816 + fc * 128:BQ3 + slot * 2816 + fc * 128 + 128],
                    AB[:, BHID + fc * 1024:BHID + fc * 1024 + 512],
                    start=(fc == 0), stop=(fc == NFC - 1)))
                kb.op("tensor", lambda E, fc=fc, slot=slot, bank2=bank2: E.matmul(
                    bank2[:, 0:512], AB[:, BQ3 + slot * 2816 + fc * 128:BQ3 + slot * 2816 + fc * 128 + 128],
                    AB[:, BHID + fc * 1024 + 512:BHID + fc * 1024 + 1024],
                    start=(fc == 0), stop=(fc == NFC - 1)), inc=(n_pe if fc == NFC - 1 else None))
            pv = kb.val(n_pe)
            for half in range(2):
                bk = bank if half == 0 else bank2
                ev = AB[:, BDE + slot * 512:BDE + slot * 512 + 512]
                w = [(n_pe, pv), (n_st + str(slot), kb.val(n_st + str(slot)))]
                vv = kb.op("vector", lambda E, bk=bk, ev=ev: E.tensor_copy(ev, bk[:, 0:512]),
                           wait=w, inc=n_ve)
                kb.op("gpsimd", lambda E, dmc=dmc, osl=osl, half=half, ev=ev: E.dma_start(
                    out=t["cc1_in"][osl, dmc, :, half * 512:half * 512 + 512], in_=ev,
                    accum_op=ALU.add), wait=[(n_ve, vv)], inc=n_st + str(slot), incn=16, drain=False)
    for p in range(2):
        kb.wait("gpsimd", n_st + str(p), kb.val(n_st + str(p)))
        kb.wait("sync", n_st + str(p), kb.val(n_st + str(p)))


def emit_final(kb):
    t = kb.tensors
    AB = t["AB"]
    n_ld = "fn_ld"; n_ve = "fn_ve"; n_st = "fn_st"
    for c in range(16):
        for half in range(2):
            for off in range(2):
                i = (c * 4 + half * 2 + off) % 2
                lv = kb.op("sync", lambda E, c=c, half=half, off=off, i=i: E.dma_start(
                    out=t["fscr"][:, ts(i, 512)],
                    in_=t["x1_sp"][ts(c, 128), half * 1024 + off * 512:half * 1024 + off * 512 + 512]),
                    inc=n_ld + str(i), incn=16,
                    wait=[(n_st + str(i), kb.val(n_st + str(i)))])
                vv = kb.op("vector", lambda E, i=i: E.tensor_scalar(
                    AB[:, BX1 + i * 512:BX1 + i * 512 + 512], t["fscr"][:, ts(i, 512)],
                    0.5, None, ALU.mult), wait=[(n_ld + str(i), lv)], inc=n_ve)
                kb.op("gpsimd", lambda E, c=c, half=half, off=off, i=i: E.dma_start(
                    out=t["cc1_in"][half, c, :, off * 512:off * 512 + 512],
                    in_=AB[:, BX1 + i * 512:BX1 + i * 512 + 512], accum_op=ALU.add),
                    wait=[(n_ve, vv)], inc=n_st + str(i), incn=16, drain=False)
    for p in range(2):
        kb.wait("gpsimd", n_st + str(p), kb.val(n_st + str(p)))
    kb.barrier()
    cv = kb.op("gpsimd", lambda E: E.collective_compute(
        "ReduceScatter", ALU.add, replica_groups=PAIRS,
        ins=[t["cc1_in"]], outs=[t["cc1_out"]]), inc="cc1", drain=False)
    kb.wait("sync", "cc1", cv)
    for c in range(16):
        for off in range(2):
            i = (c * 2 + off) % 2
            lv = kb.op("sync", lambda E, c=c, off=off, i=i: E.dma_start(
                out=AB[:, BX1 + i * 512:BX1 + i * 512 + 512],
                in_=t["cc1_out"][c, :, off * 512:off * 512 + 512]),
                inc="fn_ld" + str(i), incn=16,
                wait=[("fo_s" + str(i), kb.val("fo_s" + str(i)))])
            vv = kb.op("vector", lambda E, i=i: E.tensor_copy(
                t["fscr"][:, ts(i, 512)], AB[:, BX1 + i * 512:BX1 + i * 512 + 512]),
                wait=[("fn_ld" + str(i), lv)], inc="fn_ve")
            kb.op("sync", lambda E, c=c, off=off, i=i: E.dma_start(
                out=t["out"][c, :, off * 512:off * 512 + 512], in_=t["fscr"][:, ts(i, 512)]),
                wait=[("fn_ve", vv)], inc="fo_s" + str(i), incn=16)
    for p in range(2):
        kb.wait("sync", "fo_s" + str(p), kb.val("fo_s" + str(p)))


def kernel(**inputs):
    in_maps = [_prep_core_inputs(inputs, cid) for cid in range(NCORES)]
    nc, kb = build_program()
    res = run_bass_kernel_spmd(nc, in_maps, core_ids=list(range(NCORES)))
    out = np.empty((B, S, D), np.float32)
    for cid in range(NCORES):
        b, h = cid // 2, cid % 2
        o = np.asarray(res.results[cid]["out"])
        out[b, 1024 * h:1024 * h + 1024, :] = o.reshape(D, OWN).T
    return out
